# version 08
# baseline (speedup 1.0000x reference)
import numpy as np
import ml_dtypes
from contextlib import ExitStack

import concourse.bass as bass
import concourse.mybir as mybir
from concourse.alu_op_type import AluOpType as ALU
from concourse.bass_utils import run_bass_kernel_spmd

F32 = mybir.dt.float32
BF16 = mybir.dt.bfloat16
AF = mybir.ActivationFunctionType
AX = mybir.AxisListType

NB = 2
T = 2048
NT = NB * T
D = 1024
NC = 8
LN_EPS = 1e-5
RMS_EPS = 1e-6
ROPE_BASE = 10000.0
MOE_I = 512
MOE_E = 16
L1_M = 64
L1_INNER = 16


def _host_consts():
    bf16 = ml_dtypes.bfloat16
    c = {}
    c["ident"] = np.eye(128, dtype=bf16)
    c["ident_f32"] = np.eye(128, dtype=np.float32)
    c["onesb_col"] = np.ones((128, 1), dtype=bf16)
    c["onesb_row"] = np.ones((1, 128), dtype=bf16)
    c["ones_col_f32"] = np.ones((1, 128), dtype=np.float32)
    c["ones_row_f32"] = np.ones((1, 512), dtype=np.float32)
    half = 64
    pos = np.arange(T, dtype=np.float64)[None, :]
    freqs = (ROPE_BASE ** (-np.arange(half, dtype=np.float64) / half))[:, None]
    rad = freqs * pos
    c["ropeC"] = np.ascontiguousarray(np.concatenate([np.cos(rad)] * 2, 0)).astype(bf16)
    c["ropeS"] = np.ascontiguousarray(
        np.concatenate([-np.sin(rad), np.sin(rad)], 0)).astype(bf16)
    m = np.zeros((4, 128, 512), dtype=bf16)
    for j in range(4):
        kk = j * 128 + np.arange(128)[:, None]
        qq = np.arange(512)[None, :]
        m[j] = (kk <= qq).astype(bf16)
    c["masks"] = m
    return c


def _fm_vec(v):
    return np.ascontiguousarray(np.asarray(v, np.float32).reshape(NC, 128).T)


def _packpair(w):
    w = np.asarray(w, np.float32)
    out = np.empty((128, 8), np.float32)
    for i in range(8):
        out[0:64, i] = w[:, 2 * i]
        out[64:128, i] = w[:, 2 * i + 1]
    return np.ascontiguousarray(out)


class Sem:
    def __init__(self, stack, nc, name):
        self.h = stack.enter_context(nc.semaphore(name))
        self.n = 0

    def inc(self, instr, k=1):
        instr.then_inc(self.h, k)
        self.n += k


def build_nc(taps=()):
    nc = bass.Bass("TRN2", target_bir_lowering=False)
    dt = {}

    def din(name, shape, dtype=F32):
        dt[name] = nc.dram_tensor(name, list(shape), dtype, kind="ExternalInput").ap()
        return dt[name]

    x_in = din("x", [NB, T, D])
    out = nc.dram_tensor("out", [NB, T, D], F32, kind="ExternalOutput").ap()
    tap_out = {}
    for tn in taps:
        tap_out[tn] = nc.dram_tensor("tap_" + tn, [128, NC, NT], F32,
                                     kind="ExternalOutput").ap()

    din("l0_W", [D, D]); din("l0_b", [128, NC]); din("l0_gamma", [128, NC])
    din("l0_beta", [128, NC]); din("decay", [1, 1])
    din("gate_w", [D, MOE_E]); din("gate_b", [MOE_E, 1])
    din("sh_wg", [D, MOE_I]); din("sh_wu", [D, MOE_I]); din("sh_wd", [MOE_I, D])
    din("ex_wg", [MOE_E, D, MOE_I]); din("ex_wu", [MOE_E, D, MOE_I])
    din("ex_wd", [MOE_E, MOE_I, D])
    din("wq", [D, 8, 128]); din("wk", [D, 8, 128]); din("wv", [D, 8, 128])
    din("wo", [8, 128, D])
    din("Win", [D, L1_M]); din("bin_", [L1_M, 1])
    din("W1", [L1_M, L1_INNER]); din("b1", [L1_M, L1_INNER])
    din("W2", [L1_M, L1_INNER]); din("b2", [L1_M, 1])
    din("Wout", [L1_M, D]); din("bout", [128, NC])
    din("ident", [128, 128], BF16); din("ident_f32", [128, 128])
    din("onesb_col", [128, 1], BF16)
    din("onesb_row", [1, 128], BF16); din("ones_col_f32", [1, 128])
    din("ones_row_f32", [1, 512])
    din("ropeC", [128, T], BF16); din("ropeS", [128, T], BF16)
    din("masks", [4, 128, 512], BF16)

    stack = ExitStack()
    with stack:
        dS = Sem(stack, nc, "dS")
        dG = Sem(stack, nc, "dG")
        sT = Sem(stack, nc, "sT")
        sV = Sem(stack, nc, "sV")
        sS = Sem(stack, nc, "sS")

        def sb(name, shape, dtype):
            return stack.enter_context(nc.sbuf_tensor(name, list(shape), dtype))

        X = sb("X", [128, NC, NT], BF16)
        ident = sb("c_ident", [128, 128], BF16)
        ident_f32 = sb("c_identf", [128, 128], F32)
        onesb_col = sb("c_obc", [128, 1], BF16)
        onesb_row = sb("c_obr", [1, 128], BF16)
        ones_col_f32 = sb("c_ocf", [1, 128], F32)
        ones_row_f32 = sb("c_orf", [1, 512], F32)
        l0_b = sb("c_l0b", [128, NC], F32)
        l0_gamma = sb("c_l0g", [128, NC], F32)
        l0_beta = sb("c_l0be", [128, NC], F32)
        bout_sb = sb("c_bout", [128, NC], F32)
        decay_sb = sb("c_decay", [1, 1], F32)
        rowA = sb("rowA", [1, 512], BF16)
        rowB = sb("rowB", [1, 512], BF16)
        Abuf = sb("Abuf", [128, T], BF16)
        Bbuf = sb("Bbuf", [128, T], BF16)
        statf = sb("statf", [1, 3, 512], F32)

        block = stack.enter_context(nc.Block())
        TQ, VQ, SQ, GQ, YQ = block.tensor, block.vector, block.scalar, block.gpsimd, block.sync

        def _(y):
            for sbuf_t, nm in [(ident, "ident"), (ident_f32, "ident_f32"),
                               (onesb_col, "onesb_col"),
                               (onesb_row, "onesb_row"), (ones_col_f32, "ones_col_f32"),
                               (ones_row_f32, "ones_row_f32"),
                               (l0_b, "l0_b"), (l0_gamma, "l0_gamma"),
                               (l0_beta, "l0_beta"), (bout_sb, "bout"),
                               (decay_sb, "decay")]:
                y.dma_start(out=sbuf_t[:], in_=dt[nm][:, :]).then_inc(dS.h, 16)
                dS.n += 16
        YQ(_)
        consts_ready = dS.n

        def ln_rowmath(n, psA_row, psB_row, t_mark):
            def _(v):
                v.wait_ge(sT.h, t_mark)
                mu = statf[:, 0, 0:n]; w = statf[:, 1, 0:n]; t2 = statf[:, 2, 0:n]
                v.tensor_scalar(mu, psA_row, 1.0 / D, None, ALU.mult)
                v.tensor_scalar(w, psB_row, 1.0 / D, None, ALU.mult)
                v.tensor_tensor(t2, mu, mu, ALU.mult)
                v.tensor_tensor(w, w, t2, ALU.subtract)
                v.tensor_scalar(w, w, 1.0, LN_EPS, ALU.mult, op1=ALU.add)
                ins = v.reciprocal(w, w)
                sV.inc(ins)
            VQ(_)
            m1 = sV.n

            def _(s):
                s.wait_ge(sV.h, m1)
                ins = s.activation(rowA[:, 0:n], statf[:, 1, 0:n], AF.Sqrt)
                sS.inc(ins)
            SQ(_)
            m2 = sS.n

            def _(v):
                v.wait_ge(sS.h, m2)
                v.tensor_tensor(statf[:, 2, 0:n], statf[:, 0, 0:n], rowA[:, 0:n], ALU.mult)
                ins = v.tensor_scalar(rowB[:, 0:n], statf[:, 2, 0:n], -1.0, None, ALU.mult)
                sV.inc(ins)
            VQ(_)
            return sV.n

        with ExitStack() as ph:
            psT = [ph.enter_context(nc.psum_tensor("psTa", [128, 1024], BF16)),
                   ph.enter_context(nc.psum_tensor("psTb", [128, 1024], BF16))]
            xstage = ph.enter_context(nc.sbuf_tensor("xstage", [128, 2, D], BF16))

            ntile = NB * (T // 128)
            ld_m = [0] * ntile; tr_m = [0] * ntile; ev_m = [0] * ntile
            for i in range(ntile):
                s_i, tt = divmod(i, T // 128)

                def _(g):
                    if i >= 2:
                        g.wait_ge(sT.h, tr_m[i - 2])
                    g.dma_start(out=xstage[:, i % 2, :],
                                in_=x_in[s_i, tt * 128:(tt + 1) * 128, :]).then_inc(dG.h, 16)
                    dG.n += 16
                    ld_m[i] = dG.n
                GQ(_)

                def _(t):
                    t.wait_ge(dG.h, ld_m[i])
                    if i == 0:
                        t.wait_ge(dS.h, consts_ready)
                    if i >= 2:
                        t.wait_ge(sV.h, ev_m[i - 2])
                    for c in range(NC):
                        ins = t.transpose(psT[i % 2][:, c * 128:(c + 1) * 128],
                                          xstage[:, i % 2, c * 128:(c + 1) * 128], ident[:])
                    sT.inc(ins)
                    tr_m[i] = sT.n
                TQ(_)

                def _(v):
                    v.wait_ge(sT.h, tr_m[i])
                    t0 = s_i * T + tt * 128
                    for c in range(NC):
                        ins = v.tensor_copy(X[:, c, t0:t0 + 128],
                                            psT[i % 2][:, c * 128:(c + 1) * 128])
                    sV.inc(ins)
                    ev_m[i] = sV.n
                VQ(_)
            in_T, in_V = sT.n, sV.n

        def tap(name, tV, tT, tS):
            if name not in tap_out:
                return
            def _(g):
                g.wait_ge(sV.h, tV); g.wait_ge(sT.h, tT); g.wait_ge(sS.h, tS)
                g.dma_start(out=tap_out[name][:, :, :], in_=X[:]).then_inc(dG.h, 16)
                dG.n += 16
            GQ(_)

        tap("xin", in_V, in_T, sS.n)

        with ExitStack() as ph:
            Wrow = ph.enter_context(nc.sbuf_tensor("b1Wrow", [128, NC, D], BF16))
            Wt = ph.enter_context(nc.sbuf_tensor("b1Wt", [128, NC, D], BF16))
            Sbuf = ph.enter_context(nc.sbuf_tensor("b1S", [128, NC, T], BF16))
            Ybuf = ph.enter_context(nc.sbuf_tensor("b1Y", [128, NC, T], BF16))
            a_tile = ph.enter_context(nc.sbuf_tensor("b1a", [128, T], F32))
            a_row = ph.enter_context(nc.sbuf_tensor("b1ar", [1, 512], F32))
            a_val = ph.enter_context(nc.sbuf_tensor("b1av", [1, 1], F32))
            y2t = ph.enter_context(nc.sbuf_tensor("b1y2", [128, 2, 512], BF16))
            psY = [ph.enter_context(nc.psum_tensor("b1psY0", [128, 512], F32)),
                   ph.enter_context(nc.psum_tensor("b1psY1", [128, 512], F32)),
                   None, None]
            psSA = ph.enter_context(nc.psum_tensor("b1psSA", [128, 512], F32))
            psSB = ph.enter_context(nc.psum_tensor("b1psSB", [128, 512], F32))
            psBa = ph.enter_context(nc.psum_tensor("b1psBa", [128, 512], F32))
            psBb = ph.enter_context(nc.psum_tensor("b1psBb", [128, 512], F32))
            psT = [ph.enter_context(nc.psum_tensor("b1psT0", [128, 1024], BF16)),
                   ph.enter_context(nc.psum_tensor("b1psT1", [128, 1024], BF16))]
            psY[2] = psBa
            psY[3] = psBb

            def _(g):
                g.wait_ge(sT.h, in_T)
                g.wait_ge(sV.h, in_V)
                g.dma_start(out=Wrow[:],
                            in_=dt["l0_W"].rearrange("(c p) d -> p c d", p=128)).then_inc(dG.h, 16)
                dG.n += 16
            GQ(_)
            wrow_ok = dG.n

            def _(s):
                s.wait_ge(dS.h, consts_ready)
                ins = s.activation(a_val[:], decay_sb[:], AF.Sigmoid)
                sS.inc(ins)
            SQ(_)
            sig_ok = sS.n

            def _(v):
                v.wait_ge(sS.h, sig_ok)
                ins = v.tensor_scalar(a_row[:], ones_row_f32[:], a_val[:], None, ALU.mult)
                sV.inc(ins)
            VQ(_)
            arow_okV = sV.n

            at_t = [0] * 4; at_v = [0] * 4
            for j in range(4):
                def _(t):
                    if j == 0:
                        t.wait_ge(sV.h, max(arow_okV, in_V))
                    if j >= 2:
                        t.wait_ge(sV.h, at_v[j - 2])
                    bank = psBa if j % 2 == 0 else psBb
                    ins = t.matmul(bank[:, :], ones_col_f32[:, :], a_row[:, :],
                                   start=True, stop=True)
                    sT.inc(ins)
                    at_t[j] = sT.n
                TQ(_)

                def _(v):
                    v.wait_ge(sT.h, at_t[j])
                    bank = psBa if j % 2 == 0 else psBb
                    ins = v.tensor_copy(a_tile[:, j * 512:(j + 1) * 512], bank[:, :])
                    sV.inc(ins)
                    at_v[j] = sV.n
                VQ(_)
            atile_ok = sV.n

            wt_t = [0] * 64; wt_v = [0] * 64
            for i in range(64):
                ce, cd = divmod(i, NC)

                def _(t):
                    if i == 0:
                        t.wait_ge(dG.h, wrow_ok)
                        t.wait_ge(sV.h, in_V)
                    if i >= 2:
                        t.wait_ge(sV.h, wt_v[i - 2])
                    ins = t.transpose(psT[i % 2][:, 0:128],
                                      Wrow[:, ce, cd * 128:(cd + 1) * 128], ident[:])
                    sT.inc(ins)
                    wt_t[i] = sT.n
                TQ(_)

                def _(v):
                    v.wait_ge(sT.h, wt_t[i])
                    ins = v.tensor_copy(Wt[:, cd, ce * 128:(ce + 1) * 128],
                                        psT[i % 2][:, 0:128])
                    sV.inc(ins)
                    wt_v[i] = sV.n
                VQ(_)
            wt_ok = sV.n

            prev_y_lastS = 0
            prev_y_lastV = 0
            prev_r2_T = 0
            for s_i in range(NB):
                tb = s_i * T

                def _(v):
                    if s_i > 0:
                        v.wait_ge(sT.h, prev_r2_T)
                    for c in range(NC):
                        ins = v.tensor_tensor_scan(Sbuf[:, c, :], a_tile[:],
                                                   X[:, c, tb:tb + T], 0.0,
                                                   ALU.mult, ALU.add)
                    sV.inc(ins)
                VQ(_)
                scan_ok = sV.n

                ymm = [0] * 32; yev = [0] * 32; yevV = [0] * 32; yst_t = [0] * 32
                ystat_v = [0] * 4; yab_v = [0] * 4; ybc_t = [0] * 4

                def emit_ymm(i):
                    g, eg = divmod(i, NC)
                    def _(t):
                        if i == 0:
                            t.wait_ge(sV.h, max(scan_ok, wt_ok))
                            if s_i > 0:
                                t.wait_ge(sS.h, prev_y_lastS)
                                t.wait_ge(sV.h, prev_y_lastV)
                        if i >= 4:
                            t.wait_ge(sS.h, yev[i - 4])
                            t.wait_ge(sV.h, yevV[i - 4])
                        if i == 0 and s_i == 0:
                            t.wait_ge(sV.h, atile_ok)
                        for k in range(NC):
                            ins = t.matmul(psY[i % 4][:, :], Wt[:, k, eg * 128:(eg + 1) * 128],
                                           Sbuf[:, k, g * 512:(g + 1) * 512],
                                           start=(k == 0), stop=(k == NC - 1))
                        sT.inc(ins)
                        ymm[i] = sT.n
                    TQ(_)

                def emit_yev(i):
                    g, eg = divmod(i, NC)
                    def _(s):
                        s.wait_ge(sT.h, ymm[i])
                        if i >= 2:
                            s.wait_ge(sT.h, yst_t[i - 2])
                        ins = s.activation(y2t[:, i % 2, :], psY[i % 4][:, :],
                                           AF.Square, bias=l0_b[:, eg:eg + 1])
                        sS.inc(ins)
                        yev[i] = sS.n
                    SQ(_)

                    def _(v):
                        v.wait_ge(sS.h, yev[i])
                        if i >= 2:
                            v.wait_ge(sT.h, yst_t[i - 2])
                        ins = v.tensor_scalar(Ybuf[:, eg, g * 512:(g + 1) * 512],
                                              psY[i % 4][:, :], l0_b[:, eg:eg + 1],
                                              None, ALU.add)
                        sV.inc(ins)
                        yevV[i] = sV.n
                    VQ(_)

                def emit_yst(i):
                    g, eg = divmod(i, NC)
                    def _(t):
                        t.wait_ge(sS.h, yev[i])
                        t.wait_ge(sV.h, yevV[i])
                        if eg == 0 and g > 0:
                            t.wait_ge(sV.h, yab_v[g - 1])
                        ins = t.matmul(psSA[0:1, :], onesb_col[:, :],
                                       Ybuf[:, eg, g * 512:(g + 1) * 512],
                                       start=(eg == 0), stop=(eg == NC - 1))
                        ins = t.matmul(psSB[0:1, :], onesb_col[:, :], y2t[:, i % 2, :],
                                       start=(eg == 0), stop=(eg == NC - 1))
                        sT.inc(ins)
                        yst_t[i] = sT.n
                    TQ(_)

                def emit_ygrp(g):
                    rows_ok = ln_rowmath(512, psSA[0:1, :], psSB[0:1, :], yst_t[g * 8 + 7])
                    ystat_v[g] = rows_ok

                    def _(t):
                        t.wait_ge(sV.h, rows_ok)
                        ins = t.matmul(psSA[:, :], onesb_row[:, :], rowA[:, :],
                                       start=True, stop=True)
                        ins = t.matmul(psSB[:, :], onesb_row[:, :], rowB[:, :],
                                       start=True, stop=True)
                        sT.inc(ins)
                        ybc_t[g] = sT.n
                    TQ(_)

                    def _(v):
                        v.wait_ge(sT.h, ybc_t[g])
                        v.tensor_copy(Abuf[:, g * 512:(g + 1) * 512], psSA[:, :])
                        ins = v.tensor_copy(Bbuf[:, g * 512:(g + 1) * 512], psSB[:, :])
                        sV.inc(ins)
                        yab_v[g] = sV.n
                    VQ(_)

                for i in range(32):
                    emit_ymm(i)
                    emit_yev(i)
                    if i >= 1:
                        emit_yst(i - 1)
                        if (i - 1) % 8 == 7:
                            emit_ygrp((i - 1) // 8)
                emit_yst(31)
                emit_ygrp(3)
                prev_y_lastS = yev[31]
                prev_y_lastV = yevV[31]

                def _(v):
                    for eg in range(NC):
                        yv = Ybuf[:, eg, :]
                        v.tensor_tensor(yv, yv, Abuf[:, 0:T], ALU.mult)
                        v.tensor_tensor(yv, yv, Bbuf[:, 0:T], ALU.add)
                        v.tensor_scalar(yv, yv, l0_gamma[:, eg:eg + 1],
                                        l0_beta[:, eg:eg + 1], ALU.mult, op1=ALU.add)
                        ins = v.tensor_tensor(X[:, eg, tb:tb + T], X[:, eg, tb:tb + T],
                                              yv, ALU.add)
                    sV.inc(ins)
                VQ(_)
                r_ok = sV.n

                def _(s):
                    s.wait_ge(sV.h, r_ok)
                    for c in range(NC):
                        ins = s.activation(Sbuf[:, c, :], X[:, c, tb:tb + T], AF.Square)
                    sS.inc(ins)
                SQ(_)
                r2_ok = sS.n

                ost_t = [0] * 4; ostat_v = [0] * 4; obc_t = [0] * 4; oab_v = [0] * 4
                for g in range(4):
                    def _(t):
                        t.wait_ge(sS.h, r2_ok)
                        if g > 0:
                            t.wait_ge(sV.h, ostat_v[g - 1])
                        else:
                            t.wait_ge(sV.h, yab_v[3])
                        for c in range(NC):
                            ins = t.matmul(psSA[0:1, :], onesb_col[:, :],
                                           X[:, c, tb + g * 512:tb + (g + 1) * 512],
                                           start=(c == 0), stop=(c == NC - 1))
                            ins = t.matmul(psSB[0:1, :], onesb_col[:, :],
                                           Sbuf[:, c, g * 512:(g + 1) * 512],
                                           start=(c == 0), stop=(c == NC - 1))
                        sT.inc(ins)
                        ost_t[g] = sT.n
                    TQ(_)
                    rows_ok = ln_rowmath(512, psSA[0:1, :], psSB[0:1, :], ost_t[g])
                    ostat_v[g] = rows_ok

                    def _(t):
                        t.wait_ge(sV.h, rows_ok)
                        if g >= 2:
                            t.wait_ge(sV.h, oab_v[g - 2])
                        ins = t.matmul(psBa[:, :], onesb_row[:, :], rowA[:, :],
                                       start=True, stop=True)
                        ins = t.matmul(psBb[:, :], onesb_row[:, :], rowB[:, :],
                                       start=True, stop=True)
                        sT.inc(ins)
                        obc_t[g] = sT.n
                    TQ(_)

                    def _(v):
                        v.wait_ge(sT.h, obc_t[g])
                        v.tensor_copy(Abuf[:, g * 512:(g + 1) * 512], psBa[:, :])
                        ins = v.tensor_copy(Bbuf[:, g * 512:(g + 1) * 512], psBb[:, :])
                        sV.inc(ins)
                        oab_v[g] = sV.n
                    VQ(_)

                def _(v):
                    for c in range(NC):
                        xv = X[:, c, tb:tb + T]
                        v.tensor_tensor(xv, xv, Abuf[:, 0:T], ALU.mult)
                        ins = v.tensor_tensor(xv, xv, Bbuf[:, 0:T], ALU.add)
                    sV.inc(ins)
                VQ(_)
                prev_r2_T = ost_t[3]

            b1_V, b1_T, b1_S = sV.n, sT.n, sS.n
        tap("b1", b1_V, b1_T, b1_S)

        with ExitStack() as ph:
            wgT = ph.enter_context(nc.sbuf_tensor("wgT", [128, 2, NC, MOE_I], BF16))
            wuT = ph.enter_context(nc.sbuf_tensor("wuT", [128, 2, NC, MOE_I], BF16))
            wdT = ph.enter_context(nc.sbuf_tensor("wdT", [128, 2, 4, D], BF16))
            gateT = ph.enter_context(nc.sbuf_tensor("gateT", [128, NC, MOE_E], BF16))
            gatebT = ph.enter_context(nc.sbuf_tensor("gatebT", [MOE_E, 1], F32))
            Rsb = ph.enter_context(nc.sbuf_tensor("Rsb", [MOE_E, T], F32))
            Rt = ph.enter_context(nc.sbuf_tensor("Rt", [128, 16, MOE_E], F32))
            ctok = ph.enter_context(nc.sbuf_tensor("ctok", [128, 2, 128], F32))
            cF = ph.enter_context(nc.sbuf_tensor("cF", [MOE_E, T], BF16))
            cRow = ph.enter_context(nc.sbuf_tensor("cRow", [1, 2, T], BF16))
            cb = ph.enter_context(nc.sbuf_tensor("cb", [128, T], BF16))
            accb = ph.enter_context(nc.sbuf_tensor("accb", [128, NC, T], BF16))
            slb = ph.enter_context(nc.sbuf_tensor("slb", [128, 2, 4, 256], BF16))
            hpb = ph.enter_context(nc.sbuf_tensor("hpb", [128, 2, 4, 256], BF16))
            tv1 = ph.enter_context(nc.sbuf_tensor("tv1", [128, 4], F32))
            tv2 = ph.enter_context(nc.sbuf_tensor("tv2", [128, MOE_E], F32))
            tv3 = ph.enter_context(nc.sbuf_tensor("tv3", [128, MOE_E], F32))
            psHG = ph.enter_context(nc.psum_tensor("psHG", [128, 4, 256], F32))
            psHU = ph.enter_context(nc.psum_tensor("psHU", [128, 4, 256], F32))
            psACC = ph.enter_context(nc.psum_tensor("psACC", [128, NC, 256], F32))

            def _(g):
                g.wait_ge(sV.h, b1_V); g.wait_ge(sT.h, b1_T); g.wait_ge(sS.h, b1_S)
                g.dma_start(out=gateT[:],
                            in_=dt["gate_w"].rearrange("(c p) e -> p c e", p=128)).then_inc(dG.h, 16)
                dG.n += 16
            GQ(_)
            gate_okG = dG.n

            def _(y):
                y.wait_ge(sV.h, b1_V); y.wait_ge(sT.h, b1_T); y.wait_ge(sS.h, b1_S)
                y.dma_start(out=gatebT[:], in_=dt["gate_b"][:, :]).then_inc(dS.h, 16)
                dS.n += 16
            YQ(_)
            gate_okS = dS.n

            prev_psacc_free_V = 0
            prev_unit16_T = 0
            prev_epB2_V = 0
            for h_i in range(NB):
                hb = h_i * T
                wload = {}
                hmm = {}; silu_m = {}; h_m = {}; dmm = {}; accev = {}
                crow_dma = {}; crow_t = {}; cb_ok = {}
                ep_r = {}; ep_r2 = {}; ep_st = {}; ep_bc = {}; ep_ab = {}

                def load_unit(u):
                    def _(g):
                        slot = u % 2
                        if u >= 2:
                            g.wait_ge(sT.h, dmm[(u - 2, 7)])
                        elif h_i > 0:
                            g.wait_ge(sT.h, prev_unit16_T)
                        else:
                            g.wait_ge(sV.h, b1_V); g.wait_ge(sT.h, b1_T)
                            g.wait_ge(sS.h, b1_S)
                        if u == 0:
                            srcs = [dt["sh_wg"], dt["sh_wu"], dt["sh_wd"]]
                        else:
                            e = u - 1
                            srcs = [dt["ex_wg"][e], dt["ex_wu"][e], dt["ex_wd"][e]]
                        g.dma_start(out=wgT[:, slot, :, :],
                                    in_=srcs[0].rearrange("(c p) i -> p c i", p=128)).then_inc(dG.h, 16)
                        g.dma_start(out=wuT[:, slot, :, :],
                                    in_=srcs[1].rearrange("(c p) i -> p c i", p=128)).then_inc(dG.h, 16)
                        g.dma_start(out=wdT[:, slot, :, :],
                                    in_=srcs[2].rearrange("(c p) i -> p c i", p=128)).then_inc(dG.h, 16)
                        dG.n += 48
                        wload[u] = dG.n
                    GQ(_)

                load_unit(0)
                load_unit(1)

                rt_t = [0] * 4; rsig = [0] * 4
                for g4 in range(4):
                    def _(t):
                        if g4 == 0:
                            t.wait_ge(dG.h, gate_okG)
                            t.wait_ge(sV.h, max(b1_V, prev_psacc_free_V))
                        else:
                            t.wait_ge(sS.h, rsig[g4 - 1])
                        for k in range(NC):
                            ins = t.matmul(psACC[0:16, 0:2, :], gateT[:, k, :],
                                           X[:, k, hb + g4 * 512:hb + (g4 + 1) * 512],
                                           start=(k == 0), stop=(k == NC - 1))
                        sT.inc(ins)
                        rt_t[g4] = sT.n
                    TQ(_)

                    def _(s):
                        s.wait_ge(sT.h, rt_t[g4])
                        if g4 == 0:
                            s.wait_ge(dS.h, gate_okS)
                        ins = s.activation(Rsb[:, g4 * 512:(g4 + 1) * 512],
                                           psACC[0:16, 0:2, :], AF.Sigmoid,
                                           bias=gatebT[:, :])
                        sS.inc(ins)
                        rsig[g4] = sS.n
                    SQ(_)

                rtr_t = [0] * 16; rte_v = [0] * 16
                for i in range(16):
                    def _(t):
                        if i == 0:
                            t.wait_ge(sS.h, rsig[3])
                            t.wait_ge(sV.h, prev_epB2_V)
                        if i >= 2:
                            t.wait_ge(sV.h, rte_v[i - 2])
                        reg = psHG[:, 0, 0:16] if i % 2 == 0 else psHG[:, 2, 0:16]
                        ins = t.transpose(reg, Rsb[:, i * 128:(i + 1) * 128], ident_f32[0:16, 0:16])
                        sT.inc(ins)
                        rtr_t[i] = sT.n
                    TQ(_)

                    def _(v):
                        v.wait_ge(sT.h, rtr_t[i])
                        reg = psHG[:, 0, 0:16] if i % 2 == 0 else psHG[:, 2, 0:16]
                        ins = v.tensor_copy(Rt[:, i, :], reg)
                        sV.inc(ins)
                        rte_v[i] = sV.n
                    VQ(_)

                top_v = [0] * 16; ctr_t = [0] * 16; cte_v = [0] * 16
                def _(v):
                    v.memset(ctok[:, 0, :], 0.0)
                    ins = v.memset(ctok[:, 1, :], 0.0)
                    sV.inc(ins)
                VQ(_)
                for i in range(16):
                    def _(v):
                        v.wait_ge(sV.h, rte_v[i])
                        if i >= 2:
                            v.wait_ge(sT.h, ctr_t[i - 2])
                        r = Rt[:, i, :]
                        m1 = tv1[:, 0:1]; m2 = tv1[:, 1:2]
                        den = tv1[:, 2:3]; rec = tv1[:, 3:4]
                        def bump(ins):
                            sV.inc(ins); v.wait_ge(sV.h, sV.n)
                        bump(v.reduce_max(m1, r, AX.X))
                        bump(v.tensor_scalar(tv2[:, :], r, m1, None, ALU.is_ge))
                        bump(v.scalar_tensor_tensor(tv3[:, :], tv2[:, :], -1e30, r,
                                                    ALU.mult, ALU.add))
                        bump(v.reduce_max(m2, tv3[:, :], AX.X))
                        bump(v.tensor_tensor(den, m1, m2, ALU.add))
                        bump(v.reciprocal(rec, den))
                        bump(v.tensor_scalar(tv2[:, :], r, m2, None, ALU.is_ge))
                        ins = v.scalar_tensor_tensor(ctok[:, i % 2, 0:16], tv2[:, :], rec,
                                                     r, ALU.mult, ALU.mult)
                        sV.inc(ins)
                        top_v[i] = sV.n
                    VQ(_)

                    def _(t):
                        t.wait_ge(sV.h, top_v[i])
                        if i >= 2:
                            t.wait_ge(sV.h, cte_v[i - 2])
                        reg = psHG[:, 0, 0:128] if i % 2 == 0 else psHG[:, 2, 0:128]
                        ins = t.transpose(reg, ctok[:, i % 2, :], ident_f32[:])
                        sT.inc(ins)
                        ctr_t[i] = sT.n
                    TQ(_)

                    def _(v):
                        v.wait_ge(sT.h, ctr_t[i])
                        reg = psHG[0:16, 0, 0:128] if i % 2 == 0 else psHG[0:16, 2, 0:128]
                        ins = v.tensor_copy(cF[:, i * 128:(i + 1) * 128], reg)
                        sV.inc(ins)
                        cte_v[i] = sV.n
                    VQ(_)
                cf_ok = sV.n

                def crow_dma_issue(uu):
                    def _(y):
                        if uu == 1:
                            y.wait_ge(sV.h, cf_ok)
                        if uu >= 3:
                            y.wait_ge(sT.h, crow_t[uu - 2])
                        y.dma_start(out=cRow[0:1, uu % 2, :],
                                    in_=cF[uu - 1:uu, :]).then_inc(dS.h, 16)
                        dS.n += 16
                        crow_dma[uu] = dS.n
                    YQ(_)

                crow_dma_issue(1)
                for u in range(17):
                    slot = u % 2
                    if u + 1 <= 16:
                        load_unit(u + 1)
                    if 2 <= u + 1 <= 16:
                        crow_dma_issue(u + 1)

                    if u >= 1:
                        e = u - 1
                        cbm = {}

                        def _(t):
                            t.wait_ge(dS.h, crow_dma[u])
                            if (u - 1, 7) in h_m:
                                t.wait_ge(sV.h, h_m[(u - 1, 7)])
                            cr = cRow[0:1, u % 2, :]
                            t.matmul(psHG[:, 0:2, :], onesb_row[:, :],
                                     cr[:, 0:512], start=True, stop=True)
                            t.matmul(psHG[:, 2:4, :], onesb_row[:, :],
                                     cr[:, 512:1024], start=True, stop=True)
                            t.matmul(psHU[:, 0:2, :], onesb_row[:, :],
                                     cr[:, 1024:1536], start=True, stop=True)
                            ins = t.matmul(psHU[:, 2:4, :], onesb_row[:, :],
                                           cr[:, 1536:2048], start=True, stop=True)
                            sT.inc(ins)
                            cbm["t1"] = sT.n
                            crow_t[u] = sT.n
                        TQ(_)

                        def _(v):
                            v.wait_ge(sT.h, cbm["t1"])
                            v.tensor_copy(cb[:, 0:1024],
                                          psHG[:, :, :].rearrange("p a b -> p (a b)"))
                            ins = v.tensor_copy(cb[:, 1024:2048],
                                                psHU[:, :, :].rearrange("p a b -> p (a b)"))
                            sV.inc(ins)
                            cbm["v1"] = sV.n
                        VQ(_)
                        cb_ok[u] = cbm["v1"]

                    def emit_down(u, g):
                        def _(t):
                            t.wait_ge(sV.h, h_m[(u, g)])
                            key = (u, g - 1) if g > 0 else (u - 1, 7)
                            if key in accev:
                                t.wait_ge(sV.h, accev[key])
                            if u == 0 and g == 0:
                                t.wait_ge(sS.h, rsig[3])
                            if u > 0:
                                for b in range(4):
                                    t.matmul(psACC[:, 2 * b:2 * b + 2, :], ident[:],
                                             accb[:, 2 * b:2 * b + 2, g * 256:(g + 1) * 256],
                                             start=True, stop=False)
                            for dc in range(NC):
                                for ic in range(4):
                                    ins = t.matmul(psACC[:, dc, :],
                                                   wdT[:, u % 2, ic, dc * 128:(dc + 1) * 128],
                                                   hpb[:, g % 2, ic, :],
                                                   start=(u == 0 and ic == 0 and dc % 2 == 0),
                                                   stop=(ic == 3 and dc % 2 == 1))
                            sT.inc(ins)
                            dmm[(u, g)] = sT.n
                        TQ(_)

                        def _(v):
                            v.wait_ge(sT.h, dmm[(u, g)])
                            ins = v.tensor_copy(accb[:, :, g * 256:(g + 1) * 256],
                                                psACC[:, :, :])
                            sV.inc(ins)
                            accev[(u, g)] = sV.n
                        VQ(_)

                    for g in range(8):
                        gb = hb + g * 256

                        def _(t):
                            if g == 0:
                                t.wait_ge(dG.h, wload[u])
                                if u == 0:
                                    t.wait_ge(sV.h, cf_ok)
                                else:
                                    t.wait_ge(sV.h, cb_ok[u])
                            key = (u, g - 1) if g > 0 else (u - 1, 7)
                            if key in h_m:
                                t.wait_ge(sV.h, h_m[key])
                            for ic in range(4):
                                for k in range(NC):
                                    t.matmul(psHG[:, ic, :],
                                             wgT[:, slot, k, ic * 128:(ic + 1) * 128],
                                             X[:, k, gb:gb + 256],
                                             start=(k == 0), stop=(k == NC - 1))
                            for ic in range(4):
                                for k in range(NC):
                                    ins = t.matmul(psHU[:, ic, :],
                                                   wuT[:, slot, k, ic * 128:(ic + 1) * 128],
                                                   X[:, k, gb:gb + 256],
                                                   start=(k == 0), stop=(k == NC - 1))
                            sT.inc(ins)
                            hmm[(u, g)] = sT.n
                        TQ(_)

                        def _(s):
                            s.wait_ge(sT.h, hmm[(u, g)])
                            ins = s.activation(slb[:, g % 2, :, :], psHG[:, :, :], AF.Silu)
                            sS.inc(ins)
                            silu_m[(u, g)] = sS.n
                        SQ(_)

                        def _(v):
                            v.wait_ge(sS.h, silu_m[(u, g)])
                            hp = hpb[:, g % 2, :, :]
                            if u >= 1:
                                v.tensor_tensor(hp, psHU[:, :, :],
                                                cb[:, g * 256:g * 256 + 256][:, None, :]
                                                .to_broadcast((128, 4, 256)), ALU.mult)
                                ins = v.tensor_tensor(hp, slb[:, g % 2, :, :], hp, ALU.mult)
                            else:
                                ins = v.tensor_tensor(hp, slb[:, g % 2, :, :],
                                                      psHU[:, :, :], ALU.mult)
                            sV.inc(ins)
                            h_m[(u, g)] = sV.n
                        VQ(_)

                        if g >= 1:
                            emit_down(u, g - 1)
                    emit_down(u, 7)

                slbF = slb[:, :, :, :].rearrange("p a b c -> p (a b c)")
                hpbF = hpb[:, :, :, :].rearrange("p a b c -> p (a b c)")
                def r2reg(dc):
                    return (slbF[:, dc * 512:(dc + 1) * 512] if dc < 4
                            else hpbF[:, (dc - 4) * 512:(dc - 3) * 512])
                for gg in range(4):
                    gb = hb + gg * 512

                    def _(v):
                        v.wait_ge(sV.h, accev[(16, 2 * gg + 1)])
                        for dc in range(NC):
                            ins = v.scalar_tensor_tensor(
                                X[:, dc, gb:gb + 512], X[:, dc, gb:gb + 512],
                                2.0, accb[:, dc, gg * 512:(gg + 1) * 512],
                                ALU.mult, ALU.add)
                        sV.inc(ins)
                        ep_r[gg] = sV.n
                    VQ(_)

                    def _(s):
                        s.wait_ge(sV.h, ep_r[gg])
                        s.wait_ge(sT.h, dmm[(16, 7)])
                        for dc in range(NC):
                            ins = s.activation(r2reg(dc),
                                               X[:, dc, gb:gb + 512], AF.Square)
                        sS.inc(ins)
                        ep_r2[gg] = sS.n
                    SQ(_)

                    if gg % 2 == 0:
                        stA = psHG[0:1, 0:2, :]; stB = psHU[0:1, 0:2, :]
                        bcA = psHG[:, 0:2, :]; bcB = psHG[:, 2:4, :]
                    else:
                        stA = psACC[0:1, 0:2, :]; stB = psACC[0:1, 2:4, :]
                        bcA = psACC[:, 0:2, :]; bcB = psACC[:, 2:4, :]

                    def _(t, stA=stA, stB=stB):
                        t.wait_ge(sS.h, ep_r2[gg])
                        if gg == 0:
                            t.wait_ge(sV.h, h_m[(16, 7)])
                        elif gg == 1:
                            t.wait_ge(sV.h, accev[(16, 7)])
                        else:
                            t.wait_ge(sV.h, ep_ab[gg - 2])
                        for c in range(NC):
                            t.matmul(stA, onesb_col[:, :], X[:, c, gb:gb + 512],
                                     start=(c == 0), stop=(c == NC - 1))
                            ins = t.matmul(stB, onesb_col[:, :], r2reg(c),
                                           start=(c == 0), stop=(c == NC - 1))
                        sT.inc(ins)
                        ep_st[gg] = sT.n
                    TQ(_)

                    rows_ok = ln_rowmath(512, stA, stB, ep_st[gg])

                    def _(t, bcA=bcA, bcB=bcB):
                        t.wait_ge(sV.h, rows_ok)
                        ins = t.matmul(bcA, onesb_row[:, :], rowA[:, 0:512],
                                       start=True, stop=True)
                        ins = t.matmul(bcB, onesb_row[:, :], rowB[:, 0:512],
                                       start=True, stop=True)
                        sT.inc(ins)
                        ep_bc[gg] = sT.n
                    TQ(_)

                    def _(v, bcA=bcA, bcB=bcB):
                        v.wait_ge(sT.h, ep_bc[gg])
                        ab = Abuf[:, (gg % 2) * 512:(gg % 2) * 512 + 512]
                        bb = Bbuf[:, (gg % 2) * 512:(gg % 2) * 512 + 512]
                        v.tensor_copy(ab, bcA)
                        v.tensor_copy(bb, bcB)
                        for c in range(NC):
                            xv = X[:, c, gb:gb + 512]
                            v.tensor_tensor(xv, xv, ab, ALU.mult)
                            ins = v.tensor_tensor(xv, xv, bb, ALU.add)
                        sV.inc(ins)
                        ep_ab[gg] = sV.n
                    VQ(_)

                prev_psacc_free_V = ep_ab[3]
                prev_unit16_T = dmm[(16, 7)]
                prev_epB2_V = ep_ab[3]

            b2_V, b2_T, b2_S = sV.n, sT.n, sS.n
        tap("b2", b2_V, b2_T, b2_S)

        with ExitStack() as ph:
            ropeC = ph.enter_context(nc.sbuf_tensor("c_ropeC", [128, T], BF16))
            ropeS = ph.enter_context(nc.sbuf_tensor("c_ropeS", [128, T], BF16))
            masks = ph.enter_context(nc.sbuf_tensor("c_masks", [128, 4, 512], BF16))
            woAll = ph.enter_context(nc.sbuf_tensor("woAll", [128, 8, D], BF16))
            wqh = ph.enter_context(nc.sbuf_tensor("wqh", [128, 2, NC, 128], BF16))
            wkh = ph.enter_context(nc.sbuf_tensor("wkh", [128, 2, NC, 128], BF16))
            wvh = ph.enter_context(nc.sbuf_tensor("wvh", [128, 2, NC, 128], BF16))
            qbuf = ph.enter_context(nc.sbuf_tensor("qbuf", [128, T], BF16))
            kbuf = ph.enter_context(nc.sbuf_tensor("kbuf", [128, T], BF16))
            vbuf = ph.enter_context(nc.sbuf_tensor("vbuf", [128, 16, 128], BF16))
            ptile = ph.enter_context(nc.sbuf_tensor("ptile", [128, 4, 512], BF16))
            q2t = ph.enter_context(nc.sbuf_tensor("q2t", [128, 4, 512], BF16))
            qraw = ph.enter_context(nc.sbuf_tensor("qraw", [128, T], BF16))
            k2t = ph.enter_context(nc.sbuf_tensor("k2t", [128, 2, 512], BF16))
            o_all = ph.enter_context(nc.sbuf_tensor("o_all", [128, 8, T], BF16))
            rsk = ph.enter_context(nc.sbuf_tensor("rsk", [128, 16], F32))
            rskt = ph.enter_context(nc.sbuf_tensor("rskt", [128, 16], F32))
            rs_qb = ph.enter_context(nc.sbuf_tensor("rs_qb", [128, 512], BF16))
            recipb = ph.enter_context(nc.sbuf_tensor("recipb", [128, 512], BF16))
            rowQ = ph.enter_context(nc.sbuf_tensor("rowQ", [1, 512], BF16))
            qtmp = ph.enter_context(nc.sbuf_tensor("qtmp", [128, 512], BF16))
            krot = ph.enter_context(nc.sbuf_tensor("krot", [128, 512], BF16))
            ktc = ph.enter_context(nc.sbuf_tensor("ktc", [128, 512], BF16))
            r2b3 = ph.enter_context(nc.sbuf_tensor("r2b3", [128, NC, 512], BF16))
            psQK = ph.enter_context(nc.psum_tensor("psQK", [128, 2, 512], F32))
            psSc = ph.enter_context(nc.psum_tensor("psSc", [128, 2, 512], F32))
            psO = ph.enter_context(nc.psum_tensor("psO", [128, 2, 512], F32))
            psMix = ph.enter_context(nc.psum_tensor("psMix", [128, 512], F32))
            psDen2 = ph.enter_context(nc.psum_tensor("psDen2", [128, 512], F32))

            def _(g):
                g.wait_ge(sV.h, b2_V); g.wait_ge(sT.h, b2_T); g.wait_ge(sS.h, b2_S)
                for h8 in range(8):
                    g.dma_start(out=woAll[:, h8, :], in_=dt["wo"][h8]).then_inc(dG.h, 16)
                dG.n += 8 * 16
            GQ(_)
            wo_ok = dG.n

            def _(y):
                y.wait_ge(sV.h, b2_V); y.wait_ge(sT.h, b2_T); y.wait_ge(sS.h, b2_S)
                y.dma_start(out=ropeC[:], in_=dt["ropeC"][:, :]).then_inc(dS.h, 16)
                y.dma_start(out=ropeS[:], in_=dt["ropeS"][:, :]).then_inc(dS.h, 16)
                y.dma_start(out=masks[:],
                            in_=dt["masks"].rearrange("m p n -> p m n")).then_inc(dS.h, 16)
                dS.n += 48
            YQ(_)
            rope_okS = dS.n

            qkv_done = {}
            wh_ok = {}

            def load_head(H):
                s_i, h8 = divmod(H, 8)
                def _(g):
                    if H >= 2:
                        g.wait_ge(sT.h, qkv_done[H - 2])
                    for dst, nm in [(wqh, "wq"), (wkh, "wk"), (wvh, "wv")]:
                        g.dma_start(out=dst[:, H % 2, :, :],
                                    in_=dt[nm][:, h8, :].rearrange("(c p) d -> p c d", p=128)
                                    ).then_inc(dG.h, 16)
                    dG.n += 48
                    wh_ok[H] = dG.n
                GQ(_)

            load_head(0)
            load_head(1)

            prev_exp_S = 0
            prev_onorm_last = 0
            prev_head_free = 0

            for s_i in range(NB):
                tb = s_i * T
                for h8 in range(8):
                    H = s_i * 8 + h8
                    slot = H % 2
                    if H + 1 < 16:
                        load_head(H + 1)

                    units = [("k", j) for j in range(4)] + [("q", j) for j in range(4)] \
                          + [("v", j) for j in range(16)]
                    cons_v = {}
                    mm_t = {}
                    ksq_s = {}; msk_t = {}
                    qsq_s = {}; msq_t = {}; qst_v = {}; qsq2_s = {}; qbc_t = {}
                    qev_v = {}
                    rsk_vv = {}

                    def emit_qchain_a(jq):
                        def _(t):
                            t.wait_ge(sS.h, qsq_s[jq])
                            if jq == 0:
                                t.wait_ge(sV.h, rsk_vv["v"])
                            else:
                                t.wait_ge(sV.h, qev_v[jq - 1])
                            ins = t.matmul(psMix[0:1, :], onesb_col[:, :],
                                           q2t[:, jq, :], start=True, stop=True)
                            sT.inc(ins)
                            msq_t[jq] = sT.n
                        TQ(_)

                        def _(v):
                            v.wait_ge(sT.h, msq_t[jq])
                            ins = v.tensor_scalar(statf[:, 0, :], psMix[0:1, :],
                                                  1.0 / 128.0, RMS_EPS, ALU.mult, op1=ALU.add)
                            sV.inc(ins); v.wait_ge(sV.h, sV.n)
                            ins = v.reciprocal(statf[:, 1, :], statf[:, 0, :])
                            sV.inc(ins)
                            qst_v[jq] = sV.n
                        VQ(_)

                        def _(s):
                            s.wait_ge(sV.h, qst_v[jq])
                            ins = s.activation(rowQ[:, :], statf[:, 1, :], AF.Sqrt)
                            sS.inc(ins)
                            qsq2_s[jq] = sS.n
                        SQ(_)

                    def emit_qchain_b(jq):
                        def _(t):
                            t.wait_ge(sS.h, qsq2_s[jq])
                            ins = t.matmul(psMix[:, :], onesb_row[:, :], rowQ[:, :],
                                           start=True, stop=True)
                            sT.inc(ins)
                            qbc_t[jq] = sT.n
                        TQ(_)

                        def _(v):
                            v.wait_ge(sT.h, qbc_t[jq])
                            v.tensor_copy(rs_qb[:, :], psMix[:, :])
                            v.tensor_tensor(qtmp[:, :], qraw[:, jq * 512:(jq + 1) * 512],
                                            rs_qb[:, :], ALU.mult)
                            v.tensor_tensor(ktc[:, :], qtmp[:, :],
                                            ropeC[:, jq * 512:(jq + 1) * 512], ALU.mult)
                            v.tensor_copy(krot[0:64, :], qtmp[64:128, :])
                            v.tensor_copy(krot[64:128, :], qtmp[0:64, :])
                            v.tensor_tensor(krot[:, :], krot[:, :],
                                            ropeS[:, jq * 512:(jq + 1) * 512], ALU.mult)
                            ins = v.tensor_tensor(qbuf[:, jq * 512:(jq + 1) * 512],
                                                  ktc[:, :], krot[:, :], ALU.add)
                            sV.inc(ins)
                            qev_v[jq] = sV.n
                        VQ(_)

                    for ui, (kind, j) in enumerate(units):
                        def _(t):
                            if ui == 0:
                                t.wait_ge(dG.h, wh_ok[H])
                                if H == 0:
                                    t.wait_ge(sV.h, b2_V)
                                    t.wait_ge(sS.h, b2_S)
                                    t.wait_ge(dS.h, rope_okS)
                                else:
                                    t.wait_ge(sV.h, prev_head_free)
                            if ui >= 2:
                                for sem, mk in cons_v[ui - 2]:
                                    t.wait_ge(sem.h, mk)
                            w = {"q": wqh, "k": wkh, "v": wvh}[kind]
                            if kind in ("q", "k"):
                                for k in range(NC):
                                    ins = t.matmul(psQK[:, ui % 2, :], w[:, slot, k, :],
                                                   X[:, k, tb + j * 512:tb + (j + 1) * 512],
                                                   start=(k == 0), stop=(k == NC - 1))
                            else:
                                for k in range(NC):
                                    ins = t.matmul(psQK[:, ui % 2, 0:128],
                                                   X[:, k, tb + j * 128:tb + (j + 1) * 128],
                                                   w[:, slot, k, :],
                                                   start=(k == 0), stop=(k == NC - 1))
                            sT.inc(ins)
                            mm_t[ui] = sT.n
                        TQ(_)

                        if kind == "v":
                            def _(v):
                                v.wait_ge(sT.h, mm_t[ui])
                                ins = v.tensor_copy(vbuf[:, j, :], psQK[:, ui % 2, 0:128])
                                sV.inc(ins)
                                cons_v[ui] = [(sV, sV.n)]
                            VQ(_)
                            if j == 0:
                                emit_qchain_a(0)
                            elif j % 4 == 2 and j // 4 < 4:
                                emit_qchain_b(j // 4)
                                if j // 4 + 1 < 4:
                                    emit_qchain_a(j // 4 + 1)
                        elif kind == "k":
                            def _(s):
                                s.wait_ge(sT.h, mm_t[ui])
                                if ui >= 2:
                                    s.wait_ge(sT.h, msk_t.get(ui - 2, 0))
                                ins = s.activation(k2t[:, j % 2, :], psQK[:, ui % 2, :],
                                                   AF.Square)
                                sS.inc(ins)
                                ksq_s[ui] = sS.n
                            SQ(_)

                            def _(t):
                                t.wait_ge(sS.h, ksq_s[ui])
                                for jj in range(4):
                                    ins = t.matmul(psMix[:, j * 4 + jj:j * 4 + jj + 1],
                                                   k2t[:, j % 2, jj * 128:(jj + 1) * 128],
                                                   onesb_col[:, :], start=True, stop=True)
                                sT.inc(ins)
                                msk_t[ui] = sT.n
                            TQ(_)

                            def _(v):
                                v.wait_ge(sS.h, ksq_s[ui])
                                v.tensor_tensor(ktc[:, :], psQK[:, ui % 2, :],
                                                ropeC[:, j * 512:(j + 1) * 512], ALU.mult)
                                v.tensor_copy(krot[0:64, :], psQK[64:128, ui % 2, :])
                                v.tensor_copy(krot[64:128, :], psQK[0:64, ui % 2, :])
                                v.tensor_tensor(krot[:, :], krot[:, :],
                                                ropeS[:, j * 512:(j + 1) * 512], ALU.mult)
                                ins = v.tensor_tensor(kbuf[:, j * 512:(j + 1) * 512],
                                                      ktc[:, :], krot[:, :], ALU.add)
                                sV.inc(ins)
                                cons_v[ui] = [(sV, sV.n)]
                            VQ(_)

                            if j == 3:
                                def _(v):
                                    v.wait_ge(sT.h, msk_t[ui])
                                    ins = v.tensor_scalar(rskt[:, :], psMix[:, 0:16],
                                                          1.0 / 128.0, RMS_EPS, ALU.mult,
                                                          op1=ALU.add)
                                    sV.inc(ins); v.wait_ge(sV.h, sV.n)
                                    ins = v.reciprocal(rskt[:, :], rskt[:, :])
                                    sV.inc(ins)
                                    rsk_vv["a"] = sV.n
                                VQ(_)

                                def _(s):
                                    s.wait_ge(sV.h, rsk_vv["a"])
                                    ins = s.activation(rsk[:, :], rskt[:, :], AF.Sqrt)
                                    sS.inc(ins)
                                    rsk_vv["s"] = sS.n
                                SQ(_)

                                def _(v):
                                    v.wait_ge(sS.h, rsk_vv["s"])
                                    ins = v.tensor_scalar(rsk[:, :], rsk[:, :],
                                                          1.0 / 128.0, None, ALU.mult)
                                    sV.inc(ins)
                                    rsk_vv["v"] = sV.n
                                VQ(_)
                        else:
                            def _(s):
                                s.wait_ge(sT.h, mm_t[ui])
                                ins = s.activation(q2t[:, j, :], psQK[:, ui % 2, :],
                                                   AF.Square)
                                sS.inc(ins)
                                qsq_s[j] = sS.n
                            SQ(_)

                            def _(v):
                                v.wait_ge(sS.h, qsq_s[j])
                                ins = v.tensor_copy(qraw[:, j * 512:(j + 1) * 512],
                                                    psQK[:, ui % 2, :])
                                sV.inc(ins)
                                cons_v[ui] = [(sV, sV.n), (sS, qsq_s[j])]
                            VQ(_)

                    qkv_done[H] = sT.n
                    qkv_last_v = sV.n

                    exp_s = {}; msk_v = {}; sc_t = {}; pv_t = {}
                    onorm_v = {}; den_v = {}; denb_t = {}
                    tiles = [(qtg, kt) for qtg in range(4) for kt in range(4 * qtg + 4)]
                    scslots = [psSc[:, 0, :], psSc[:, 1, :], psQK[:, 0, :], psQK[:, 1, :]]

                    def emit_sc(m):
                        qtg, kt = tiles[m]
                        def _(t):
                            if m == 0:
                                t.wait_ge(sV.h, qkv_last_v)
                                if prev_exp_S:
                                    t.wait_ge(sS.h, prev_exp_S)
                            if m >= 4:
                                t.wait_ge(sS.h, exp_s[m - 4])
                            ins = t.matmul(scslots[m % 4],
                                           kbuf[:, kt * 128:(kt + 1) * 128],
                                           qbuf[:, qtg * 512:(qtg + 1) * 512],
                                           start=True, stop=True)
                            sT.inc(ins)
                            sc_t[m] = sT.n
                        TQ(_)

                        def _(s):
                            s.wait_ge(sT.h, sc_t[m])
                            if m >= 4:
                                s.wait_ge(sT.h, pv_t[m - 4])
                            ins = s.activation(ptile[:, m % 4, :], scslots[m % 4],
                                               AF.Exp, scale=rsk[:, kt:kt + 1])
                            sS.inc(ins)
                            exp_s[m] = sS.n
                        SQ(_)

                        if kt >= 4 * qtg:
                            def _(v):
                                v.wait_ge(sS.h, exp_s[m])
                                ins = v.tensor_tensor(ptile[:, m % 4, :], ptile[:, m % 4, :],
                                                      masks[:, kt - 4 * qtg, :], ALU.mult)
                                sV.inc(ins)
                                msk_v[m] = sV.n
                            VQ(_)

                    def emit_pv(m):
                        qtg, kt = tiles[m]
                        nkt = 4 * qtg + 4
                        def _(t):
                            if m in msk_v:
                                t.wait_ge(sV.h, msk_v[m])
                            else:
                                t.wait_ge(sS.h, exp_s[m])
                            if kt == 0:
                                if qtg > 0:
                                    t.wait_ge(sV.h, den_v[qtg - 1])
                                if qtg >= 2:
                                    t.wait_ge(sV.h, onorm_v[qtg - 2])
                            t.matmul(psMix[0:1, :], onesb_col[:, :], ptile[:, m % 4, :],
                                     start=(kt == 0), stop=(kt == nkt - 1))
                            ins = t.matmul(psO[:, qtg % 2, :], vbuf[:, kt, :],
                                           ptile[:, m % 4, :],
                                           start=(kt == 0), stop=(kt == nkt - 1))
                            sT.inc(ins)
                            pv_t[m] = sT.n
                        TQ(_)

                        if kt == nkt - 1:
                            def _(v):
                                v.wait_ge(sT.h, pv_t[m])
                                with nc.allow_low_precision(reason="bf16 recip row; tile is bf16"):
                                    ins = v.reciprocal(rowQ[:, :], psMix[0:1, :])
                                sV.inc(ins)
                                den_v[qtg] = sV.n
                            VQ(_)

                            def _(t):
                                t.wait_ge(sV.h, den_v[qtg])
                                if qtg >= 2:
                                    t.wait_ge(sV.h, onorm_v[qtg - 2])
                                ins = t.matmul(psDen2[:, :], onesb_row[:, :], rowQ[:, :],
                                               start=True, stop=True)
                                sT.inc(ins)
                                denb_t[qtg] = sT.n
                            TQ(_)

                            def _(v):
                                v.wait_ge(sT.h, denb_t[qtg])
                                v.tensor_copy(recipb[:, :], psDen2[:, :])
                                ins = v.tensor_tensor(o_all[:, h8, qtg * 512:(qtg + 1) * 512],
                                                      psO[:, qtg % 2, :], recipb[:, :],
                                                      ALU.mult)
                                sV.inc(ins)
                                onorm_v[qtg] = sV.n
                            VQ(_)

                    for m in range(len(tiles)):
                        emit_sc(m)
                        if m >= 2:
                            emit_pv(m - 2)
                    emit_pv(len(tiles) - 2)
                    emit_pv(len(tiles) - 1)

                    prev_exp_S = exp_s[len(tiles) - 1]
                    prev_onorm_last = onorm_v[3]
                    prev_head_free = onorm_v[3]

                op_t = {}; opr_v = {}
                for i in range(32):
                    dc, g4 = divmod(i, 4)

                    def _(t):
                        if i == 0:
                            t.wait_ge(sV.h, prev_onorm_last)
                            t.wait_ge(dG.h, wo_ok)
                        if i >= 2:
                            t.wait_ge(sV.h, opr_v[i - 2])
                        for h8 in range(8):
                            ins = t.matmul(psQK[:, i % 2, :],
                                           woAll[:, h8, dc * 128:(dc + 1) * 128],
                                           o_all[:, h8, g4 * 512:(g4 + 1) * 512],
                                           start=(h8 == 0), stop=(h8 == 7))
                        sT.inc(ins)
                        op_t[i] = sT.n
                    TQ(_)

                    def _(v):
                        v.wait_ge(sT.h, op_t[i])
                        sl = slice(tb + g4 * 512, tb + (g4 + 1) * 512)
                        ins = v.tensor_tensor(X[:, dc, sl], X[:, dc, sl],
                                              psQK[:, i % 2, :], ALU.add)
                        sV.inc(ins)
                        opr_v[i] = sV.n
                    VQ(_)
                r3_ok = sV.n

                b3r2_s = {}; b3st_t = {}; b3rows = {}; b3bc_t = {}; b3ab_v = {}
                for g4 in range(4):
                    def _(s):
                        s.wait_ge(sV.h, r3_ok)
                        if g4 > 0:
                            s.wait_ge(sT.h, b3st_t[g4 - 1])
                        for c in range(NC):
                            ins = s.activation(r2b3[:, c, :],
                                               X[:, c, tb + g4 * 512:tb + (g4 + 1) * 512],
                                               AF.Square)
                        sS.inc(ins)
                        b3r2_s[g4] = sS.n
                    SQ(_)

                    def _(t):
                        t.wait_ge(sS.h, b3r2_s[g4])
                        if g4 == 0:
                            t.wait_ge(sV.h, prev_onorm_last)
                        else:
                            t.wait_ge(sV.h, b3rows[g4 - 1])
                        for c in range(NC):
                            t.matmul(psSc[0:1, 0, :], onesb_col[:, :],
                                     X[:, c, tb + g4 * 512:tb + (g4 + 1) * 512],
                                     start=(c == 0), stop=(c == NC - 1))
                            ins = t.matmul(psSc[0:1, 1, :], onesb_col[:, :],
                                           r2b3[:, c, :],
                                           start=(c == 0), stop=(c == NC - 1))
                        sT.inc(ins)
                        b3st_t[g4] = sT.n
                    TQ(_)

                    rows_ok = ln_rowmath(512, psSc[0:1, 0, :], psSc[0:1, 1, :], b3st_t[g4])
                    b3rows[g4] = rows_ok

                    def _(t):
                        t.wait_ge(sV.h, rows_ok)
                        if g4 >= 2:
                            t.wait_ge(sV.h, b3ab_v[g4 - 2])
                        ins = t.matmul(psO[:, 0, :], onesb_row[:, :], rowA[:, :],
                                       start=True, stop=True)
                        ins = t.matmul(psO[:, 1, :], onesb_row[:, :], rowB[:, :],
                                       start=True, stop=True)
                        sT.inc(ins)
                        b3bc_t[g4] = sT.n
                    TQ(_)

                    def _(v):
                        v.wait_ge(sT.h, b3bc_t[g4])
                        v.tensor_copy(Abuf[:, g4 * 512:(g4 + 1) * 512], psO[:, 0, :])
                        ins = v.tensor_copy(Bbuf[:, g4 * 512:(g4 + 1) * 512], psO[:, 1, :])
                        sV.inc(ins)
                        b3ab_v[g4] = sV.n
                    VQ(_)

                def _(v):
                    for c in range(NC):
                        xv = X[:, c, tb:tb + T]
                        v.tensor_tensor(xv, xv, Abuf[:, 0:T], ALU.mult)
                        ins = v.tensor_tensor(xv, xv, Bbuf[:, 0:T], ALU.add)
                    sV.inc(ins)
                VQ(_)
                prev_onorm_last = sV.n
                prev_head_free = sV.n

            b3_V, b3_T, b3_S = sV.n, sT.n, sS.n
        tap("b3", b3_V, b3_T, b3_S)

        with ExitStack() as ph:
            WinT = ph.enter_context(nc.sbuf_tensor("WinT", [128, NC, L1_M], BF16))
            WoutT = ph.enter_context(nc.sbuf_tensor("WoutT", [L1_M, D], BF16))
            W1T = ph.enter_context(nc.sbuf_tensor("W1T", [L1_M, L1_INNER], F32))
            b1T = ph.enter_context(nc.sbuf_tensor("b1T", [L1_M, L1_INNER], F32))
            W2T = ph.enter_context(nc.sbuf_tensor("W2T", [L1_M, L1_INNER], F32))
            b2T = ph.enter_context(nc.sbuf_tensor("b2T", [L1_M, 1], F32))
            binT = ph.enter_context(nc.sbuf_tensor("binT", [L1_M, 1], F32))
            bcomb = ph.enter_context(nc.sbuf_tensor("bcomb", [L1_M, L1_INNER], F32))
            trelu = ph.enter_context(nc.sbuf_tensor("trelu", [L1_M, 2, 512], BF16))
            gbuf = ph.enter_context(nc.sbuf_tensor("gbuf", [L1_M, 2, 512], BF16))
            r2b4 = ph.enter_context(nc.sbuf_tensor("r2b4", [128, NC, 512], BF16))
            rOut = ph.enter_context(nc.sbuf_tensor("rOut", [128, NC, 512], F32))
            ostage = ph.enter_context(nc.sbuf_tensor("ostage", [128, 2, D], F32))
            psS4 = ph.enter_context(nc.psum_tensor("psS4", [128, 512], F32))
            psO4 = ph.enter_context(nc.psum_tensor("psO4", [128, 2, 512], F32))
            psB4 = ph.enter_context(nc.psum_tensor("psB4", [128, 2, 512], F32))

            def _(g):
                g.wait_ge(sV.h, b3_V); g.wait_ge(sT.h, b3_T); g.wait_ge(sS.h, b3_S)
                g.dma_start(out=WinT[:],
                            in_=dt["Win"].rearrange("(c p) m -> p c m", p=128)).then_inc(dG.h, 16)
                g.dma_start(out=WoutT[:], in_=dt["Wout"][:, :]).then_inc(dG.h, 16)
                dG.n += 32
            GQ(_)
            w4_okG = dG.n

            def _(y):
                y.wait_ge(sV.h, b3_V); y.wait_ge(sT.h, b3_T); y.wait_ge(sS.h, b3_S)
                for dst, nm in [(W1T, "W1"), (b1T, "b1"), (W2T, "W2"),
                                (b2T, "b2"), (binT, "bin_")]:
                    y.dma_start(out=dst[:], in_=dt[nm][:, :]).then_inc(dS.h, 16)
                dS.n += 5 * 16
            YQ(_)
            w4_okS = dS.n

            def _(v):
                v.wait_ge(dS.h, w4_okS)
                for i in range(L1_INNER):
                    ins = v.tensor_scalar(bcomb[:, i:i + 1], W1T[:, i:i + 1],
                                          binT[:, :], b1T[:, i:i + 1], ALU.mult, op1=ALU.add)
                sV.inc(ins)
            VQ(_)
            bcomb_ok = sV.n

            smm_t = {}; relu_s = {}; gacc_v = {}; omm_t = {}; ffr_v = {}
            for g in range(8):
                gb = g * 512

                def _(t):
                    if g == 0:
                        t.wait_ge(dG.h, w4_okG)
                        t.wait_ge(sV.h, b3_V)
                    else:
                        t.wait_ge(sS.h, relu_s[(g - 1, L1_INNER - 1)])
                    for k in range(NC):
                        ins = t.matmul(psS4[0:L1_M, :], WinT[:, k, :], X[:, k, gb:gb + 512],
                                       start=(k == 0), stop=(k == NC - 1))
                    sT.inc(ins)
                    smm_t[g] = sT.n
                TQ(_)

                for i in range(L1_INNER):
                    def _(s):
                        s.wait_ge(sT.h, smm_t[g])
                        if i == 0:
                            s.wait_ge(sV.h, bcomb_ok)
                        if i < 2 and g > 0:
                            s.wait_ge(sV.h, gacc_v[(g - 1, L1_INNER - 1)])
                        if i >= 2:
                            s.wait_ge(sV.h, gacc_v[(g, i - 2)])
                        ins = s.activation(trelu[:, i % 2, :], psS4[0:L1_M, :], AF.Relu,
                                           bias=bcomb[:, i:i + 1], scale=W1T[:, i:i + 1])
                        sS.inc(ins)
                        relu_s[(g, i)] = sS.n
                    SQ(_)

                    def _(v):
                        v.wait_ge(sS.h, relu_s[(g, i)])
                        if i == 0:
                            ins = v.tensor_scalar(gbuf[:, g % 2, :], trelu[:, i % 2, :],
                                                  W2T[:, i:i + 1], b2T[:, :],
                                                  ALU.mult, op1=ALU.add)
                        else:
                            ins = v.scalar_tensor_tensor(gbuf[:, g % 2, :], trelu[:, i % 2, :],
                                                         W2T[:, i:i + 1], gbuf[:, g % 2, :],
                                                         ALU.mult, ALU.add)
                        sV.inc(ins)
                        gacc_v[(g, i)] = sV.n
                    VQ(_)

                for dc in range(NC):
                    jj = g * NC + dc

                    def _(t):
                        t.wait_ge(sV.h, gacc_v[(g, L1_INNER - 1)])
                        if jj >= 2:
                            t.wait_ge(sV.h, ffr_v[jj - 2])
                        ins = t.matmul(psO4[:, jj % 2, :],
                                       WoutT[0:L1_M, dc * 128:(dc + 1) * 128],
                                       gbuf[0:L1_M, g % 2, :], start=True, stop=True)
                        sT.inc(ins)
                        omm_t[jj] = sT.n
                    TQ(_)

                    def _(v):
                        v.wait_ge(sT.h, omm_t[jj])
                        ins = v.scalar_tensor_tensor(X[:, dc, gb:gb + 512],
                                                     psO4[:, jj % 2, :], bout_sb[:, dc:dc + 1],
                                                     X[:, dc, gb:gb + 512], ALU.add, ALU.add)
                        sV.inc(ins)
                        ffr_v[jj] = sV.n
                    VQ(_)
            ffn_done_v = sV.n

            f4r2_s = {}; fst_t = {}; f4rows = {}; fbc_t = {}; fab_v = {}
            ftr_t = {}; fev_v = {}; grp_fev = {}; ostage_dma = {}
            for g in range(8):
                gb = g * 512

                def _(s):
                    s.wait_ge(sV.h, ffn_done_v)
                    if g > 0:
                        s.wait_ge(sT.h, fst_t[g - 1])
                    for c in range(NC):
                        ins = s.activation(r2b4[:, c, :], X[:, c, gb:gb + 512], AF.Square)
                    sS.inc(ins)
                    f4r2_s[g] = sS.n
                SQ(_)

                def _(t):
                    t.wait_ge(sS.h, f4r2_s[g])
                    if g == 0:
                        t.wait_ge(sV.h, ffn_done_v)
                    else:
                        t.wait_ge(sV.h, grp_fev[g - 1])
                    for c in range(NC):
                        t.matmul(psS4[0:1, :], onesb_col[:, :], X[:, c, gb:gb + 512],
                                 start=(c == 0), stop=(c == NC - 1))
                        ins = t.matmul(psO4[0:1, 0, :], onesb_col[:, :], r2b4[:, c, :],
                                       start=(c == 0), stop=(c == NC - 1))
                    sT.inc(ins)
                    fst_t[g] = sT.n
                TQ(_)

                rows_ok = ln_rowmath(512, psS4[0:1, :], psO4[0:1, 0, :], fst_t[g])
                f4rows[g] = rows_ok

                def _(t):
                    t.wait_ge(sV.h, rows_ok)
                    ins = t.matmul(psB4[:, 0, :], onesb_row[:, :], rowA[:, :],
                                   start=True, stop=True)
                    ins = t.matmul(psB4[:, 1, :], onesb_row[:, :], rowB[:, :],
                                   start=True, stop=True)
                    sT.inc(ins)
                    fbc_t[g] = sT.n
                TQ(_)

                def _(v):
                    v.wait_ge(sT.h, fbc_t[g])
                    v.tensor_copy(Abuf[:, 0:512], psB4[:, 0, :])
                    v.tensor_copy(Bbuf[:, 0:512], psB4[:, 1, :])
                    for c in range(NC):
                        v.tensor_tensor(rOut[:, c, :], X[:, c, gb:gb + 512],
                                        Abuf[:, 0:512], ALU.mult)
                        ins = v.tensor_tensor(rOut[:, c, :], rOut[:, c, :],
                                              Bbuf[:, 0:512], ALU.add)
                    sV.inc(ins)
                    fab_v[g] = sV.n
                VQ(_)

                for tt in range(4):
                    gtile = g * 4 + tt
                    jj0 = gtile * NC
                    for c in range(NC):
                        jj = jj0 + c

                        def _(t):
                            t.wait_ge(sV.h, fab_v[g])
                            if jj >= 2:
                                t.wait_ge(sV.h, fev_v[jj - 2])
                            ins = t.transpose(psO4[:, jj % 2, 0:128],
                                              rOut[:, c, tt * 128:(tt + 1) * 128],
                                              ident_f32[:])
                            sT.inc(ins)
                            ftr_t[jj] = sT.n
                        TQ(_)

                        def _(v):
                            v.wait_ge(sT.h, ftr_t[jj])
                            if c == 0 and gtile >= 2:
                                v.wait_ge(dS.h, ostage_dma[gtile - 2])
                            ins = v.tensor_copy(
                                ostage[:, gtile % 2, c * 128:(c + 1) * 128],
                                psO4[:, jj % 2, 0:128])
                            sV.inc(ins)
                            fev_v[jj] = sV.n
                        VQ(_)

                    def _(y):
                        y.wait_ge(sV.h, fev_v[jj0 + NC - 1])
                        tok0 = gb + tt * 128
                        s_ii, toff = divmod(tok0, T)
                        y.dma_start(out=out[s_ii, toff:toff + 128, :],
                                    in_=ostage[:, gtile % 2, :]).then_inc(dS.h, 16)
                        dS.n += 16
                        ostage_dma[gtile] = dS.n
                    YQ(_)
                grp_fev[g] = sV.n

            def _(y):
                y.wait_ge(dS.h, dS.n)
            YQ(_)

            def _(g):
                g.wait_ge(dG.h, dG.n)
            GQ(_)

    return nc, dt


_BUILD_CACHE = {}


def _get_nc(taps=()):
    key = tuple(taps)
    if key not in _BUILD_CACHE:
        _BUILD_CACHE[key] = build_nc(taps)
    return _BUILD_CACHE[key]


def _make_in_maps(inputs):
    consts = _host_consts()
    z = np.ascontiguousarray(np.asarray(inputs["L0_attn_Z"], np.float32))
    shared = {
        "l0_W": np.ascontiguousarray(inputs["L0_attn_W"], np.float32),
        "l0_b": _fm_vec(inputs["L0_attn_b"]),
        "l0_gamma": _fm_vec(inputs["L0_attn_gamma"]),
        "l0_beta": _fm_vec(inputs["L0_attn_beta"]),
        "decay": np.asarray(inputs["L0_attn_decay"], np.float32).reshape(1, 1),
        "gate_w": np.ascontiguousarray(inputs["L0_ffn_gate_w"], np.float32),
        "gate_b": np.asarray(inputs["L0_ffn_gate_b"], np.float32).reshape(MOE_E, 1),
        "sh_wg": np.ascontiguousarray(inputs["L0_ffn_shared_wg"], np.float32),
        "sh_wu": np.ascontiguousarray(inputs["L0_ffn_shared_wu"], np.float32),
        "sh_wd": np.ascontiguousarray(inputs["L0_ffn_shared_wd"], np.float32),
        "ex_wg": np.ascontiguousarray(inputs["L0_ffn_experts_wg"], np.float32),
        "ex_wu": np.ascontiguousarray(inputs["L0_ffn_experts_wu"], np.float32),
        "ex_wd": np.ascontiguousarray(inputs["L0_ffn_experts_wd"], np.float32),
        "wq": np.ascontiguousarray(inputs["L1_attn_wq"], np.float32),
        "wk": np.ascontiguousarray(inputs["L1_attn_wk"], np.float32),
        "wv": np.ascontiguousarray(inputs["L1_attn_wv"], np.float32),
        "wo": np.ascontiguousarray(inputs["L1_attn_wo"], np.float32),
        "Win": np.ascontiguousarray(inputs["L1_ffn_Win"], np.float32),
        "bin_": np.asarray(inputs["L1_ffn_bin_"], np.float32).reshape(L1_M, 1),
        "W1": np.ascontiguousarray(inputs["L1_ffn_W1"], np.float32),
        "b1": np.ascontiguousarray(inputs["L1_ffn_b1"], np.float32),
        "W2": np.ascontiguousarray(inputs["L1_ffn_W2"], np.float32),
        "b2": np.asarray(inputs["L1_ffn_b2"], np.float32).reshape(L1_M, 1),
        "Wout": np.ascontiguousarray(inputs["L1_ffn_Wout"], np.float32),
        "bout": _fm_vec(inputs["L1_ffn_bout"]),
    }
    shared.update(consts)
    maps = []
    for c in range(8):
        m = dict(shared)
        m["x"] = np.ascontiguousarray(z[2 * c:2 * c + 2])
        maps.append(m)
    return maps


def kernel(**inputs):
    nc, _ = _get_nc()
    in_maps = _make_in_maps(inputs)
    res = run_bass_kernel_spmd(nc, in_maps, core_ids=list(range(8)))
    return np.concatenate([r["out"] for r in res.results], axis=0)


# version 09
# speedup vs baseline: 1.0323x; 1.0323x over previous
import numpy as np
import ml_dtypes
from contextlib import ExitStack

import concourse.bass as bass
import concourse.mybir as mybir
from concourse.alu_op_type import AluOpType as ALU
from concourse.bass_utils import run_bass_kernel_spmd

F32 = mybir.dt.float32
BF16 = mybir.dt.bfloat16
AF = mybir.ActivationFunctionType
AX = mybir.AxisListType

NB = 2
T = 2048
NT = NB * T
D = 1024
NC = 8
LN_EPS = 1e-5
RMS_EPS = 1e-6
ROPE_BASE = 10000.0
MOE_I = 512
MOE_E = 16
L1_M = 64
L1_INNER = 16


def _host_consts():
    bf16 = ml_dtypes.bfloat16
    c = {}
    c["ident"] = np.eye(128, dtype=bf16)
    c["ident_f32"] = np.eye(128, dtype=np.float32)
    c["onesb_col"] = np.ones((128, 1), dtype=bf16)
    c["onesb_row"] = np.ones((1, 128), dtype=bf16)
    c["ones_col_f32"] = np.ones((1, 128), dtype=np.float32)
    c["ones_row_f32"] = np.ones((1, 512), dtype=np.float32)
    half = 64
    pos = np.arange(T, dtype=np.float64)[None, :]
    freqs = (ROPE_BASE ** (-np.arange(half, dtype=np.float64) / half))[:, None]
    rad = freqs * pos
    c["ropeC"] = np.ascontiguousarray(np.concatenate([np.cos(rad)] * 2, 0)).astype(bf16)
    c["ropeS"] = np.ascontiguousarray(
        np.concatenate([-np.sin(rad), np.sin(rad)], 0)).astype(bf16)
    m = np.zeros((4, 128, 512), dtype=bf16)
    for j in range(4):
        kk = j * 128 + np.arange(128)[:, None]
        qq = np.arange(512)[None, :]
        m[j] = (kk <= qq).astype(bf16)
    c["masks"] = m
    return c


def _fm_vec(v):
    return np.ascontiguousarray(np.asarray(v, np.float32).reshape(NC, 128).T)


def _packpair(w):
    w = np.asarray(w, np.float32)
    out = np.empty((128, 8), np.float32)
    for i in range(8):
        out[0:64, i] = w[:, 2 * i]
        out[64:128, i] = w[:, 2 * i + 1]
    return np.ascontiguousarray(out)


def _packpair(w):
    w = np.asarray(w, np.float32)
    out = np.empty((128, 8), np.float32)
    for i in range(8):
        out[0:64, i] = w[:, 2 * i]
        out[64:128, i] = w[:, 2 * i + 1]
    return np.ascontiguousarray(out)


class Sem:
    def __init__(self, stack, nc, name):
        self.h = stack.enter_context(nc.semaphore(name))
        self.n = 0

    def inc(self, instr, k=1):
        instr.then_inc(self.h, k)
        self.n += k


def build_nc(taps=()):
    nc = bass.Bass("TRN2", target_bir_lowering=False)
    dt = {}

    def din(name, shape, dtype=F32):
        dt[name] = nc.dram_tensor(name, list(shape), dtype, kind="ExternalInput").ap()
        return dt[name]

    x_in = din("x", [NB, T, D])
    out = nc.dram_tensor("out", [NB, T, D], F32, kind="ExternalOutput").ap()
    tap_out = {}
    for tn in taps:
        tap_out[tn] = nc.dram_tensor("tap_" + tn, [128, NC, NT], F32,
                                     kind="ExternalOutput").ap()

    din("l0_W", [D, D]); din("l0_b", [128, NC]); din("l0_gamma", [128, NC])
    din("l0_beta", [128, NC]); din("decay", [1, 1])
    din("gate_w", [D, MOE_E]); din("gate_b", [MOE_E, 1])
    din("sh_wg", [D, MOE_I]); din("sh_wu", [D, MOE_I]); din("sh_wd", [MOE_I, D])
    din("ex_wg", [MOE_E, D, MOE_I]); din("ex_wu", [MOE_E, D, MOE_I])
    din("ex_wd", [MOE_E, MOE_I, D])
    din("wq", [D, 8, 128]); din("wk", [D, 8, 128]); din("wv", [D, 8, 128])
    din("wo", [8, 128, D])
    din("Win2", [D, 128]); din("binP", [128, 1])
    din("W1P", [128, 8]); din("b1P", [128, 8])
    din("W2P", [128, 8]); din("b2", [L1_M, 1])
    din("Wout", [L1_M, D]); din("bout", [128, NC])
    din("ident", [128, 128], BF16); din("ident_f32", [128, 128])
    din("onesb_col", [128, 1], BF16)
    din("onesb_row", [1, 128], BF16); din("ones_col_f32", [1, 128])
    din("ones_row_f32", [1, 512])
    din("ropeC", [128, T], BF16); din("ropeS", [128, T], BF16)
    din("masks", [4, 128, 512], BF16)

    stack = ExitStack()
    with stack:
        dS = Sem(stack, nc, "dS")
        dG = Sem(stack, nc, "dG")
        sT = Sem(stack, nc, "sT")
        sV = Sem(stack, nc, "sV")
        sS = Sem(stack, nc, "sS")

        def sb(name, shape, dtype):
            return stack.enter_context(nc.sbuf_tensor(name, list(shape), dtype))

        X = sb("X", [128, NC, NT], BF16)
        ident = sb("c_ident", [128, 128], BF16)
        ident_f32 = sb("c_identf", [128, 128], F32)
        onesb_col = sb("c_obc", [128, 1], BF16)
        onesb_row = sb("c_obr", [1, 128], BF16)
        ones_col_f32 = sb("c_ocf", [1, 128], F32)
        ones_row_f32 = sb("c_orf", [1, 512], F32)
        l0_b = sb("c_l0b", [128, NC], F32)
        l0_gamma = sb("c_l0g", [128, NC], F32)
        l0_beta = sb("c_l0be", [128, NC], F32)
        bout_sb = sb("c_bout", [128, NC], F32)
        decay_sb = sb("c_decay", [1, 1], F32)
        rowA = sb("rowA", [1, 512], BF16)
        rowB = sb("rowB", [1, 512], BF16)
        Abuf = sb("Abuf", [128, T], BF16)
        Bbuf = sb("Bbuf", [128, T], BF16)
        statf = sb("statf", [1, 3, 512], F32)

        block = stack.enter_context(nc.Block())
        TQ, VQ, SQ, GQ, YQ = block.tensor, block.vector, block.scalar, block.gpsimd, block.sync

        def _(y):
            for sbuf_t, nm in [(ident, "ident"), (ident_f32, "ident_f32"),
                               (onesb_col, "onesb_col"),
                               (onesb_row, "onesb_row"), (ones_col_f32, "ones_col_f32"),
                               (ones_row_f32, "ones_row_f32"),
                               (l0_b, "l0_b"), (l0_gamma, "l0_gamma"),
                               (l0_beta, "l0_beta"), (bout_sb, "bout"),
                               (decay_sb, "decay")]:
                y.dma_start(out=sbuf_t[:], in_=dt[nm][:, :]).then_inc(dS.h, 16)
                dS.n += 16
        YQ(_)
        consts_ready = dS.n

        def ln_rowmath(n, psA_row, psB_row, t_mark):
            def _(v):
                v.wait_ge(sT.h, t_mark)
                mu = statf[:, 0, 0:n]; w = statf[:, 1, 0:n]; t2 = statf[:, 2, 0:n]
                v.tensor_scalar(mu, psA_row, 1.0 / D, None, ALU.mult)
                v.tensor_scalar(w, psB_row, 1.0 / D, None, ALU.mult)
                v.tensor_tensor(t2, mu, mu, ALU.mult)
                v.tensor_tensor(w, w, t2, ALU.subtract)
                v.tensor_scalar(w, w, 1.0, LN_EPS, ALU.mult, op1=ALU.add)
                ins = v.reciprocal(w, w)
                sV.inc(ins)
            VQ(_)
            m1 = sV.n

            def _(s):
                s.wait_ge(sV.h, m1)
                ins = s.activation(rowA[:, 0:n], statf[:, 1, 0:n], AF.Sqrt)
                sS.inc(ins)
            SQ(_)
            m2 = sS.n

            def _(v):
                v.wait_ge(sS.h, m2)
                v.tensor_tensor(statf[:, 2, 0:n], statf[:, 0, 0:n], rowA[:, 0:n], ALU.mult)
                ins = v.tensor_scalar(rowB[:, 0:n], statf[:, 2, 0:n], -1.0, None, ALU.mult)
                sV.inc(ins)
            VQ(_)
            return sV.n

        with ExitStack() as ph:
            psT = [ph.enter_context(nc.psum_tensor("psTa", [128, 1024], BF16)),
                   ph.enter_context(nc.psum_tensor("psTb", [128, 1024], BF16))]
            xstage = ph.enter_context(nc.sbuf_tensor("xstage", [128, 2, D], BF16))

            ntile = NB * (T // 128)
            ld_m = [0] * ntile; tr_m = [0] * ntile; ev_m = [0] * ntile
            for i in range(ntile):
                s_i, tt = divmod(i, T // 128)

                def _(g):
                    if i >= 2:
                        g.wait_ge(sT.h, tr_m[i - 2])
                    g.dma_start(out=xstage[:, i % 2, :],
                                in_=x_in[s_i, tt * 128:(tt + 1) * 128, :]).then_inc(dG.h, 16)
                    dG.n += 16
                    ld_m[i] = dG.n
                GQ(_)

                def _(t):
                    t.wait_ge(dG.h, ld_m[i])
                    if i == 0:
                        t.wait_ge(dS.h, consts_ready)
                    if i >= 2:
                        t.wait_ge(sV.h, ev_m[i - 2])
                    for c in range(NC):
                        ins = t.transpose(psT[i % 2][:, c * 128:(c + 1) * 128],
                                          xstage[:, i % 2, c * 128:(c + 1) * 128], ident[:])
                    sT.inc(ins)
                    tr_m[i] = sT.n
                TQ(_)

                def _(v):
                    v.wait_ge(sT.h, tr_m[i])
                    t0 = s_i * T + tt * 128
                    for c in range(NC):
                        ins = v.tensor_copy(X[:, c, t0:t0 + 128],
                                            psT[i % 2][:, c * 128:(c + 1) * 128])
                    sV.inc(ins)
                    ev_m[i] = sV.n
                VQ(_)
            in_T, in_V = sT.n, sV.n

        def tap(name, tV, tT, tS):
            if name not in tap_out:
                return
            def _(g):
                g.wait_ge(sV.h, tV); g.wait_ge(sT.h, tT); g.wait_ge(sS.h, tS)
                g.dma_start(out=tap_out[name][:, :, :], in_=X[:]).then_inc(dG.h, 16)
                dG.n += 16
            GQ(_)

        tap("xin", in_V, in_T, sS.n)

        with ExitStack() as ph:
            Wrow = ph.enter_context(nc.sbuf_tensor("b1Wrow", [128, NC, D], BF16))
            Wt = ph.enter_context(nc.sbuf_tensor("b1Wt", [128, NC, D], BF16))
            Sbuf = ph.enter_context(nc.sbuf_tensor("b1S", [128, NC, T], BF16))
            Ybuf = ph.enter_context(nc.sbuf_tensor("b1Y", [128, NC, T], BF16))
            a_tile = ph.enter_context(nc.sbuf_tensor("b1a", [128, T], F32))
            a_row = ph.enter_context(nc.sbuf_tensor("b1ar", [1, 512], F32))
            a_val = ph.enter_context(nc.sbuf_tensor("b1av", [1, 1], F32))
            y2t = ph.enter_context(nc.sbuf_tensor("b1y2", [128, 2, 512], BF16))
            psY = [ph.enter_context(nc.psum_tensor("b1psY0", [128, 512], F32)),
                   ph.enter_context(nc.psum_tensor("b1psY1", [128, 512], F32)),
                   None, None]
            psSA = ph.enter_context(nc.psum_tensor("b1psSA", [128, 512], F32))
            psSB = ph.enter_context(nc.psum_tensor("b1psSB", [128, 512], F32))
            psBa = ph.enter_context(nc.psum_tensor("b1psBa", [128, 512], F32))
            psBb = ph.enter_context(nc.psum_tensor("b1psBb", [128, 512], F32))
            psT = [ph.enter_context(nc.psum_tensor("b1psT0", [128, 1024], BF16)),
                   ph.enter_context(nc.psum_tensor("b1psT1", [128, 1024], BF16))]
            psY[2] = psBa
            psY[3] = psBb

            def _(g):
                g.wait_ge(sT.h, in_T)
                g.wait_ge(sV.h, in_V)
                g.dma_start(out=Wrow[:],
                            in_=dt["l0_W"].rearrange("(c p) d -> p c d", p=128)).then_inc(dG.h, 16)
                dG.n += 16
            GQ(_)
            wrow_ok = dG.n

            def _(s):
                s.wait_ge(dS.h, consts_ready)
                ins = s.activation(a_val[:], decay_sb[:], AF.Sigmoid)
                sS.inc(ins)
            SQ(_)
            sig_ok = sS.n

            def _(v):
                v.wait_ge(sS.h, sig_ok)
                ins = v.tensor_scalar(a_row[:], ones_row_f32[:], a_val[:], None, ALU.mult)
                sV.inc(ins)
            VQ(_)
            arow_okV = sV.n

            at_t = [0] * 4; at_v = [0] * 4
            for j in range(4):
                def _(t):
                    if j == 0:
                        t.wait_ge(sV.h, max(arow_okV, in_V))
                    if j >= 2:
                        t.wait_ge(sV.h, at_v[j - 2])
                    bank = psBa if j % 2 == 0 else psBb
                    ins = t.matmul(bank[:, :], ones_col_f32[:, :], a_row[:, :],
                                   start=True, stop=True)
                    sT.inc(ins)
                    at_t[j] = sT.n
                TQ(_)

                def _(v):
                    v.wait_ge(sT.h, at_t[j])
                    bank = psBa if j % 2 == 0 else psBb
                    ins = v.tensor_copy(a_tile[:, j * 512:(j + 1) * 512], bank[:, :])
                    sV.inc(ins)
                    at_v[j] = sV.n
                VQ(_)
            atile_ok = sV.n

            wt_t = [0] * 64; wt_v = [0] * 64
            for i in range(64):
                ce, cd = divmod(i, NC)

                def _(t):
                    if i == 0:
                        t.wait_ge(dG.h, wrow_ok)
                        t.wait_ge(sV.h, in_V)
                    if i >= 2:
                        t.wait_ge(sV.h, wt_v[i - 2])
                    ins = t.transpose(psT[i % 2][:, 0:128],
                                      Wrow[:, ce, cd * 128:(cd + 1) * 128], ident[:])
                    sT.inc(ins)
                    wt_t[i] = sT.n
                TQ(_)

                def _(v):
                    v.wait_ge(sT.h, wt_t[i])
                    ins = v.tensor_copy(Wt[:, cd, ce * 128:(ce + 1) * 128],
                                        psT[i % 2][:, 0:128])
                    sV.inc(ins)
                    wt_v[i] = sV.n
                VQ(_)
            wt_ok = sV.n

            prev_y_lastS = 0
            prev_y_lastV = 0
            prev_r2_T = 0
            for s_i in range(NB):
                tb = s_i * T

                def _(v):
                    if s_i > 0:
                        v.wait_ge(sT.h, prev_r2_T)
                    for c in range(NC):
                        ins = v.tensor_tensor_scan(Sbuf[:, c, :], a_tile[:],
                                                   X[:, c, tb:tb + T], 0.0,
                                                   ALU.mult, ALU.add)
                    sV.inc(ins)
                VQ(_)
                scan_ok = sV.n

                ymm = [0] * 32; yev = [0] * 32; yevV = [0] * 32; yst_t = [0] * 32
                ystat_v = [0] * 4; yab_v = [0] * 4; ybc_t = [0] * 4

                def emit_ymm(i):
                    g, eg = divmod(i, NC)
                    def _(t):
                        if i == 0:
                            t.wait_ge(sV.h, max(scan_ok, wt_ok))
                            if s_i > 0:
                                t.wait_ge(sS.h, prev_y_lastS)
                                t.wait_ge(sV.h, prev_y_lastV)
                        if i >= 4:
                            t.wait_ge(sS.h, yev[i - 4])
                            t.wait_ge(sV.h, yevV[i - 4])
                        if i == 0 and s_i == 0:
                            t.wait_ge(sV.h, atile_ok)
                        for k in range(NC):
                            ins = t.matmul(psY[i % 4][:, :], Wt[:, k, eg * 128:(eg + 1) * 128],
                                           Sbuf[:, k, g * 512:(g + 1) * 512],
                                           start=(k == 0), stop=(k == NC - 1))
                        sT.inc(ins)
                        ymm[i] = sT.n
                    TQ(_)

                def emit_yev(i):
                    g, eg = divmod(i, NC)
                    def _(s):
                        s.wait_ge(sT.h, ymm[i])
                        if i >= 2:
                            s.wait_ge(sT.h, yst_t[i - 2])
                        ins = s.activation(y2t[:, i % 2, :], psY[i % 4][:, :],
                                           AF.Square, bias=l0_b[:, eg:eg + 1])
                        sS.inc(ins)
                        yev[i] = sS.n
                    SQ(_)

                    def _(v):
                        v.wait_ge(sS.h, yev[i])
                        if i >= 2:
                            v.wait_ge(sT.h, yst_t[i - 2])
                        ins = v.tensor_scalar(Ybuf[:, eg, g * 512:(g + 1) * 512],
                                              psY[i % 4][:, :], l0_b[:, eg:eg + 1],
                                              None, ALU.add)
                        sV.inc(ins)
                        yevV[i] = sV.n
                    VQ(_)

                def emit_yst(i):
                    g, eg = divmod(i, NC)
                    def _(t):
                        t.wait_ge(sS.h, yev[i])
                        t.wait_ge(sV.h, yevV[i])
                        if eg == 0 and g > 0:
                            t.wait_ge(sV.h, yab_v[g - 1])
                        ins = t.matmul(psSA[0:1, :], onesb_col[:, :],
                                       Ybuf[:, eg, g * 512:(g + 1) * 512],
                                       start=(eg == 0), stop=(eg == NC - 1))
                        ins = t.matmul(psSB[0:1, :], onesb_col[:, :], y2t[:, i % 2, :],
                                       start=(eg == 0), stop=(eg == NC - 1))
                        sT.inc(ins)
                        yst_t[i] = sT.n
                    TQ(_)

                def emit_ygrp(g):
                    rows_ok = ln_rowmath(512, psSA[0:1, :], psSB[0:1, :], yst_t[g * 8 + 7])
                    ystat_v[g] = rows_ok

                    def _(t):
                        t.wait_ge(sV.h, rows_ok)
                        ins = t.matmul(psSA[:, :], onesb_row[:, :], rowA[:, :],
                                       start=True, stop=True)
                        ins = t.matmul(psSB[:, :], onesb_row[:, :], rowB[:, :],
                                       start=True, stop=True)
                        sT.inc(ins)
                        ybc_t[g] = sT.n
                    TQ(_)

                    def _(v):
                        v.wait_ge(sT.h, ybc_t[g])
                        v.tensor_copy(Abuf[:, g * 512:(g + 1) * 512], psSA[:, :])
                        ins = v.tensor_copy(Bbuf[:, g * 512:(g + 1) * 512], psSB[:, :])
                        sV.inc(ins)
                        yab_v[g] = sV.n
                    VQ(_)

                for i in range(32):
                    emit_ymm(i)
                    emit_yev(i)
                    if i >= 1:
                        emit_yst(i - 1)
                        if (i - 1) % 8 == 7:
                            emit_ygrp((i - 1) // 8)
                emit_yst(31)
                emit_ygrp(3)
                prev_y_lastS = yev[31]
                prev_y_lastV = yevV[31]

                def _(v):
                    for eg in range(NC):
                        yv = Ybuf[:, eg, :]
                        v.tensor_tensor(yv, yv, Abuf[:, 0:T], ALU.mult)
                        v.tensor_tensor(yv, yv, Bbuf[:, 0:T], ALU.add)
                        v.tensor_scalar(yv, yv, l0_gamma[:, eg:eg + 1],
                                        l0_beta[:, eg:eg + 1], ALU.mult, op1=ALU.add)
                        ins = v.tensor_tensor(X[:, eg, tb:tb + T], X[:, eg, tb:tb + T],
                                              yv, ALU.add)
                    sV.inc(ins)
                VQ(_)
                r_ok = sV.n

                def _(s):
                    s.wait_ge(sV.h, r_ok)
                    for c in range(NC):
                        ins = s.activation(Sbuf[:, c, :], X[:, c, tb:tb + T], AF.Square)
                    sS.inc(ins)
                SQ(_)
                r2_ok = sS.n

                ost_t = [0] * 4; ostat_v = [0] * 4; obc_t = [0] * 4; oab_v = [0] * 4
                for g in range(4):
                    def _(t):
                        t.wait_ge(sS.h, r2_ok)
                        if g > 0:
                            t.wait_ge(sV.h, ostat_v[g - 1])
                        else:
                            t.wait_ge(sV.h, yab_v[3])
                        for c in range(NC):
                            ins = t.matmul(psSA[0:1, :], onesb_col[:, :],
                                           X[:, c, tb + g * 512:tb + (g + 1) * 512],
                                           start=(c == 0), stop=(c == NC - 1))
                            ins = t.matmul(psSB[0:1, :], onesb_col[:, :],
                                           Sbuf[:, c, g * 512:(g + 1) * 512],
                                           start=(c == 0), stop=(c == NC - 1))
                        sT.inc(ins)
                        ost_t[g] = sT.n
                    TQ(_)
                    rows_ok = ln_rowmath(512, psSA[0:1, :], psSB[0:1, :], ost_t[g])
                    ostat_v[g] = rows_ok

                    def _(t):
                        t.wait_ge(sV.h, rows_ok)
                        if g >= 2:
                            t.wait_ge(sV.h, oab_v[g - 2])
                        ins = t.matmul(psBa[:, :], onesb_row[:, :], rowA[:, :],
                                       start=True, stop=True)
                        ins = t.matmul(psBb[:, :], onesb_row[:, :], rowB[:, :],
                                       start=True, stop=True)
                        sT.inc(ins)
                        obc_t[g] = sT.n
                    TQ(_)

                    def _(v):
                        v.wait_ge(sT.h, obc_t[g])
                        v.tensor_copy(Abuf[:, g * 512:(g + 1) * 512], psBa[:, :])
                        ins = v.tensor_copy(Bbuf[:, g * 512:(g + 1) * 512], psBb[:, :])
                        sV.inc(ins)
                        oab_v[g] = sV.n
                    VQ(_)

                def _(v):
                    for c in range(NC):
                        xv = X[:, c, tb:tb + T]
                        v.tensor_tensor(xv, xv, Abuf[:, 0:T], ALU.mult)
                        ins = v.tensor_tensor(xv, xv, Bbuf[:, 0:T], ALU.add)
                    sV.inc(ins)
                VQ(_)
                prev_r2_T = ost_t[3]

            b1_V, b1_T, b1_S = sV.n, sT.n, sS.n
        tap("b1", b1_V, b1_T, b1_S)

        with ExitStack() as ph:
            wgT = ph.enter_context(nc.sbuf_tensor("wgT", [128, 2, NC, MOE_I], BF16))
            wuT = ph.enter_context(nc.sbuf_tensor("wuT", [128, 2, NC, MOE_I], BF16))
            wdT = ph.enter_context(nc.sbuf_tensor("wdT", [128, 2, 4, D], BF16))
            gateT = ph.enter_context(nc.sbuf_tensor("gateT", [128, NC, MOE_E], BF16))
            gatebT = ph.enter_context(nc.sbuf_tensor("gatebT", [MOE_E, 1], F32))
            Rsb = ph.enter_context(nc.sbuf_tensor("Rsb", [MOE_E, T], F32))
            Rt = ph.enter_context(nc.sbuf_tensor("Rt", [128, 16, MOE_E], F32))
            ctok = ph.enter_context(nc.sbuf_tensor("ctok", [128, 2, 128], F32))
            cF = ph.enter_context(nc.sbuf_tensor("cF", [MOE_E, T], BF16))
            cRow = ph.enter_context(nc.sbuf_tensor("cRow", [1, 2, T], BF16))
            cb = ph.enter_context(nc.sbuf_tensor("cb", [128, T], BF16))
            accb = ph.enter_context(nc.sbuf_tensor("accb", [128, NC, T], BF16))
            slb = ph.enter_context(nc.sbuf_tensor("slb", [128, 2, 4, 256], BF16))
            hpb = ph.enter_context(nc.sbuf_tensor("hpb", [128, 2, 4, 256], BF16))
            tv1 = ph.enter_context(nc.sbuf_tensor("tv1", [128, 4], F32))
            tv2 = ph.enter_context(nc.sbuf_tensor("tv2", [128, MOE_E], F32))
            tv3 = ph.enter_context(nc.sbuf_tensor("tv3", [128, MOE_E], F32))
            psHG = ph.enter_context(nc.psum_tensor("psHG", [128, 4, 256], F32))
            psHU = ph.enter_context(nc.psum_tensor("psHU", [128, 4, 256], F32))
            psACC = ph.enter_context(nc.psum_tensor("psACC", [128, NC, 256], F32))

            def _(g):
                g.wait_ge(sV.h, b1_V); g.wait_ge(sT.h, b1_T); g.wait_ge(sS.h, b1_S)
                g.dma_start(out=gateT[:],
                            in_=dt["gate_w"].rearrange("(c p) e -> p c e", p=128)).then_inc(dG.h, 16)
                dG.n += 16
            GQ(_)
            gate_okG = dG.n

            def _(y):
                y.wait_ge(sV.h, b1_V); y.wait_ge(sT.h, b1_T); y.wait_ge(sS.h, b1_S)
                y.dma_start(out=gatebT[:], in_=dt["gate_b"][:, :]).then_inc(dS.h, 16)
                dS.n += 16
            YQ(_)
            gate_okS = dS.n

            prev_psacc_free_V = 0
            prev_unit16_T = 0
            prev_epB2_V = 0
            for h_i in range(NB):
                hb = h_i * T
                wload = {}
                hmm = {}; silu_m = {}; h_m = {}; dmm = {}; accev = {}
                crow_dma = {}; crow_t = {}; cb_ok = {}
                ep_r = {}; ep_r2 = {}; ep_st = {}; ep_bc = {}; ep_ab = {}

                def load_unit(u):
                    def _(g):
                        slot = u % 2
                        if u >= 2:
                            g.wait_ge(sT.h, dmm[(u - 2, 7)])
                        elif h_i > 0:
                            g.wait_ge(sT.h, prev_unit16_T)
                        else:
                            g.wait_ge(sV.h, b1_V); g.wait_ge(sT.h, b1_T)
                            g.wait_ge(sS.h, b1_S)
                        if u == 0:
                            srcs = [dt["sh_wg"], dt["sh_wu"], dt["sh_wd"]]
                        else:
                            e = u - 1
                            srcs = [dt["ex_wg"][e], dt["ex_wu"][e], dt["ex_wd"][e]]
                        g.dma_start(out=wgT[:, slot, :, :],
                                    in_=srcs[0].rearrange("(c p) i -> p c i", p=128)).then_inc(dG.h, 16)
                        g.dma_start(out=wuT[:, slot, :, :],
                                    in_=srcs[1].rearrange("(c p) i -> p c i", p=128)).then_inc(dG.h, 16)
                        g.dma_start(out=wdT[:, slot, :, :],
                                    in_=srcs[2].rearrange("(c p) i -> p c i", p=128)).then_inc(dG.h, 16)
                        dG.n += 48
                        wload[u] = dG.n
                    GQ(_)

                load_unit(0)
                load_unit(1)

                rt_t = [0] * 4; rsig = [0] * 4
                for g4 in range(4):
                    def _(t):
                        if g4 == 0:
                            t.wait_ge(dG.h, gate_okG)
                            t.wait_ge(sV.h, max(b1_V, prev_psacc_free_V))
                        else:
                            t.wait_ge(sS.h, rsig[g4 - 1])
                        for k in range(NC):
                            ins = t.matmul(psACC[0:16, 0:2, :], gateT[:, k, :],
                                           X[:, k, hb + g4 * 512:hb + (g4 + 1) * 512],
                                           start=(k == 0), stop=(k == NC - 1))
                        sT.inc(ins)
                        rt_t[g4] = sT.n
                    TQ(_)

                    def _(s):
                        s.wait_ge(sT.h, rt_t[g4])
                        if g4 == 0:
                            s.wait_ge(dS.h, gate_okS)
                        ins = s.activation(Rsb[:, g4 * 512:(g4 + 1) * 512],
                                           psACC[0:16, 0:2, :], AF.Sigmoid,
                                           bias=gatebT[:, :])
                        sS.inc(ins)
                        rsig[g4] = sS.n
                    SQ(_)

                rtr_t = [0] * 16; rte_v = [0] * 16
                for i in range(16):
                    def _(t):
                        if i == 0:
                            t.wait_ge(sS.h, rsig[3])
                            t.wait_ge(sV.h, prev_epB2_V)
                        if i >= 2:
                            t.wait_ge(sV.h, rte_v[i - 2])
                        reg = psHG[:, 0, 0:16] if i % 2 == 0 else psHG[:, 2, 0:16]
                        ins = t.transpose(reg, Rsb[:, i * 128:(i + 1) * 128], ident_f32[0:16, 0:16])
                        sT.inc(ins)
                        rtr_t[i] = sT.n
                    TQ(_)

                    def _(v):
                        v.wait_ge(sT.h, rtr_t[i])
                        reg = psHG[:, 0, 0:16] if i % 2 == 0 else psHG[:, 2, 0:16]
                        ins = v.tensor_copy(Rt[:, i, :], reg)
                        sV.inc(ins)
                        rte_v[i] = sV.n
                    VQ(_)

                top_v = [0] * 16; ctr_t = [0] * 16; cte_v = [0] * 16
                def _(v):
                    v.memset(ctok[:, 0, :], 0.0)
                    ins = v.memset(ctok[:, 1, :], 0.0)
                    sV.inc(ins)
                VQ(_)
                for i in range(16):
                    def _(v):
                        v.wait_ge(sV.h, rte_v[i])
                        if i >= 2:
                            v.wait_ge(sT.h, ctr_t[i - 2])
                        r = Rt[:, i, :]
                        m1 = tv1[:, 0:1]; m2 = tv1[:, 1:2]
                        den = tv1[:, 2:3]; rec = tv1[:, 3:4]
                        def bump(ins):
                            sV.inc(ins); v.wait_ge(sV.h, sV.n)
                        bump(v.reduce_max(m1, r, AX.X))
                        bump(v.tensor_scalar(tv2[:, :], r, m1, None, ALU.is_ge))
                        bump(v.scalar_tensor_tensor(tv3[:, :], tv2[:, :], -1e30, r,
                                                    ALU.mult, ALU.add))
                        bump(v.reduce_max(m2, tv3[:, :], AX.X))
                        bump(v.tensor_tensor(den, m1, m2, ALU.add))
                        bump(v.reciprocal(rec, den))
                        bump(v.tensor_scalar(tv2[:, :], r, m2, None, ALU.is_ge))
                        ins = v.scalar_tensor_tensor(ctok[:, i % 2, 0:16], tv2[:, :], rec,
                                                     r, ALU.mult, ALU.mult)
                        sV.inc(ins)
                        top_v[i] = sV.n
                    VQ(_)

                    def _(t):
                        t.wait_ge(sV.h, top_v[i])
                        if i >= 2:
                            t.wait_ge(sV.h, cte_v[i - 2])
                        reg = psHG[:, 0, 0:128] if i % 2 == 0 else psHG[:, 2, 0:128]
                        ins = t.transpose(reg, ctok[:, i % 2, :], ident_f32[:])
                        sT.inc(ins)
                        ctr_t[i] = sT.n
                    TQ(_)

                    def _(v):
                        v.wait_ge(sT.h, ctr_t[i])
                        reg = psHG[0:16, 0, 0:128] if i % 2 == 0 else psHG[0:16, 2, 0:128]
                        ins = v.tensor_copy(cF[:, i * 128:(i + 1) * 128], reg)
                        sV.inc(ins)
                        cte_v[i] = sV.n
                    VQ(_)
                cf_ok = sV.n

                def crow_dma_issue(uu):
                    def _(y):
                        if uu == 1:
                            y.wait_ge(sV.h, cf_ok)
                        if uu >= 3:
                            y.wait_ge(sT.h, crow_t[uu - 2])
                        y.dma_start(out=cRow[0:1, uu % 2, :],
                                    in_=cF[uu - 1:uu, :]).then_inc(dS.h, 16)
                        dS.n += 16
                        crow_dma[uu] = dS.n
                    YQ(_)

                crow_dma_issue(1)
                for u in range(17):
                    slot = u % 2
                    if u + 1 <= 16:
                        load_unit(u + 1)
                    if 2 <= u + 1 <= 16:
                        crow_dma_issue(u + 1)

                    if u >= 1:
                        e = u - 1
                        cbm = {}

                        def _(t):
                            t.wait_ge(dS.h, crow_dma[u])
                            if (u - 1, 7) in h_m:
                                t.wait_ge(sV.h, h_m[(u - 1, 7)])
                            cr = cRow[0:1, u % 2, :]
                            t.matmul(psHG[:, 0:2, :], onesb_row[:, :],
                                     cr[:, 0:512], start=True, stop=True)
                            t.matmul(psHG[:, 2:4, :], onesb_row[:, :],
                                     cr[:, 512:1024], start=True, stop=True)
                            t.matmul(psHU[:, 0:2, :], onesb_row[:, :],
                                     cr[:, 1024:1536], start=True, stop=True)
                            ins = t.matmul(psHU[:, 2:4, :], onesb_row[:, :],
                                           cr[:, 1536:2048], start=True, stop=True)
                            sT.inc(ins)
                            cbm["t1"] = sT.n
                            crow_t[u] = sT.n
                        TQ(_)

                        def _(v):
                            v.wait_ge(sT.h, cbm["t1"])
                            v.tensor_copy(cb[:, 0:1024],
                                          psHG[:, :, :].rearrange("p a b -> p (a b)"))
                            ins = v.tensor_copy(cb[:, 1024:2048],
                                                psHU[:, :, :].rearrange("p a b -> p (a b)"))
                            sV.inc(ins)
                            cbm["v1"] = sV.n
                        VQ(_)
                        cb_ok[u] = cbm["v1"]

                    def emit_down(u, g):
                        def _(t):
                            t.wait_ge(sV.h, h_m[(u, g)])
                            key = (u, g - 1) if g > 0 else (u - 1, 7)
                            if key in accev:
                                t.wait_ge(sV.h, accev[key])
                            if u == 0 and g == 0:
                                t.wait_ge(sS.h, rsig[3])
                            if u > 0:
                                for b in range(4):
                                    t.matmul(psACC[:, 2 * b:2 * b + 2, :], ident[:],
                                             accb[:, 2 * b:2 * b + 2, g * 256:(g + 1) * 256],
                                             start=True, stop=False)
                            for dc in range(NC):
                                for ic in range(4):
                                    ins = t.matmul(psACC[:, dc, :],
                                                   wdT[:, u % 2, ic, dc * 128:(dc + 1) * 128],
                                                   hpb[:, g % 2, ic, :],
                                                   start=(u == 0 and ic == 0 and dc % 2 == 0),
                                                   stop=(ic == 3 and dc % 2 == 1))
                            sT.inc(ins)
                            dmm[(u, g)] = sT.n
                        TQ(_)

                        def _(v):
                            v.wait_ge(sT.h, dmm[(u, g)])
                            ins = v.tensor_copy(accb[:, :, g * 256:(g + 1) * 256],
                                                psACC[:, :, :])
                            sV.inc(ins)
                            accev[(u, g)] = sV.n
                        VQ(_)

                    for g in range(8):
                        gb = hb + g * 256

                        def _(t):
                            if g == 0:
                                t.wait_ge(dG.h, wload[u])
                                if u == 0:
                                    t.wait_ge(sV.h, cf_ok)
                                else:
                                    t.wait_ge(sV.h, cb_ok[u])
                            key = (u, g - 1) if g > 0 else (u - 1, 7)
                            if key in h_m:
                                t.wait_ge(sV.h, h_m[key])
                            for ic in range(4):
                                for k in range(NC):
                                    t.matmul(psHG[:, ic, :],
                                             wgT[:, slot, k, ic * 128:(ic + 1) * 128],
                                             X[:, k, gb:gb + 256],
                                             start=(k == 0), stop=(k == NC - 1))
                            for ic in range(4):
                                for k in range(NC):
                                    ins = t.matmul(psHU[:, ic, :],
                                                   wuT[:, slot, k, ic * 128:(ic + 1) * 128],
                                                   X[:, k, gb:gb + 256],
                                                   start=(k == 0), stop=(k == NC - 1))
                            sT.inc(ins)
                            hmm[(u, g)] = sT.n
                        TQ(_)

                        def _(s):
                            s.wait_ge(sT.h, hmm[(u, g)])
                            ins = s.activation(slb[:, g % 2, :, :], psHG[:, :, :], AF.Silu)
                            sS.inc(ins)
                            silu_m[(u, g)] = sS.n
                        SQ(_)

                        def _(v):
                            v.wait_ge(sS.h, silu_m[(u, g)])
                            hp = hpb[:, g % 2, :, :]
                            if u >= 1:
                                v.tensor_tensor(hp, psHU[:, :, :],
                                                cb[:, g * 256:g * 256 + 256][:, None, :]
                                                .to_broadcast((128, 4, 256)), ALU.mult)
                                ins = v.tensor_tensor(hp, slb[:, g % 2, :, :], hp, ALU.mult)
                            else:
                                ins = v.tensor_tensor(hp, slb[:, g % 2, :, :],
                                                      psHU[:, :, :], ALU.mult)
                            sV.inc(ins)
                            h_m[(u, g)] = sV.n
                        VQ(_)

                        if g >= 1:
                            emit_down(u, g - 1)
                    emit_down(u, 7)

                slbF = slb[:, :, :, :].rearrange("p a b c -> p (a b c)")
                hpbF = hpb[:, :, :, :].rearrange("p a b c -> p (a b c)")
                def r2reg(dc):
                    return (slbF[:, dc * 512:(dc + 1) * 512] if dc < 4
                            else hpbF[:, (dc - 4) * 512:(dc - 3) * 512])
                for gg in range(4):
                    gb = hb + gg * 512

                    def _(v):
                        v.wait_ge(sV.h, accev[(16, 2 * gg + 1)])
                        for dc in range(NC):
                            ins = v.scalar_tensor_tensor(
                                X[:, dc, gb:gb + 512], X[:, dc, gb:gb + 512],
                                2.0, accb[:, dc, gg * 512:(gg + 1) * 512],
                                ALU.mult, ALU.add)
                        sV.inc(ins)
                        ep_r[gg] = sV.n
                    VQ(_)

                    def _(s):
                        s.wait_ge(sV.h, ep_r[gg])
                        s.wait_ge(sT.h, dmm[(16, 7)])
                        for dc in range(NC):
                            ins = s.activation(r2reg(dc),
                                               X[:, dc, gb:gb + 512], AF.Square)
                        sS.inc(ins)
                        ep_r2[gg] = sS.n
                    SQ(_)

                    if gg % 2 == 0:
                        stA = psHG[0:1, 0:2, :]; stB = psHU[0:1, 0:2, :]
                        bcA = psHG[:, 0:2, :]; bcB = psHG[:, 2:4, :]
                    else:
                        stA = psACC[0:1, 0:2, :]; stB = psACC[0:1, 2:4, :]
                        bcA = psACC[:, 0:2, :]; bcB = psACC[:, 2:4, :]

                    def _(t, stA=stA, stB=stB):
                        t.wait_ge(sS.h, ep_r2[gg])
                        if gg == 0:
                            t.wait_ge(sV.h, h_m[(16, 7)])
                        elif gg == 1:
                            t.wait_ge(sV.h, accev[(16, 7)])
                        else:
                            t.wait_ge(sV.h, ep_ab[gg - 2])
                        for c in range(NC):
                            t.matmul(stA, onesb_col[:, :], X[:, c, gb:gb + 512],
                                     start=(c == 0), stop=(c == NC - 1))
                            ins = t.matmul(stB, onesb_col[:, :], r2reg(c),
                                           start=(c == 0), stop=(c == NC - 1))
                        sT.inc(ins)
                        ep_st[gg] = sT.n
                    TQ(_)

                    rows_ok = ln_rowmath(512, stA, stB, ep_st[gg])

                    def _(t, bcA=bcA, bcB=bcB):
                        t.wait_ge(sV.h, rows_ok)
                        ins = t.matmul(bcA, onesb_row[:, :], rowA[:, 0:512],
                                       start=True, stop=True)
                        ins = t.matmul(bcB, onesb_row[:, :], rowB[:, 0:512],
                                       start=True, stop=True)
                        sT.inc(ins)
                        ep_bc[gg] = sT.n
                    TQ(_)

                    def _(v, bcA=bcA, bcB=bcB):
                        v.wait_ge(sT.h, ep_bc[gg])
                        ab = Abuf[:, (gg % 2) * 512:(gg % 2) * 512 + 512]
                        bb = Bbuf[:, (gg % 2) * 512:(gg % 2) * 512 + 512]
                        v.tensor_copy(ab, bcA)
                        v.tensor_copy(bb, bcB)
                        for c in range(NC):
                            xv = X[:, c, gb:gb + 512]
                            v.tensor_tensor(xv, xv, ab, ALU.mult)
                            ins = v.tensor_tensor(xv, xv, bb, ALU.add)
                        sV.inc(ins)
                        ep_ab[gg] = sV.n
                    VQ(_)

                prev_psacc_free_V = ep_ab[3]
                prev_unit16_T = dmm[(16, 7)]
                prev_epB2_V = ep_ab[3]

            b2_V, b2_T, b2_S = sV.n, sT.n, sS.n
        tap("b2", b2_V, b2_T, b2_S)

        with ExitStack() as ph:
            ropeC = ph.enter_context(nc.sbuf_tensor("c_ropeC", [128, T], BF16))
            ropeS = ph.enter_context(nc.sbuf_tensor("c_ropeS", [128, T], BF16))
            masks = ph.enter_context(nc.sbuf_tensor("c_masks", [128, 4, 512], BF16))
            woAll = ph.enter_context(nc.sbuf_tensor("woAll", [128, 8, D], BF16))
            wqh = ph.enter_context(nc.sbuf_tensor("wqh", [128, 2, NC, 128], BF16))
            wkh = ph.enter_context(nc.sbuf_tensor("wkh", [128, 2, NC, 128], BF16))
            wvh = ph.enter_context(nc.sbuf_tensor("wvh", [128, 2, NC, 128], BF16))
            qbuf = ph.enter_context(nc.sbuf_tensor("qbuf", [128, T], BF16))
            kbuf = ph.enter_context(nc.sbuf_tensor("kbuf", [128, T], BF16))
            vbuf = ph.enter_context(nc.sbuf_tensor("vbuf", [128, 16, 128], BF16))
            ptile = ph.enter_context(nc.sbuf_tensor("ptile", [128, 4, 512], BF16))
            q2t = ph.enter_context(nc.sbuf_tensor("q2t", [128, 4, 512], BF16))
            qraw = ph.enter_context(nc.sbuf_tensor("qraw", [128, T], BF16))
            k2t = ph.enter_context(nc.sbuf_tensor("k2t", [128, 2, 512], BF16))
            o_all = ph.enter_context(nc.sbuf_tensor("o_all", [128, 8, T], BF16))
            rsk = ph.enter_context(nc.sbuf_tensor("rsk", [128, 16], F32))
            rskt = ph.enter_context(nc.sbuf_tensor("rskt", [128, 16], F32))
            rs_qb = ph.enter_context(nc.sbuf_tensor("rs_qb", [128, 512], BF16))
            recipb = ph.enter_context(nc.sbuf_tensor("recipb", [128, 512], BF16))
            rowQ = ph.enter_context(nc.sbuf_tensor("rowQ", [1, 512], BF16))
            qtmp = ph.enter_context(nc.sbuf_tensor("qtmp", [128, 512], BF16))
            krot = ph.enter_context(nc.sbuf_tensor("krot", [128, 512], BF16))
            ktc = ph.enter_context(nc.sbuf_tensor("ktc", [128, 512], BF16))
            r2b3 = ph.enter_context(nc.sbuf_tensor("r2b3", [128, NC, 512], BF16))
            psQK = ph.enter_context(nc.psum_tensor("psQK", [128, 2, 512], F32))
            psSc = ph.enter_context(nc.psum_tensor("psSc", [128, 2, 512], F32))
            psO = ph.enter_context(nc.psum_tensor("psO", [128, 2, 512], F32))
            psMix = ph.enter_context(nc.psum_tensor("psMix", [128, 512], F32))
            psDen2 = ph.enter_context(nc.psum_tensor("psDen2", [128, 512], F32))

            def _(g):
                g.wait_ge(sV.h, b2_V); g.wait_ge(sT.h, b2_T); g.wait_ge(sS.h, b2_S)
                for h8 in range(8):
                    g.dma_start(out=woAll[:, h8, :], in_=dt["wo"][h8]).then_inc(dG.h, 16)
                dG.n += 8 * 16
            GQ(_)
            wo_ok = dG.n

            def _(y):
                y.wait_ge(sV.h, b2_V); y.wait_ge(sT.h, b2_T); y.wait_ge(sS.h, b2_S)
                y.dma_start(out=ropeC[:], in_=dt["ropeC"][:, :]).then_inc(dS.h, 16)
                y.dma_start(out=ropeS[:], in_=dt["ropeS"][:, :]).then_inc(dS.h, 16)
                y.dma_start(out=masks[:],
                            in_=dt["masks"].rearrange("m p n -> p m n")).then_inc(dS.h, 16)
                dS.n += 48
            YQ(_)
            rope_okS = dS.n

            qkv_done = {}
            wh_ok = {}

            def load_head(H):
                s_i, h8 = divmod(H, 8)
                def _(g):
                    if H >= 2:
                        g.wait_ge(sT.h, qkv_done[H - 2])
                    for dst, nm in [(wqh, "wq"), (wkh, "wk"), (wvh, "wv")]:
                        g.dma_start(out=dst[:, H % 2, :, :],
                                    in_=dt[nm][:, h8, :].rearrange("(c p) d -> p c d", p=128)
                                    ).then_inc(dG.h, 16)
                    dG.n += 48
                    wh_ok[H] = dG.n
                GQ(_)

            load_head(0)
            load_head(1)

            prev_exp_S = 0
            prev_onorm_last = 0
            prev_head_free = 0

            for s_i in range(NB):
                tb = s_i * T
                for h8 in range(8):
                    H = s_i * 8 + h8
                    slot = H % 2
                    if H + 1 < 16:
                        load_head(H + 1)

                    units = [("k", j) for j in range(4)] + [("q", j) for j in range(4)] \
                          + [("v", j) for j in range(16)]
                    cons_v = {}
                    mm_t = {}
                    ksq_s = {}; msk_t = {}
                    qsq_s = {}; msq_t = {}; qst_v = {}; qsq2_s = {}; qbc_t = {}
                    qev_v = {}
                    rsk_vv = {}

                    def emit_qchain_a(jq):
                        def _(t):
                            t.wait_ge(sS.h, qsq_s[jq])
                            if jq == 0:
                                t.wait_ge(sV.h, rsk_vv["v"])
                            else:
                                t.wait_ge(sV.h, qev_v[jq - 1])
                            ins = t.matmul(psMix[0:1, :], onesb_col[:, :],
                                           q2t[:, jq, :], start=True, stop=True)
                            sT.inc(ins)
                            msq_t[jq] = sT.n
                        TQ(_)

                        def _(v):
                            v.wait_ge(sT.h, msq_t[jq])
                            ins = v.tensor_scalar(statf[:, 0, :], psMix[0:1, :],
                                                  1.0 / 128.0, RMS_EPS, ALU.mult, op1=ALU.add)
                            sV.inc(ins); v.wait_ge(sV.h, sV.n)
                            ins = v.reciprocal(statf[:, 1, :], statf[:, 0, :])
                            sV.inc(ins)
                            qst_v[jq] = sV.n
                        VQ(_)

                        def _(s):
                            s.wait_ge(sV.h, qst_v[jq])
                            ins = s.activation(rowQ[:, :], statf[:, 1, :], AF.Sqrt)
                            sS.inc(ins)
                            qsq2_s[jq] = sS.n
                        SQ(_)

                    def emit_qchain_b(jq):
                        def _(t):
                            t.wait_ge(sS.h, qsq2_s[jq])
                            ins = t.matmul(psMix[:, :], onesb_row[:, :], rowQ[:, :],
                                           start=True, stop=True)
                            sT.inc(ins)
                            qbc_t[jq] = sT.n
                        TQ(_)

                        def _(v):
                            v.wait_ge(sT.h, qbc_t[jq])
                            v.tensor_copy(rs_qb[:, :], psMix[:, :])
                            v.tensor_tensor(qtmp[:, :], qraw[:, jq * 512:(jq + 1) * 512],
                                            rs_qb[:, :], ALU.mult)
                            v.tensor_tensor(ktc[:, :], qtmp[:, :],
                                            ropeC[:, jq * 512:(jq + 1) * 512], ALU.mult)
                            v.tensor_copy(krot[0:64, :], qtmp[64:128, :])
                            v.tensor_copy(krot[64:128, :], qtmp[0:64, :])
                            v.tensor_tensor(krot[:, :], krot[:, :],
                                            ropeS[:, jq * 512:(jq + 1) * 512], ALU.mult)
                            ins = v.tensor_tensor(qbuf[:, jq * 512:(jq + 1) * 512],
                                                  ktc[:, :], krot[:, :], ALU.add)
                            sV.inc(ins)
                            qev_v[jq] = sV.n
                        VQ(_)

                    for ui, (kind, j) in enumerate(units):
                        def _(t):
                            if ui == 0:
                                t.wait_ge(dG.h, wh_ok[H])
                                if H == 0:
                                    t.wait_ge(sV.h, b2_V)
                                    t.wait_ge(sS.h, b2_S)
                                    t.wait_ge(dS.h, rope_okS)
                                else:
                                    t.wait_ge(sV.h, prev_head_free)
                            if ui >= 2:
                                for sem, mk in cons_v[ui - 2]:
                                    t.wait_ge(sem.h, mk)
                            w = {"q": wqh, "k": wkh, "v": wvh}[kind]
                            if kind in ("q", "k"):
                                for k in range(NC):
                                    ins = t.matmul(psQK[:, ui % 2, :], w[:, slot, k, :],
                                                   X[:, k, tb + j * 512:tb + (j + 1) * 512],
                                                   start=(k == 0), stop=(k == NC - 1))
                            else:
                                for k in range(NC):
                                    ins = t.matmul(psQK[:, ui % 2, 0:128],
                                                   X[:, k, tb + j * 128:tb + (j + 1) * 128],
                                                   w[:, slot, k, :],
                                                   start=(k == 0), stop=(k == NC - 1))
                            sT.inc(ins)
                            mm_t[ui] = sT.n
                        TQ(_)

                        if kind == "v":
                            def _(v):
                                v.wait_ge(sT.h, mm_t[ui])
                                ins = v.tensor_copy(vbuf[:, j, :], psQK[:, ui % 2, 0:128])
                                sV.inc(ins)
                                cons_v[ui] = [(sV, sV.n)]
                            VQ(_)
                            if j == 0:
                                emit_qchain_a(0)
                            elif j % 4 == 2 and j // 4 < 4:
                                emit_qchain_b(j // 4)
                                if j // 4 + 1 < 4:
                                    emit_qchain_a(j // 4 + 1)
                        elif kind == "k":
                            def _(s):
                                s.wait_ge(sT.h, mm_t[ui])
                                if ui >= 2:
                                    s.wait_ge(sT.h, msk_t.get(ui - 2, 0))
                                ins = s.activation(k2t[:, j % 2, :], psQK[:, ui % 2, :],
                                                   AF.Square)
                                sS.inc(ins)
                                ksq_s[ui] = sS.n
                            SQ(_)

                            def _(t):
                                t.wait_ge(sS.h, ksq_s[ui])
                                for jj in range(4):
                                    ins = t.matmul(psMix[:, j * 4 + jj:j * 4 + jj + 1],
                                                   k2t[:, j % 2, jj * 128:(jj + 1) * 128],
                                                   onesb_col[:, :], start=True, stop=True)
                                sT.inc(ins)
                                msk_t[ui] = sT.n
                            TQ(_)

                            def _(v):
                                v.wait_ge(sS.h, ksq_s[ui])
                                v.tensor_tensor(ktc[:, :], psQK[:, ui % 2, :],
                                                ropeC[:, j * 512:(j + 1) * 512], ALU.mult)
                                v.tensor_copy(krot[0:64, :], psQK[64:128, ui % 2, :])
                                v.tensor_copy(krot[64:128, :], psQK[0:64, ui % 2, :])
                                v.tensor_tensor(krot[:, :], krot[:, :],
                                                ropeS[:, j * 512:(j + 1) * 512], ALU.mult)
                                ins = v.tensor_tensor(kbuf[:, j * 512:(j + 1) * 512],
                                                      ktc[:, :], krot[:, :], ALU.add)
                                sV.inc(ins)
                                cons_v[ui] = [(sV, sV.n)]
                            VQ(_)

                            if j == 3:
                                def _(v):
                                    v.wait_ge(sT.h, msk_t[ui])
                                    ins = v.tensor_scalar(rskt[:, :], psMix[:, 0:16],
                                                          1.0 / 128.0, RMS_EPS, ALU.mult,
                                                          op1=ALU.add)
                                    sV.inc(ins); v.wait_ge(sV.h, sV.n)
                                    ins = v.reciprocal(rskt[:, :], rskt[:, :])
                                    sV.inc(ins)
                                    rsk_vv["a"] = sV.n
                                VQ(_)

                                def _(s):
                                    s.wait_ge(sV.h, rsk_vv["a"])
                                    ins = s.activation(rsk[:, :], rskt[:, :], AF.Sqrt)
                                    sS.inc(ins)
                                    rsk_vv["s"] = sS.n
                                SQ(_)

                                def _(v):
                                    v.wait_ge(sS.h, rsk_vv["s"])
                                    ins = v.tensor_scalar(rsk[:, :], rsk[:, :],
                                                          1.0 / 128.0, None, ALU.mult)
                                    sV.inc(ins)
                                    rsk_vv["v"] = sV.n
                                VQ(_)
                        else:
                            def _(s):
                                s.wait_ge(sT.h, mm_t[ui])
                                ins = s.activation(q2t[:, j, :], psQK[:, ui % 2, :],
                                                   AF.Square)
                                sS.inc(ins)
                                qsq_s[j] = sS.n
                            SQ(_)

                            def _(v):
                                v.wait_ge(sS.h, qsq_s[j])
                                ins = v.tensor_copy(qraw[:, j * 512:(j + 1) * 512],
                                                    psQK[:, ui % 2, :])
                                sV.inc(ins)
                                cons_v[ui] = [(sV, sV.n), (sS, qsq_s[j])]
                            VQ(_)

                    qkv_done[H] = sT.n
                    qkv_last_v = sV.n

                    exp_s = {}; msk_v = {}; sc_t = {}; pv_t = {}
                    onorm_v = {}; den_v = {}; denb_t = {}
                    tiles = [(qtg, kt) for qtg in range(4) for kt in range(4 * qtg + 4)]
                    scslots = [psSc[:, 0, :], psSc[:, 1, :], psQK[:, 0, :], psQK[:, 1, :]]

                    def emit_sc(m):
                        qtg, kt = tiles[m]
                        def _(t):
                            if m == 0:
                                t.wait_ge(sV.h, qkv_last_v)
                                if prev_exp_S:
                                    t.wait_ge(sS.h, prev_exp_S)
                            if m >= 4:
                                t.wait_ge(sS.h, exp_s[m - 4])
                            ins = t.matmul(scslots[m % 4],
                                           kbuf[:, kt * 128:(kt + 1) * 128],
                                           qbuf[:, qtg * 512:(qtg + 1) * 512],
                                           start=True, stop=True)
                            sT.inc(ins)
                            sc_t[m] = sT.n
                        TQ(_)

                        def _(s):
                            s.wait_ge(sT.h, sc_t[m])
                            if m >= 4:
                                s.wait_ge(sT.h, pv_t[m - 4])
                            ins = s.activation(ptile[:, m % 4, :], scslots[m % 4],
                                               AF.Exp, scale=rsk[:, kt:kt + 1])
                            sS.inc(ins)
                            exp_s[m] = sS.n
                        SQ(_)

                        if kt >= 4 * qtg:
                            def _(v):
                                v.wait_ge(sS.h, exp_s[m])
                                ins = v.tensor_tensor(ptile[:, m % 4, :], ptile[:, m % 4, :],
                                                      masks[:, kt - 4 * qtg, :], ALU.mult)
                                sV.inc(ins)
                                msk_v[m] = sV.n
                            VQ(_)

                    def emit_pv(m):
                        qtg, kt = tiles[m]
                        nkt = 4 * qtg + 4
                        def _(t):
                            if m in msk_v:
                                t.wait_ge(sV.h, msk_v[m])
                            else:
                                t.wait_ge(sS.h, exp_s[m])
                            if kt == 0:
                                if qtg > 0:
                                    t.wait_ge(sV.h, den_v[qtg - 1])
                                if qtg >= 2:
                                    t.wait_ge(sV.h, onorm_v[qtg - 2])
                            t.matmul(psMix[0:1, :], onesb_col[:, :], ptile[:, m % 4, :],
                                     start=(kt == 0), stop=(kt == nkt - 1))
                            ins = t.matmul(psO[:, qtg % 2, :], vbuf[:, kt, :],
                                           ptile[:, m % 4, :],
                                           start=(kt == 0), stop=(kt == nkt - 1))
                            sT.inc(ins)
                            pv_t[m] = sT.n
                        TQ(_)

                        if kt == nkt - 1:
                            def _(v):
                                v.wait_ge(sT.h, pv_t[m])
                                with nc.allow_low_precision(reason="bf16 recip row; tile is bf16"):
                                    ins = v.reciprocal(rowQ[:, :], psMix[0:1, :])
                                sV.inc(ins)
                                den_v[qtg] = sV.n
                            VQ(_)

                            def _(t):
                                t.wait_ge(sV.h, den_v[qtg])
                                if qtg >= 2:
                                    t.wait_ge(sV.h, onorm_v[qtg - 2])
                                ins = t.matmul(psDen2[:, :], onesb_row[:, :], rowQ[:, :],
                                               start=True, stop=True)
                                sT.inc(ins)
                                denb_t[qtg] = sT.n
                            TQ(_)

                            def _(v):
                                v.wait_ge(sT.h, denb_t[qtg])
                                v.tensor_copy(recipb[:, :], psDen2[:, :])
                                ins = v.tensor_tensor(o_all[:, h8, qtg * 512:(qtg + 1) * 512],
                                                      psO[:, qtg % 2, :], recipb[:, :],
                                                      ALU.mult)
                                sV.inc(ins)
                                onorm_v[qtg] = sV.n
                            VQ(_)

                    for m in range(len(tiles)):
                        emit_sc(m)
                        if m >= 2:
                            emit_pv(m - 2)
                    emit_pv(len(tiles) - 2)
                    emit_pv(len(tiles) - 1)

                    prev_exp_S = exp_s[len(tiles) - 1]
                    prev_onorm_last = onorm_v[3]
                    prev_head_free = onorm_v[3]

                op_t = {}; opr_v = {}
                for i in range(32):
                    dc, g4 = divmod(i, 4)

                    def _(t):
                        if i == 0:
                            t.wait_ge(sV.h, prev_onorm_last)
                            t.wait_ge(dG.h, wo_ok)
                        if i >= 2:
                            t.wait_ge(sV.h, opr_v[i - 2])
                        for h8 in range(8):
                            ins = t.matmul(psQK[:, i % 2, :],
                                           woAll[:, h8, dc * 128:(dc + 1) * 128],
                                           o_all[:, h8, g4 * 512:(g4 + 1) * 512],
                                           start=(h8 == 0), stop=(h8 == 7))
                        sT.inc(ins)
                        op_t[i] = sT.n
                    TQ(_)

                    def _(v):
                        v.wait_ge(sT.h, op_t[i])
                        sl = slice(tb + g4 * 512, tb + (g4 + 1) * 512)
                        ins = v.tensor_tensor(X[:, dc, sl], X[:, dc, sl],
                                              psQK[:, i % 2, :], ALU.add)
                        sV.inc(ins)
                        opr_v[i] = sV.n
                    VQ(_)
                r3_ok = sV.n

                b3r2_s = {}; b3st_t = {}; b3rows = {}; b3bc_t = {}; b3ab_v = {}
                for g4 in range(4):
                    def _(s):
                        s.wait_ge(sV.h, r3_ok)
                        if g4 > 0:
                            s.wait_ge(sT.h, b3st_t[g4 - 1])
                        for c in range(NC):
                            ins = s.activation(r2b3[:, c, :],
                                               X[:, c, tb + g4 * 512:tb + (g4 + 1) * 512],
                                               AF.Square)
                        sS.inc(ins)
                        b3r2_s[g4] = sS.n
                    SQ(_)

                    def _(t):
                        t.wait_ge(sS.h, b3r2_s[g4])
                        if g4 == 0:
                            t.wait_ge(sV.h, prev_onorm_last)
                        else:
                            t.wait_ge(sV.h, b3rows[g4 - 1])
                        for c in range(NC):
                            t.matmul(psSc[0:1, 0, :], onesb_col[:, :],
                                     X[:, c, tb + g4 * 512:tb + (g4 + 1) * 512],
                                     start=(c == 0), stop=(c == NC - 1))
                            ins = t.matmul(psSc[0:1, 1, :], onesb_col[:, :],
                                           r2b3[:, c, :],
                                           start=(c == 0), stop=(c == NC - 1))
                        sT.inc(ins)
                        b3st_t[g4] = sT.n
                    TQ(_)

                    rows_ok = ln_rowmath(512, psSc[0:1, 0, :], psSc[0:1, 1, :], b3st_t[g4])
                    b3rows[g4] = rows_ok

                    def _(t):
                        t.wait_ge(sV.h, rows_ok)
                        if g4 >= 2:
                            t.wait_ge(sV.h, b3ab_v[g4 - 2])
                        ins = t.matmul(psO[:, 0, :], onesb_row[:, :], rowA[:, :],
                                       start=True, stop=True)
                        ins = t.matmul(psO[:, 1, :], onesb_row[:, :], rowB[:, :],
                                       start=True, stop=True)
                        sT.inc(ins)
                        b3bc_t[g4] = sT.n
                    TQ(_)

                    def _(v):
                        v.wait_ge(sT.h, b3bc_t[g4])
                        v.tensor_copy(Abuf[:, g4 * 512:(g4 + 1) * 512], psO[:, 0, :])
                        ins = v.tensor_copy(Bbuf[:, g4 * 512:(g4 + 1) * 512], psO[:, 1, :])
                        sV.inc(ins)
                        b3ab_v[g4] = sV.n
                    VQ(_)

                def _(v):
                    for c in range(NC):
                        xv = X[:, c, tb:tb + T]
                        v.tensor_tensor(xv, xv, Abuf[:, 0:T], ALU.mult)
                        ins = v.tensor_tensor(xv, xv, Bbuf[:, 0:T], ALU.add)
                    sV.inc(ins)
                VQ(_)
                prev_onorm_last = sV.n
                prev_head_free = sV.n

            b3_V, b3_T, b3_S = sV.n, sT.n, sS.n
        tap("b3", b3_V, b3_T, b3_S)

        with ExitStack() as ph:
            WinT = ph.enter_context(nc.sbuf_tensor("WinT", [128, NC, 128], BF16))
            WoutT = ph.enter_context(nc.sbuf_tensor("WoutT", [L1_M, D], BF16))
            W1T = ph.enter_context(nc.sbuf_tensor("W1T", [128, 8], F32))
            b1T = ph.enter_context(nc.sbuf_tensor("b1T", [128, 8], F32))
            W2T = ph.enter_context(nc.sbuf_tensor("W2T", [128, 8], F32))
            b2T = ph.enter_context(nc.sbuf_tensor("b2T", [L1_M, 1], F32))
            binT = ph.enter_context(nc.sbuf_tensor("binT", [128, 1], F32))
            bcomb = ph.enter_context(nc.sbuf_tensor("bcomb", [128, 8], F32))
            trelu = ph.enter_context(nc.sbuf_tensor("trelu", [128, 2, 512], BF16))
            gP = ph.enter_context(nc.sbuf_tensor("gP", [128, 2, 512], BF16))
            gbuf = ph.enter_context(nc.sbuf_tensor("gbuf", [L1_M, 2, 512], BF16))
            gtmp = ph.enter_context(nc.sbuf_tensor("gtmp", [L1_M, 2, 512], BF16))
            r2b4 = ph.enter_context(nc.sbuf_tensor("r2b4", [128, NC, 512], BF16))
            rOut = ph.enter_context(nc.sbuf_tensor("rOut", [128, NC, 512], F32))
            ostage = ph.enter_context(nc.sbuf_tensor("ostage", [128, 2, D], F32))
            psS4 = ph.enter_context(nc.psum_tensor("psS4", [128, 2, 512], F32))
            psO4 = ph.enter_context(nc.psum_tensor("psO4", [128, 2, 512], F32))
            psB4 = ph.enter_context(nc.psum_tensor("psB4", [128, 2, 512], F32))

            def _(g):
                g.wait_ge(sV.h, b3_V); g.wait_ge(sT.h, b3_T); g.wait_ge(sS.h, b3_S)
                g.dma_start(out=WinT[:],
                            in_=dt["Win2"].rearrange("(c p) m -> p c m", p=128)).then_inc(dG.h, 16)
                g.dma_start(out=WoutT[:], in_=dt["Wout"][:, :]).then_inc(dG.h, 16)
                dG.n += 32
            GQ(_)
            w4_okG = dG.n

            def _(y):
                y.wait_ge(sV.h, b3_V); y.wait_ge(sT.h, b3_T); y.wait_ge(sS.h, b3_S)
                for dst, nm in [(W1T, "W1P"), (b1T, "b1P"), (W2T, "W2P"),
                                (b2T, "b2"), (binT, "binP")]:
                    y.dma_start(out=dst[:], in_=dt[nm][:, :]).then_inc(dS.h, 16)
                dS.n += 5 * 16
            YQ(_)
            w4_okS = dS.n

            def _(v):
                v.wait_ge(dS.h, w4_okS)
                for i in range(8):
                    ins = v.tensor_scalar(bcomb[:, i:i + 1], W1T[:, i:i + 1],
                                          binT[:, :], b1T[:, i:i + 1], ALU.mult, op1=ALU.add)
                sV.inc(ins)
            VQ(_)
            bcomb_ok = sV.n

            smm_t = {}; relu_s = {}; gacc_v = {}; gfin_v = {}; omm_t = {}; ffr_v = {}
            for g in range(8):
                gb = g * 512

                def _(t):
                    if g == 0:
                        t.wait_ge(dG.h, w4_okG)
                        t.wait_ge(sV.h, b3_V)
                    elif g >= 2:
                        t.wait_ge(sS.h, relu_s[(g - 2, 7)])
                    for k in range(NC):
                        ins = t.matmul(psS4[:, g % 2, :], WinT[:, k, :], X[:, k, gb:gb + 512],
                                       start=(k == 0), stop=(k == NC - 1))
                    sT.inc(ins)
                    smm_t[g] = sT.n
                TQ(_)

                for i in range(8):
                    def _(s):
                        s.wait_ge(sT.h, smm_t[g])
                        if i == 0:
                            s.wait_ge(sV.h, bcomb_ok)
                            if g > 0:
                                s.wait_ge(sV.h, gacc_v[(g - 1, 7)])
                        if i >= 2:
                            s.wait_ge(sV.h, gacc_v[(g, i - 2)])
                        ins = s.activation(trelu[:, i % 2, :], psS4[:, g % 2, :], AF.Relu,
                                           bias=bcomb[:, i:i + 1], scale=W1T[:, i:i + 1])
                        sS.inc(ins)
                        relu_s[(g, i)] = sS.n
                    SQ(_)

                    def _(v):
                        v.wait_ge(sS.h, relu_s[(g, i)])
                        if i == 0:
                            ins = v.tensor_scalar(gP[:, g % 2, :], trelu[:, i % 2, :],
                                                  W2T[:, i:i + 1], None, ALU.mult)
                        else:
                            ins = v.scalar_tensor_tensor(gP[:, g % 2, :], trelu[:, i % 2, :],
                                                         W2T[:, i:i + 1], gP[:, g % 2, :],
                                                         ALU.mult, ALU.add)
                        sV.inc(ins)
                        gacc_v[(g, i)] = sV.n
                    VQ(_)

                def _(v):
                    v.tensor_copy(gtmp[:, g % 2, :], gP[64:128, g % 2, :])
                    ins = v.scalar_tensor_tensor(gbuf[:, g % 2, :], gP[0:64, g % 2, :],
                                                 b2T[:, :], gtmp[:, g % 2, :],
                                                 ALU.add, ALU.add)
                    sV.inc(ins)
                    gfin_v[g] = sV.n
                VQ(_)

                for dc in range(NC):
                    jj = g * NC + dc

                    def _(t):
                        t.wait_ge(sV.h, gfin_v[g])
                        if jj >= 2:
                            t.wait_ge(sV.h, ffr_v[jj - 2])
                        ins = t.matmul(psO4[:, jj % 2, :],
                                       WoutT[0:L1_M, dc * 128:(dc + 1) * 128],
                                       gbuf[0:L1_M, g % 2, :], start=True, stop=True)
                        sT.inc(ins)
                        omm_t[jj] = sT.n
                    TQ(_)

                    def _(v):
                        v.wait_ge(sT.h, omm_t[jj])
                        ins = v.scalar_tensor_tensor(X[:, dc, gb:gb + 512],
                                                     psO4[:, jj % 2, :], bout_sb[:, dc:dc + 1],
                                                     X[:, dc, gb:gb + 512], ALU.add, ALU.add)
                        sV.inc(ins)
                        ffr_v[jj] = sV.n
                    VQ(_)
            ffn_done_v = sV.n

            f4r2_s = {}; fst_t = {}; f4rows = {}; fbc_t = {}; fab_v = {}
            ftr_t = {}; fev_v = {}; grp_fev = {}; ostage_dma = {}
            for g in range(8):
                gb = g * 512

                def _(s):
                    s.wait_ge(sV.h, ffn_done_v)
                    if g > 0:
                        s.wait_ge(sT.h, fst_t[g - 1])
                    for c in range(NC):
                        ins = s.activation(r2b4[:, c, :], X[:, c, gb:gb + 512], AF.Square)
                    sS.inc(ins)
                    f4r2_s[g] = sS.n
                SQ(_)

                def _(t):
                    t.wait_ge(sS.h, f4r2_s[g])
                    if g == 0:
                        t.wait_ge(sV.h, ffn_done_v)
                    else:
                        t.wait_ge(sV.h, grp_fev[g - 1])
                    for c in range(NC):
                        t.matmul(psS4[0:1, 0, :], onesb_col[:, :], X[:, c, gb:gb + 512],
                                 start=(c == 0), stop=(c == NC - 1))
                        ins = t.matmul(psO4[0:1, 0, :], onesb_col[:, :], r2b4[:, c, :],
                                       start=(c == 0), stop=(c == NC - 1))
                    sT.inc(ins)
                    fst_t[g] = sT.n
                TQ(_)

                rows_ok = ln_rowmath(512, psS4[0:1, 0, :], psO4[0:1, 0, :], fst_t[g])
                f4rows[g] = rows_ok

                def _(t):
                    t.wait_ge(sV.h, rows_ok)
                    ins = t.matmul(psB4[:, 0, :], onesb_row[:, :], rowA[:, :],
                                   start=True, stop=True)
                    ins = t.matmul(psB4[:, 1, :], onesb_row[:, :], rowB[:, :],
                                   start=True, stop=True)
                    sT.inc(ins)
                    fbc_t[g] = sT.n
                TQ(_)

                def _(v):
                    v.wait_ge(sT.h, fbc_t[g])
                    v.tensor_copy(Abuf[:, 0:512], psB4[:, 0, :])
                    v.tensor_copy(Bbuf[:, 0:512], psB4[:, 1, :])
                    for c in range(NC):
                        v.tensor_tensor(rOut[:, c, :], X[:, c, gb:gb + 512],
                                        Abuf[:, 0:512], ALU.mult)
                        ins = v.tensor_tensor(rOut[:, c, :], rOut[:, c, :],
                                              Bbuf[:, 0:512], ALU.add)
                    sV.inc(ins)
                    fab_v[g] = sV.n
                VQ(_)

                for tt in range(4):
                    gtile = g * 4 + tt
                    jj0 = gtile * NC
                    for c in range(NC):
                        jj = jj0 + c

                        def _(t):
                            t.wait_ge(sV.h, fab_v[g])
                            if jj >= 2:
                                t.wait_ge(sV.h, fev_v[jj - 2])
                            ins = t.transpose(psO4[:, jj % 2, 0:128],
                                              rOut[:, c, tt * 128:(tt + 1) * 128],
                                              ident_f32[:])
                            sT.inc(ins)
                            ftr_t[jj] = sT.n
                        TQ(_)

                        def _(v):
                            v.wait_ge(sT.h, ftr_t[jj])
                            if c == 0 and gtile >= 2:
                                v.wait_ge(dS.h, ostage_dma[gtile - 2])
                            ins = v.tensor_copy(
                                ostage[:, gtile % 2, c * 128:(c + 1) * 128],
                                psO4[:, jj % 2, 0:128])
                            sV.inc(ins)
                            fev_v[jj] = sV.n
                        VQ(_)

                    def _(y):
                        y.wait_ge(sV.h, fev_v[jj0 + NC - 1])
                        tok0 = gb + tt * 128
                        s_ii, toff = divmod(tok0, T)
                        y.dma_start(out=out[s_ii, toff:toff + 128, :],
                                    in_=ostage[:, gtile % 2, :]).then_inc(dS.h, 16)
                        dS.n += 16
                        ostage_dma[gtile] = dS.n
                    YQ(_)
                grp_fev[g] = sV.n

            def _(y):
                y.wait_ge(dS.h, dS.n)
            YQ(_)

            def _(g):
                g.wait_ge(dG.h, dG.n)
            GQ(_)

    return nc, dt


_BUILD_CACHE = {}


def _get_nc(taps=()):
    key = tuple(taps)
    if key not in _BUILD_CACHE:
        _BUILD_CACHE[key] = build_nc(taps)
    return _BUILD_CACHE[key]


def _make_in_maps(inputs):
    consts = _host_consts()
    z = np.ascontiguousarray(np.asarray(inputs["L0_attn_Z"], np.float32))
    shared = {
        "l0_W": np.ascontiguousarray(inputs["L0_attn_W"], np.float32),
        "l0_b": _fm_vec(inputs["L0_attn_b"]),
        "l0_gamma": _fm_vec(inputs["L0_attn_gamma"]),
        "l0_beta": _fm_vec(inputs["L0_attn_beta"]),
        "decay": np.asarray(inputs["L0_attn_decay"], np.float32).reshape(1, 1),
        "gate_w": np.ascontiguousarray(inputs["L0_ffn_gate_w"], np.float32),
        "gate_b": np.asarray(inputs["L0_ffn_gate_b"], np.float32).reshape(MOE_E, 1),
        "sh_wg": np.ascontiguousarray(inputs["L0_ffn_shared_wg"], np.float32),
        "sh_wu": np.ascontiguousarray(inputs["L0_ffn_shared_wu"], np.float32),
        "sh_wd": np.ascontiguousarray(inputs["L0_ffn_shared_wd"], np.float32),
        "ex_wg": np.ascontiguousarray(inputs["L0_ffn_experts_wg"], np.float32),
        "ex_wu": np.ascontiguousarray(inputs["L0_ffn_experts_wu"], np.float32),
        "ex_wd": np.ascontiguousarray(inputs["L0_ffn_experts_wd"], np.float32),
        "wq": np.ascontiguousarray(inputs["L1_attn_wq"], np.float32),
        "wk": np.ascontiguousarray(inputs["L1_attn_wk"], np.float32),
        "wv": np.ascontiguousarray(inputs["L1_attn_wv"], np.float32),
        "wo": np.ascontiguousarray(inputs["L1_attn_wo"], np.float32),
        "Win2": np.ascontiguousarray(np.concatenate(
            [np.asarray(inputs["L1_ffn_Win"], np.float32)] * 2, axis=1)),
        "binP": np.ascontiguousarray(np.concatenate(
            [np.asarray(inputs["L1_ffn_bin_"], np.float32)] * 2).reshape(128, 1)),
        "W1P": _packpair(inputs["L1_ffn_W1"]),
        "b1P": _packpair(inputs["L1_ffn_b1"]),
        "W2P": _packpair(inputs["L1_ffn_W2"]),
        "b2": np.asarray(inputs["L1_ffn_b2"], np.float32).reshape(L1_M, 1),
        "Wout": np.ascontiguousarray(inputs["L1_ffn_Wout"], np.float32),
        "bout": _fm_vec(inputs["L1_ffn_bout"]),
    }
    shared.update(consts)
    maps = []
    for c in range(8):
        m = dict(shared)
        m["x"] = np.ascontiguousarray(z[2 * c:2 * c + 2])
        maps.append(m)
    return maps


def kernel(**inputs):
    nc, _ = _get_nc()
    in_maps = _make_in_maps(inputs)
    res = run_bass_kernel_spmd(nc, in_maps, core_ids=list(range(8)))
    return np.concatenate([r["out"] for r in res.results], axis=0)


# version 10
# speedup vs baseline: 1.0515x; 1.0185x over previous
import numpy as np
import ml_dtypes
from contextlib import ExitStack

import concourse.bass as bass
import concourse.mybir as mybir
from concourse.alu_op_type import AluOpType as ALU
from concourse.bass_utils import run_bass_kernel_spmd

F32 = mybir.dt.float32
BF16 = mybir.dt.bfloat16
AF = mybir.ActivationFunctionType
AX = mybir.AxisListType

NB = 2
T = 2048
NT = NB * T
D = 1024
NC = 8
LN_EPS = 1e-5
RMS_EPS = 1e-6
ROPE_BASE = 10000.0
MOE_I = 512
MOE_E = 16
L1_M = 64
L1_INNER = 16


def _host_consts():
    bf16 = ml_dtypes.bfloat16
    c = {}
    c["ident"] = np.eye(128, dtype=bf16)
    c["ident_f32"] = np.eye(128, dtype=np.float32)
    c["onesb_col"] = np.ones((128, 1), dtype=bf16)
    c["onesb_row"] = np.ones((1, 128), dtype=bf16)
    c["ones_col_f32"] = np.ones((1, 128), dtype=np.float32)
    c["ones_row_f32"] = np.ones((1, 512), dtype=np.float32)
    half = 64
    pos = np.arange(T, dtype=np.float64)[None, :]
    freqs = (ROPE_BASE ** (-np.arange(half, dtype=np.float64) / half))[:, None]
    rad = freqs * pos
    c["ropeC"] = np.ascontiguousarray(np.concatenate([np.cos(rad)] * 2, 0)).astype(bf16)
    c["ropeS"] = np.ascontiguousarray(
        np.concatenate([-np.sin(rad), np.sin(rad)], 0)).astype(bf16)
    m = np.zeros((4, 128, 512), dtype=bf16)
    for j in range(4):
        kk = j * 128 + np.arange(128)[:, None]
        qq = np.arange(512)[None, :]
        m[j] = (kk <= qq).astype(bf16)
    c["masks"] = m
    return c


def _fm_vec(v):
    return np.ascontiguousarray(np.asarray(v, np.float32).reshape(NC, 128).T)


def _packpair(w):
    w = np.asarray(w, np.float32)
    out = np.empty((128, 8), np.float32)
    for i in range(8):
        out[0:64, i] = w[:, 2 * i]
        out[64:128, i] = w[:, 2 * i + 1]
    return np.ascontiguousarray(out)


def _packpair(w):
    w = np.asarray(w, np.float32)
    out = np.empty((128, 8), np.float32)
    for i in range(8):
        out[0:64, i] = w[:, 2 * i]
        out[64:128, i] = w[:, 2 * i + 1]
    return np.ascontiguousarray(out)


class Sem:
    def __init__(self, stack, nc, name):
        self.h = stack.enter_context(nc.semaphore(name))
        self.n = 0

    def inc(self, instr, k=1):
        instr.then_inc(self.h, k)
        self.n += k


def build_nc(taps=()):
    nc = bass.Bass("TRN2", target_bir_lowering=False)
    dt = {}

    def din(name, shape, dtype=F32):
        dt[name] = nc.dram_tensor(name, list(shape), dtype, kind="ExternalInput").ap()
        return dt[name]

    x_in = din("x", [NB, T, D])
    out = nc.dram_tensor("out", [NB, T, D], F32, kind="ExternalOutput").ap()
    tap_out = {}
    for tn in taps:
        tap_out[tn] = nc.dram_tensor("tap_" + tn, [128, NC, NT], F32,
                                     kind="ExternalOutput").ap()

    din("l0_W", [D, D]); din("l0_b", [128, NC]); din("l0_gamma", [128, NC])
    din("l0_beta", [128, NC]); din("decay", [1, 1])
    din("gate_w", [D, MOE_E]); din("gate_b", [MOE_E, 1])
    din("sh_wg", [D, MOE_I]); din("sh_wu", [D, MOE_I]); din("sh_wd", [MOE_I, D])
    din("ex_wg", [MOE_E, D, MOE_I]); din("ex_wu", [MOE_E, D, MOE_I])
    din("ex_wd", [MOE_E, MOE_I, D])
    din("wq", [D, 8, 128]); din("wk", [D, 8, 128]); din("wv", [D, 8, 128])
    din("wo", [8, 128, D])
    din("Win2", [D, 128]); din("binP", [128, 1])
    din("W1P", [128, 8]); din("b1P", [128, 8])
    din("W2P", [128, 8]); din("b2", [L1_M, 1])
    din("Wout", [L1_M, D]); din("bout", [128, NC])
    din("ident", [128, 128], BF16); din("ident_f32", [128, 128])
    din("onesb_col", [128, 1], BF16)
    din("onesb_row", [1, 128], BF16); din("ones_col_f32", [1, 128])
    din("ones_row_f32", [1, 512])
    din("ropeC", [128, T], BF16); din("ropeS", [128, T], BF16)
    din("masks", [4, 128, 512], BF16)

    stack = ExitStack()
    with stack:
        dS = Sem(stack, nc, "dS")
        dG = Sem(stack, nc, "dG")
        sT = Sem(stack, nc, "sT")
        sV = Sem(stack, nc, "sV")
        sS = Sem(stack, nc, "sS")

        def sb(name, shape, dtype):
            return stack.enter_context(nc.sbuf_tensor(name, list(shape), dtype))

        X = sb("X", [128, NC, NT], BF16)
        ident = sb("c_ident", [128, 128], BF16)
        ident_f32 = sb("c_identf", [128, 128], F32)
        onesb_col = sb("c_obc", [128, 1], BF16)
        onesb_row = sb("c_obr", [1, 128], BF16)
        ones_col_f32 = sb("c_ocf", [1, 128], F32)
        ones_row_f32 = sb("c_orf", [1, 512], F32)
        l0_b = sb("c_l0b", [128, NC], F32)
        l0_gamma = sb("c_l0g", [128, NC], F32)
        l0_beta = sb("c_l0be", [128, NC], F32)
        bout_sb = sb("c_bout", [128, NC], F32)
        decay_sb = sb("c_decay", [1, 1], F32)
        rowA = sb("rowA", [1, 512], BF16)
        rowB = sb("rowB", [1, 512], BF16)
        Abuf = sb("Abuf", [128, T], BF16)
        Bbuf = sb("Bbuf", [128, T], BF16)
        statf = sb("statf", [1, 3, 512], F32)

        block = stack.enter_context(nc.Block())
        TQ, VQ, SQ, GQ, YQ = block.tensor, block.vector, block.scalar, block.gpsimd, block.sync

        def _(y):
            for sbuf_t, nm in [(ident, "ident"), (ident_f32, "ident_f32"),
                               (onesb_col, "onesb_col"),
                               (onesb_row, "onesb_row"), (ones_col_f32, "ones_col_f32"),
                               (ones_row_f32, "ones_row_f32"),
                               (l0_b, "l0_b"), (l0_gamma, "l0_gamma"),
                               (l0_beta, "l0_beta"), (bout_sb, "bout"),
                               (decay_sb, "decay")]:
                y.dma_start(out=sbuf_t[:], in_=dt[nm][:, :]).then_inc(dS.h, 16)
                dS.n += 16
        YQ(_)
        consts_ready = dS.n

        def ln_rowmath(n, psA_row, psB_row, t_mark):
            def _(v):
                v.wait_ge(sT.h, t_mark)
                mu = statf[:, 0, 0:n]; w = statf[:, 1, 0:n]; t2 = statf[:, 2, 0:n]
                v.tensor_scalar(mu, psA_row, 1.0 / D, None, ALU.mult)
                v.tensor_scalar(w, psB_row, 1.0 / D, None, ALU.mult)
                v.tensor_tensor(t2, mu, mu, ALU.mult)
                v.tensor_tensor(w, w, t2, ALU.subtract)
                v.tensor_scalar(w, w, 1.0, LN_EPS, ALU.mult, op1=ALU.add)
                ins = v.reciprocal(w, w)
                sV.inc(ins)
            VQ(_)
            m1 = sV.n

            def _(s):
                s.wait_ge(sV.h, m1)
                ins = s.activation(rowA[:, 0:n], statf[:, 1, 0:n], AF.Sqrt)
                sS.inc(ins)
            SQ(_)
            m2 = sS.n

            def _(v):
                v.wait_ge(sS.h, m2)
                v.tensor_tensor(statf[:, 2, 0:n], statf[:, 0, 0:n], rowA[:, 0:n], ALU.mult)
                ins = v.tensor_scalar(rowB[:, 0:n], statf[:, 2, 0:n], -1.0, None, ALU.mult)
                sV.inc(ins)
            VQ(_)
            return sV.n

        with ExitStack() as ph:
            psT = [ph.enter_context(nc.psum_tensor("psTa", [128, 1024], BF16)),
                   ph.enter_context(nc.psum_tensor("psTb", [128, 1024], BF16))]
            xstage = ph.enter_context(nc.sbuf_tensor("xstage", [128, 2, D], BF16))

            ntile = NB * (T // 128)
            ld_m = [0] * ntile; tr_m = [0] * ntile; ev_m = [0] * ntile
            for i in range(ntile):
                s_i, tt = divmod(i, T // 128)

                def _(g):
                    if i >= 2:
                        g.wait_ge(sT.h, tr_m[i - 2])
                    g.dma_start(out=xstage[:, i % 2, :],
                                in_=x_in[s_i, tt * 128:(tt + 1) * 128, :]).then_inc(dG.h, 16)
                    dG.n += 16
                    ld_m[i] = dG.n
                GQ(_)

                def _(t):
                    t.wait_ge(dG.h, ld_m[i])
                    if i == 0:
                        t.wait_ge(dS.h, consts_ready)
                    if i >= 2:
                        t.wait_ge(sV.h, ev_m[i - 2])
                    for c in range(NC):
                        ins = t.transpose(psT[i % 2][:, c * 128:(c + 1) * 128],
                                          xstage[:, i % 2, c * 128:(c + 1) * 128], ident[:])
                    sT.inc(ins)
                    tr_m[i] = sT.n
                TQ(_)

                def _(v):
                    v.wait_ge(sT.h, tr_m[i])
                    t0 = s_i * T + tt * 128
                    for c in range(NC):
                        ins = v.tensor_copy(X[:, c, t0:t0 + 128],
                                            psT[i % 2][:, c * 128:(c + 1) * 128])
                    sV.inc(ins)
                    ev_m[i] = sV.n
                VQ(_)
            in_T, in_V = sT.n, sV.n

        def tap(name, tV, tT, tS):
            if name not in tap_out:
                return
            def _(g):
                g.wait_ge(sV.h, tV); g.wait_ge(sT.h, tT); g.wait_ge(sS.h, tS)
                g.dma_start(out=tap_out[name][:, :, :], in_=X[:]).then_inc(dG.h, 16)
                dG.n += 16
            GQ(_)

        tap("xin", in_V, in_T, sS.n)

        with ExitStack() as ph:
            Wrow = ph.enter_context(nc.sbuf_tensor("b1Wrow", [128, NC, D], BF16))
            Wt = ph.enter_context(nc.sbuf_tensor("b1Wt", [128, NC, D], BF16))
            Sbuf = ph.enter_context(nc.sbuf_tensor("b1S", [128, NC, T], BF16))
            Ybuf = ph.enter_context(nc.sbuf_tensor("b1Y", [128, NC, T], BF16))
            a_tile = ph.enter_context(nc.sbuf_tensor("b1a", [128, T], F32))
            a_row = ph.enter_context(nc.sbuf_tensor("b1ar", [1, 512], F32))
            a_val = ph.enter_context(nc.sbuf_tensor("b1av", [1, 1], F32))
            y2t = ph.enter_context(nc.sbuf_tensor("b1y2", [128, 2, 512], BF16))
            psY = [ph.enter_context(nc.psum_tensor("b1psY0", [128, 512], F32)),
                   ph.enter_context(nc.psum_tensor("b1psY1", [128, 512], F32)),
                   None, None]
            psSA = ph.enter_context(nc.psum_tensor("b1psSA", [128, 512], F32))
            psSB = ph.enter_context(nc.psum_tensor("b1psSB", [128, 512], F32))
            psBa = ph.enter_context(nc.psum_tensor("b1psBa", [128, 512], F32))
            psBb = ph.enter_context(nc.psum_tensor("b1psBb", [128, 512], F32))
            psT = [ph.enter_context(nc.psum_tensor("b1psT0", [128, 1024], BF16)),
                   ph.enter_context(nc.psum_tensor("b1psT1", [128, 1024], BF16))]
            psY[2] = psBa
            psY[3] = psBb

            def _(g):
                g.wait_ge(sT.h, in_T)
                g.wait_ge(sV.h, in_V)
                g.dma_start(out=Wrow[:],
                            in_=dt["l0_W"].rearrange("(c p) d -> p c d", p=128)).then_inc(dG.h, 16)
                dG.n += 16
            GQ(_)
            wrow_ok = dG.n

            def _(s):
                s.wait_ge(dS.h, consts_ready)
                ins = s.activation(a_val[:], decay_sb[:], AF.Sigmoid)
                sS.inc(ins)
            SQ(_)
            sig_ok = sS.n

            def _(v):
                v.wait_ge(sS.h, sig_ok)
                ins = v.tensor_scalar(a_row[:], ones_row_f32[:], a_val[:], None, ALU.mult)
                sV.inc(ins)
            VQ(_)
            arow_okV = sV.n

            at_t = [0] * 4; at_v = [0] * 4
            for j in range(4):
                def _(t):
                    if j == 0:
                        t.wait_ge(sV.h, max(arow_okV, in_V))
                    if j >= 2:
                        t.wait_ge(sV.h, at_v[j - 2])
                    bank = psBa if j % 2 == 0 else psBb
                    ins = t.matmul(bank[:, :], ones_col_f32[:, :], a_row[:, :],
                                   start=True, stop=True)
                    sT.inc(ins)
                    at_t[j] = sT.n
                TQ(_)

                def _(v):
                    v.wait_ge(sT.h, at_t[j])
                    bank = psBa if j % 2 == 0 else psBb
                    ins = v.tensor_copy(a_tile[:, j * 512:(j + 1) * 512], bank[:, :])
                    sV.inc(ins)
                    at_v[j] = sV.n
                VQ(_)
            atile_ok = sV.n

            wt_t = [0] * 64; wt_v = [0] * 64
            for i in range(64):
                ce, cd = divmod(i, NC)

                def _(t):
                    if i == 0:
                        t.wait_ge(dG.h, wrow_ok)
                        t.wait_ge(sV.h, in_V)
                    if i >= 2:
                        t.wait_ge(sV.h, wt_v[i - 2])
                    ins = t.transpose(psT[i % 2][:, 0:128],
                                      Wrow[:, ce, cd * 128:(cd + 1) * 128], ident[:])
                    sT.inc(ins)
                    wt_t[i] = sT.n
                TQ(_)

                def _(v):
                    v.wait_ge(sT.h, wt_t[i])
                    ins = v.tensor_copy(Wt[:, cd, ce * 128:(ce + 1) * 128],
                                        psT[i % 2][:, 0:128])
                    sV.inc(ins)
                    wt_v[i] = sV.n
                VQ(_)
            wt_ok = sV.n

            prev_y_lastS = 0
            prev_y_lastV = 0
            prev_r2_T = 0
            for s_i in range(NB):
                tb = s_i * T

                def _(v):
                    if s_i > 0:
                        v.wait_ge(sT.h, prev_r2_T)
                    for c in range(NC):
                        ins = v.tensor_tensor_scan(Sbuf[:, c, :], a_tile[:],
                                                   X[:, c, tb:tb + T], 0.0,
                                                   ALU.mult, ALU.add)
                    sV.inc(ins)
                VQ(_)
                scan_ok = sV.n

                ymm = [0] * 32; yev = [0] * 32; yevV = [0] * 32; yst_t = [0] * 32
                ystat_v = [0] * 4; yab_v = [0] * 4; ybc_t = [0] * 4

                def emit_ymm(i):
                    g, eg = divmod(i, NC)
                    def _(t):
                        if i == 0:
                            t.wait_ge(sV.h, max(scan_ok, wt_ok))
                            if s_i > 0:
                                t.wait_ge(sS.h, prev_y_lastS)
                        if i >= 4:
                            t.wait_ge(sS.h, yev[i - 4])
                        if i == 0 and s_i == 0:
                            t.wait_ge(sV.h, atile_ok)
                        for k in range(NC):
                            ins = t.matmul(psY[i % 4][:, :], Wt[:, k, eg * 128:(eg + 1) * 128],
                                           Sbuf[:, k, g * 512:(g + 1) * 512],
                                           start=(k == 0), stop=(k == NC - 1))
                        sT.inc(ins)
                        ymm[i] = sT.n
                    TQ(_)

                def emit_yev(i):
                    g, eg = divmod(i, NC)
                    def _(s):
                        s.wait_ge(sT.h, ymm[i])
                        if i >= 2:
                            s.wait_ge(sT.h, yst_t[i - 2])
                        ins = s.activation(Ybuf[:, eg, g * 512:(g + 1) * 512],
                                           psY[i % 4][:, :], AF.Identity,
                                           bias=l0_b[:, eg:eg + 1])
                        ins = s.activation(y2t[:, i % 2, :], psY[i % 4][:, :],
                                           AF.Square, bias=l0_b[:, eg:eg + 1])
                        sS.inc(ins)
                        yev[i] = sS.n
                        yevV[i] = 0
                    SQ(_)

                def emit_yst(i):
                    g, eg = divmod(i, NC)
                    def _(t):
                        t.wait_ge(sS.h, yev[i])
                        if eg == 0 and g > 0:
                            t.wait_ge(sV.h, yab_v[g - 1])
                        ins = t.matmul(psSA[0:1, :], onesb_col[:, :],
                                       Ybuf[:, eg, g * 512:(g + 1) * 512],
                                       start=(eg == 0), stop=(eg == NC - 1))
                        ins = t.matmul(psSB[0:1, :], onesb_col[:, :], y2t[:, i % 2, :],
                                       start=(eg == 0), stop=(eg == NC - 1))
                        sT.inc(ins)
                        yst_t[i] = sT.n
                    TQ(_)

                def emit_ygrp(g):
                    rows_ok = ln_rowmath(512, psSA[0:1, :], psSB[0:1, :], yst_t[g * 8 + 7])
                    ystat_v[g] = rows_ok

                    def _(t):
                        t.wait_ge(sV.h, rows_ok)
                        ins = t.matmul(psSA[:, :], onesb_row[:, :], rowA[:, :],
                                       start=True, stop=True)
                        ins = t.matmul(psSB[:, :], onesb_row[:, :], rowB[:, :],
                                       start=True, stop=True)
                        sT.inc(ins)
                        ybc_t[g] = sT.n
                    TQ(_)

                    def _(v):
                        v.wait_ge(sT.h, ybc_t[g])
                        v.tensor_copy(Abuf[:, g * 512:(g + 1) * 512], psSA[:, :])
                        ins = v.tensor_copy(Bbuf[:, g * 512:(g + 1) * 512], psSB[:, :])
                        sV.inc(ins)
                        yab_v[g] = sV.n
                    VQ(_)

                for i in range(32):
                    emit_ymm(i)
                    emit_yev(i)
                    if i >= 1:
                        emit_yst(i - 1)
                        if (i - 1) % 8 == 7:
                            emit_ygrp((i - 1) // 8)
                emit_yst(31)
                emit_ygrp(3)
                prev_y_lastS = yev[31]
                prev_y_lastV = yevV[31]

                def _(v):
                    v.wait_ge(sS.h, yev[31])
                    for eg in range(NC):
                        yv = Ybuf[:, eg, :]
                        v.tensor_tensor(yv, yv, Abuf[:, 0:T], ALU.mult)
                        v.tensor_tensor(yv, yv, Bbuf[:, 0:T], ALU.add)
                        v.tensor_scalar(yv, yv, l0_gamma[:, eg:eg + 1],
                                        l0_beta[:, eg:eg + 1], ALU.mult, op1=ALU.add)
                        ins = v.tensor_tensor(X[:, eg, tb:tb + T], X[:, eg, tb:tb + T],
                                              yv, ALU.add)
                    sV.inc(ins)
                VQ(_)
                r_ok = sV.n

                def _(s):
                    s.wait_ge(sV.h, r_ok)
                    for c in range(NC):
                        ins = s.activation(Sbuf[:, c, :], X[:, c, tb:tb + T], AF.Square)
                    sS.inc(ins)
                SQ(_)
                r2_ok = sS.n

                ost_t = [0] * 4; ostat_v = [0] * 4; obc_t = [0] * 4; oab_v = [0] * 4
                for g in range(4):
                    def _(t):
                        t.wait_ge(sS.h, r2_ok)
                        if g > 0:
                            t.wait_ge(sV.h, ostat_v[g - 1])
                        else:
                            t.wait_ge(sV.h, yab_v[3])
                        for c in range(NC):
                            ins = t.matmul(psSA[0:1, :], onesb_col[:, :],
                                           X[:, c, tb + g * 512:tb + (g + 1) * 512],
                                           start=(c == 0), stop=(c == NC - 1))
                            ins = t.matmul(psSB[0:1, :], onesb_col[:, :],
                                           Sbuf[:, c, g * 512:(g + 1) * 512],
                                           start=(c == 0), stop=(c == NC - 1))
                        sT.inc(ins)
                        ost_t[g] = sT.n
                    TQ(_)
                    rows_ok = ln_rowmath(512, psSA[0:1, :], psSB[0:1, :], ost_t[g])
                    ostat_v[g] = rows_ok

                    def _(t):
                        t.wait_ge(sV.h, rows_ok)
                        if g >= 2:
                            t.wait_ge(sV.h, oab_v[g - 2])
                        ins = t.matmul(psBa[:, :], onesb_row[:, :], rowA[:, :],
                                       start=True, stop=True)
                        ins = t.matmul(psBb[:, :], onesb_row[:, :], rowB[:, :],
                                       start=True, stop=True)
                        sT.inc(ins)
                        obc_t[g] = sT.n
                    TQ(_)

                    def _(v):
                        v.wait_ge(sT.h, obc_t[g])
                        v.tensor_copy(Abuf[:, g * 512:(g + 1) * 512], psBa[:, :])
                        ins = v.tensor_copy(Bbuf[:, g * 512:(g + 1) * 512], psBb[:, :])
                        sV.inc(ins)
                        oab_v[g] = sV.n
                    VQ(_)

                def _(v):
                    for c in range(NC):
                        xv = X[:, c, tb:tb + T]
                        v.tensor_tensor(xv, xv, Abuf[:, 0:T], ALU.mult)
                        ins = v.tensor_tensor(xv, xv, Bbuf[:, 0:T], ALU.add)
                    sV.inc(ins)
                VQ(_)
                prev_r2_T = ost_t[3]

            b1_V, b1_T, b1_S = sV.n, sT.n, sS.n
        tap("b1", b1_V, b1_T, b1_S)

        with ExitStack() as ph:
            wgT = ph.enter_context(nc.sbuf_tensor("wgT", [128, 2, NC, MOE_I], BF16))
            wuT = ph.enter_context(nc.sbuf_tensor("wuT", [128, 2, NC, MOE_I], BF16))
            wdT = ph.enter_context(nc.sbuf_tensor("wdT", [128, 2, 4, D], BF16))
            gateT = ph.enter_context(nc.sbuf_tensor("gateT", [128, NC, MOE_E], BF16))
            gatebT = ph.enter_context(nc.sbuf_tensor("gatebT", [MOE_E, 1], F32))
            Rsb = ph.enter_context(nc.sbuf_tensor("Rsb", [MOE_E, T], F32))
            Rt = ph.enter_context(nc.sbuf_tensor("Rt", [128, 16, MOE_E], F32))
            ctok = ph.enter_context(nc.sbuf_tensor("ctok", [128, 2, 128], F32))
            cF = ph.enter_context(nc.sbuf_tensor("cF", [MOE_E, T], BF16))
            cRow = ph.enter_context(nc.sbuf_tensor("cRow", [1, 2, T], BF16))
            cb = ph.enter_context(nc.sbuf_tensor("cb", [128, T], BF16))
            accb = ph.enter_context(nc.sbuf_tensor("accb", [128, NC, T], BF16))
            slb = ph.enter_context(nc.sbuf_tensor("slb", [128, 2, 4, 256], BF16))
            hpb = ph.enter_context(nc.sbuf_tensor("hpb", [128, 2, 4, 256], BF16))
            tv1 = ph.enter_context(nc.sbuf_tensor("tv1", [128, 4], F32))
            tv2 = ph.enter_context(nc.sbuf_tensor("tv2", [128, MOE_E], F32))
            tv3 = ph.enter_context(nc.sbuf_tensor("tv3", [128, MOE_E], F32))
            psHG = ph.enter_context(nc.psum_tensor("psHG", [128, 4, 256], F32))
            psHU = ph.enter_context(nc.psum_tensor("psHU", [128, 4, 256], F32))
            psACC = ph.enter_context(nc.psum_tensor("psACC", [128, NC, 256], F32))

            def _(g):
                g.wait_ge(sV.h, b1_V); g.wait_ge(sT.h, b1_T); g.wait_ge(sS.h, b1_S)
                g.dma_start(out=gateT[:],
                            in_=dt["gate_w"].rearrange("(c p) e -> p c e", p=128)).then_inc(dG.h, 16)
                dG.n += 16
            GQ(_)
            gate_okG = dG.n

            def _(y):
                y.wait_ge(sV.h, b1_V); y.wait_ge(sT.h, b1_T); y.wait_ge(sS.h, b1_S)
                y.dma_start(out=gatebT[:], in_=dt["gate_b"][:, :]).then_inc(dS.h, 16)
                dS.n += 16
            YQ(_)
            gate_okS = dS.n

            prev_psacc_free_V = 0
            prev_unit16_T = 0
            prev_epB2_V = 0
            for h_i in range(NB):
                hb = h_i * T
                wload = {}
                hmm = {}; silu_m = {}; h_m = {}; dmm = {}; accev = {}
                crow_dma = {}; crow_t = {}; cb_ok = {}
                ep_r = {}; ep_r2 = {}; ep_st = {}; ep_bc = {}; ep_ab = {}

                def load_unit(u):
                    def _(g):
                        slot = u % 2
                        if u >= 2:
                            g.wait_ge(sT.h, dmm[(u - 2, 7)])
                        elif h_i > 0:
                            g.wait_ge(sT.h, prev_unit16_T)
                        else:
                            g.wait_ge(sV.h, b1_V); g.wait_ge(sT.h, b1_T)
                            g.wait_ge(sS.h, b1_S)
                        if u == 0:
                            srcs = [dt["sh_wg"], dt["sh_wu"], dt["sh_wd"]]
                        else:
                            e = u - 1
                            srcs = [dt["ex_wg"][e], dt["ex_wu"][e], dt["ex_wd"][e]]
                        g.dma_start(out=wgT[:, slot, :, :],
                                    in_=srcs[0].rearrange("(c p) i -> p c i", p=128)).then_inc(dG.h, 16)
                        g.dma_start(out=wuT[:, slot, :, :],
                                    in_=srcs[1].rearrange("(c p) i -> p c i", p=128)).then_inc(dG.h, 16)
                        g.dma_start(out=wdT[:, slot, :, :],
                                    in_=srcs[2].rearrange("(c p) i -> p c i", p=128)).then_inc(dG.h, 16)
                        dG.n += 48
                        wload[u] = dG.n
                    GQ(_)

                load_unit(0)
                load_unit(1)

                rt_t = [0] * 4; rsig = [0] * 4
                for g4 in range(4):
                    def _(t):
                        if g4 == 0:
                            t.wait_ge(dG.h, gate_okG)
                            t.wait_ge(sV.h, max(b1_V, prev_psacc_free_V))
                        else:
                            t.wait_ge(sS.h, rsig[g4 - 1])
                        for k in range(NC):
                            ins = t.matmul(psACC[0:16, 0:2, :], gateT[:, k, :],
                                           X[:, k, hb + g4 * 512:hb + (g4 + 1) * 512],
                                           start=(k == 0), stop=(k == NC - 1))
                        sT.inc(ins)
                        rt_t[g4] = sT.n
                    TQ(_)

                    def _(s):
                        s.wait_ge(sT.h, rt_t[g4])
                        if g4 == 0:
                            s.wait_ge(dS.h, gate_okS)
                        ins = s.activation(Rsb[:, g4 * 512:(g4 + 1) * 512],
                                           psACC[0:16, 0:2, :], AF.Sigmoid,
                                           bias=gatebT[:, :])
                        sS.inc(ins)
                        rsig[g4] = sS.n
                    SQ(_)

                rtr_t = [0] * 16; rte_v = [0] * 16
                for i in range(16):
                    def _(t):
                        if i == 0:
                            t.wait_ge(sS.h, rsig[3])
                            t.wait_ge(sV.h, prev_epB2_V)
                        if i >= 2:
                            t.wait_ge(sV.h, rte_v[i - 2])
                        reg = psHG[:, 0, 0:16] if i % 2 == 0 else psHG[:, 2, 0:16]
                        ins = t.transpose(reg, Rsb[:, i * 128:(i + 1) * 128], ident_f32[0:16, 0:16])
                        sT.inc(ins)
                        rtr_t[i] = sT.n
                    TQ(_)

                    def _(v):
                        v.wait_ge(sT.h, rtr_t[i])
                        reg = psHG[:, 0, 0:16] if i % 2 == 0 else psHG[:, 2, 0:16]
                        ins = v.tensor_copy(Rt[:, i, :], reg)
                        sV.inc(ins)
                        rte_v[i] = sV.n
                    VQ(_)

                top_v = [0] * 16; ctr_t = [0] * 16; cte_v = [0] * 16
                def _(v):
                    v.memset(ctok[:, 0, :], 0.0)
                    ins = v.memset(ctok[:, 1, :], 0.0)
                    sV.inc(ins)
                VQ(_)
                for i in range(16):
                    def _(v):
                        v.wait_ge(sV.h, rte_v[i])
                        if i >= 2:
                            v.wait_ge(sT.h, ctr_t[i - 2])
                        r = Rt[:, i, :]
                        m1 = tv1[:, 0:1]; m2 = tv1[:, 1:2]
                        den = tv1[:, 2:3]; rec = tv1[:, 3:4]
                        def bump(ins):
                            sV.inc(ins); v.wait_ge(sV.h, sV.n)
                        bump(v.reduce_max(m1, r, AX.X))
                        bump(v.tensor_scalar(tv2[:, :], r, m1, None, ALU.is_ge))
                        bump(v.scalar_tensor_tensor(tv3[:, :], tv2[:, :], -1e30, r,
                                                    ALU.mult, ALU.add))
                        bump(v.reduce_max(m2, tv3[:, :], AX.X))
                        bump(v.tensor_tensor(den, m1, m2, ALU.add))
                        bump(v.reciprocal(rec, den))
                        bump(v.tensor_scalar(tv2[:, :], r, m2, None, ALU.is_ge))
                        ins = v.scalar_tensor_tensor(ctok[:, i % 2, 0:16], tv2[:, :], rec,
                                                     r, ALU.mult, ALU.mult)
                        sV.inc(ins)
                        top_v[i] = sV.n
                    VQ(_)

                    def _(t):
                        t.wait_ge(sV.h, top_v[i])
                        if i >= 2:
                            t.wait_ge(sV.h, cte_v[i - 2])
                        reg = psHG[:, 0, 0:128] if i % 2 == 0 else psHG[:, 2, 0:128]
                        ins = t.transpose(reg, ctok[:, i % 2, :], ident_f32[:])
                        sT.inc(ins)
                        ctr_t[i] = sT.n
                    TQ(_)

                    def _(v):
                        v.wait_ge(sT.h, ctr_t[i])
                        reg = psHG[0:16, 0, 0:128] if i % 2 == 0 else psHG[0:16, 2, 0:128]
                        ins = v.tensor_copy(cF[:, i * 128:(i + 1) * 128], reg)
                        sV.inc(ins)
                        cte_v[i] = sV.n
                    VQ(_)
                cf_ok = sV.n

                def crow_dma_issue(uu):
                    def _(y):
                        if uu == 1:
                            y.wait_ge(sV.h, cf_ok)
                        if uu >= 3:
                            y.wait_ge(sT.h, crow_t[uu - 2])
                        y.dma_start(out=cRow[0:1, uu % 2, :],
                                    in_=cF[uu - 1:uu, :]).then_inc(dS.h, 16)
                        dS.n += 16
                        crow_dma[uu] = dS.n
                    YQ(_)

                crow_dma_issue(1)
                for u in range(17):
                    slot = u % 2
                    if u + 1 <= 16:
                        load_unit(u + 1)
                    if 2 <= u + 1 <= 16:
                        crow_dma_issue(u + 1)

                    if u >= 1:
                        e = u - 1
                        cbm = {}

                        def _(t):
                            t.wait_ge(dS.h, crow_dma[u])
                            if (u - 1, 7) in h_m:
                                t.wait_ge(sV.h, h_m[(u - 1, 7)])
                            cr = cRow[0:1, u % 2, :]
                            t.matmul(psHG[:, 0:2, :], onesb_row[:, :],
                                     cr[:, 0:512], start=True, stop=True)
                            t.matmul(psHG[:, 2:4, :], onesb_row[:, :],
                                     cr[:, 512:1024], start=True, stop=True)
                            t.matmul(psHU[:, 0:2, :], onesb_row[:, :],
                                     cr[:, 1024:1536], start=True, stop=True)
                            ins = t.matmul(psHU[:, 2:4, :], onesb_row[:, :],
                                           cr[:, 1536:2048], start=True, stop=True)
                            sT.inc(ins)
                            cbm["t1"] = sT.n
                            crow_t[u] = sT.n
                        TQ(_)

                        def _(v):
                            v.wait_ge(sT.h, cbm["t1"])
                            v.tensor_copy(cb[:, 0:1024],
                                          psHG[:, :, :].rearrange("p a b -> p (a b)"))
                            ins = v.tensor_copy(cb[:, 1024:2048],
                                                psHU[:, :, :].rearrange("p a b -> p (a b)"))
                            sV.inc(ins)
                            cbm["v1"] = sV.n
                        VQ(_)
                        cb_ok[u] = cbm["v1"]

                    def emit_down(u, g):
                        def _(t):
                            t.wait_ge(sV.h, h_m[(u, g)])
                            key = (u, g - 1) if g > 0 else (u - 1, 7)
                            if key in accev:
                                t.wait_ge(sV.h, accev[key])
                            if u == 0 and g == 0:
                                t.wait_ge(sS.h, rsig[3])
                            if u > 0:
                                for b in range(4):
                                    t.matmul(psACC[:, 2 * b:2 * b + 2, :], ident[:],
                                             accb[:, 2 * b:2 * b + 2, g * 256:(g + 1) * 256],
                                             start=True, stop=False)
                            for dc in range(NC):
                                for ic in range(4):
                                    ins = t.matmul(psACC[:, dc, :],
                                                   wdT[:, u % 2, ic, dc * 128:(dc + 1) * 128],
                                                   hpb[:, g % 2, ic, :],
                                                   start=(u == 0 and ic == 0 and dc % 2 == 0),
                                                   stop=(ic == 3 and dc % 2 == 1))
                            sT.inc(ins)
                            dmm[(u, g)] = sT.n
                        TQ(_)

                        def _(v):
                            v.wait_ge(sT.h, dmm[(u, g)])
                            ins = v.tensor_copy(accb[:, :, g * 256:(g + 1) * 256],
                                                psACC[:, :, :])
                            sV.inc(ins)
                            accev[(u, g)] = sV.n
                        VQ(_)

                    for g in range(8):
                        gb = hb + g * 256

                        def _(t):
                            if g == 0:
                                t.wait_ge(dG.h, wload[u])
                                if u == 0:
                                    t.wait_ge(sV.h, cf_ok)
                                else:
                                    t.wait_ge(sV.h, cb_ok[u])
                            key = (u, g - 1) if g > 0 else (u - 1, 7)
                            if key in h_m:
                                t.wait_ge(sV.h, h_m[key])
                            for ic in range(4):
                                for k in range(NC):
                                    t.matmul(psHG[:, ic, :],
                                             wgT[:, slot, k, ic * 128:(ic + 1) * 128],
                                             X[:, k, gb:gb + 256],
                                             start=(k == 0), stop=(k == NC - 1))
                            for ic in range(4):
                                for k in range(NC):
                                    ins = t.matmul(psHU[:, ic, :],
                                                   wuT[:, slot, k, ic * 128:(ic + 1) * 128],
                                                   X[:, k, gb:gb + 256],
                                                   start=(k == 0), stop=(k == NC - 1))
                            sT.inc(ins)
                            hmm[(u, g)] = sT.n
                        TQ(_)

                        def _(s):
                            s.wait_ge(sT.h, hmm[(u, g)])
                            ins = s.activation(slb[:, g % 2, :, :], psHG[:, :, :], AF.Silu)
                            sS.inc(ins)
                            silu_m[(u, g)] = sS.n
                        SQ(_)

                        def _(v):
                            v.wait_ge(sS.h, silu_m[(u, g)])
                            hp = hpb[:, g % 2, :, :]
                            if u >= 1:
                                v.tensor_tensor(hp, psHU[:, :, :],
                                                cb[:, g * 256:g * 256 + 256][:, None, :]
                                                .to_broadcast((128, 4, 256)), ALU.mult)
                                ins = v.tensor_tensor(hp, slb[:, g % 2, :, :], hp, ALU.mult)
                            else:
                                ins = v.tensor_tensor(hp, slb[:, g % 2, :, :],
                                                      psHU[:, :, :], ALU.mult)
                            sV.inc(ins)
                            h_m[(u, g)] = sV.n
                        VQ(_)

                        if g >= 1:
                            emit_down(u, g - 1)
                    emit_down(u, 7)

                slbF = slb[:, :, :, :].rearrange("p a b c -> p (a b c)")
                hpbF = hpb[:, :, :, :].rearrange("p a b c -> p (a b c)")
                def r2reg(dc):
                    return (slbF[:, dc * 512:(dc + 1) * 512] if dc < 4
                            else hpbF[:, (dc - 4) * 512:(dc - 3) * 512])
                for gg in range(4):
                    gb = hb + gg * 512

                    def _(v):
                        v.wait_ge(sV.h, accev[(16, 2 * gg + 1)])
                        for dc in range(NC):
                            ins = v.scalar_tensor_tensor(
                                X[:, dc, gb:gb + 512], X[:, dc, gb:gb + 512],
                                2.0, accb[:, dc, gg * 512:(gg + 1) * 512],
                                ALU.mult, ALU.add)
                        sV.inc(ins)
                        ep_r[gg] = sV.n
                    VQ(_)

                    def _(s):
                        s.wait_ge(sV.h, ep_r[gg])
                        s.wait_ge(sT.h, dmm[(16, 7)])
                        for dc in range(NC):
                            ins = s.activation(r2reg(dc),
                                               X[:, dc, gb:gb + 512], AF.Square)
                        sS.inc(ins)
                        ep_r2[gg] = sS.n
                    SQ(_)

                    if gg % 2 == 0:
                        stA = psHG[0:1, 0:2, :]; stB = psHU[0:1, 0:2, :]
                        bcA = psHG[:, 0:2, :]; bcB = psHG[:, 2:4, :]
                    else:
                        stA = psACC[0:1, 0:2, :]; stB = psACC[0:1, 2:4, :]
                        bcA = psACC[:, 0:2, :]; bcB = psACC[:, 2:4, :]

                    def _(t, stA=stA, stB=stB):
                        t.wait_ge(sS.h, ep_r2[gg])
                        if gg == 0:
                            t.wait_ge(sV.h, h_m[(16, 7)])
                        elif gg == 1:
                            t.wait_ge(sV.h, accev[(16, 7)])
                        else:
                            t.wait_ge(sV.h, ep_ab[gg - 2])
                        for c in range(NC):
                            t.matmul(stA, onesb_col[:, :], X[:, c, gb:gb + 512],
                                     start=(c == 0), stop=(c == NC - 1))
                            ins = t.matmul(stB, onesb_col[:, :], r2reg(c),
                                           start=(c == 0), stop=(c == NC - 1))
                        sT.inc(ins)
                        ep_st[gg] = sT.n
                    TQ(_)

                    rows_ok = ln_rowmath(512, stA, stB, ep_st[gg])

                    def _(t, bcA=bcA, bcB=bcB):
                        t.wait_ge(sV.h, rows_ok)
                        ins = t.matmul(bcA, onesb_row[:, :], rowA[:, 0:512],
                                       start=True, stop=True)
                        ins = t.matmul(bcB, onesb_row[:, :], rowB[:, 0:512],
                                       start=True, stop=True)
                        sT.inc(ins)
                        ep_bc[gg] = sT.n
                    TQ(_)

                    def _(v, bcA=bcA, bcB=bcB):
                        v.wait_ge(sT.h, ep_bc[gg])
                        ab = Abuf[:, (gg % 2) * 512:(gg % 2) * 512 + 512]
                        bb = Bbuf[:, (gg % 2) * 512:(gg % 2) * 512 + 512]
                        v.tensor_copy(ab, bcA)
                        v.tensor_copy(bb, bcB)
                        for c in range(NC):
                            xv = X[:, c, gb:gb + 512]
                            v.tensor_tensor(xv, xv, ab, ALU.mult)
                            ins = v.tensor_tensor(xv, xv, bb, ALU.add)
                        sV.inc(ins)
                        ep_ab[gg] = sV.n
                    VQ(_)

                prev_psacc_free_V = ep_ab[3]
                prev_unit16_T = dmm[(16, 7)]
                prev_epB2_V = ep_ab[3]

            b2_V, b2_T, b2_S = sV.n, sT.n, sS.n
        tap("b2", b2_V, b2_T, b2_S)

        with ExitStack() as ph:
            ropeC = ph.enter_context(nc.sbuf_tensor("c_ropeC", [128, T], BF16))
            ropeS = ph.enter_context(nc.sbuf_tensor("c_ropeS", [128, T], BF16))
            masks = ph.enter_context(nc.sbuf_tensor("c_masks", [128, 4, 512], BF16))
            woAll = ph.enter_context(nc.sbuf_tensor("woAll", [128, 8, D], BF16))
            wqh = ph.enter_context(nc.sbuf_tensor("wqh", [128, 2, NC, 128], BF16))
            wkh = ph.enter_context(nc.sbuf_tensor("wkh", [128, 2, NC, 128], BF16))
            wvh = ph.enter_context(nc.sbuf_tensor("wvh", [128, 2, NC, 128], BF16))
            qbuf = ph.enter_context(nc.sbuf_tensor("qbuf", [128, T], BF16))
            kbuf = ph.enter_context(nc.sbuf_tensor("kbuf", [128, T], BF16))
            vbuf = ph.enter_context(nc.sbuf_tensor("vbuf", [128, 16, 128], BF16))
            ptile = ph.enter_context(nc.sbuf_tensor("ptile", [128, 4, 512], BF16))
            q2t = ph.enter_context(nc.sbuf_tensor("q2t", [128, 4, 512], BF16))
            qraw = ph.enter_context(nc.sbuf_tensor("qraw", [128, T], BF16))
            k2t = ph.enter_context(nc.sbuf_tensor("k2t", [128, 2, 512], BF16))
            o_all = ph.enter_context(nc.sbuf_tensor("o_all", [128, 8, T], BF16))
            rsk = ph.enter_context(nc.sbuf_tensor("rsk", [128, 16], F32))
            rskt = ph.enter_context(nc.sbuf_tensor("rskt", [128, 16], F32))
            rs_qb = ph.enter_context(nc.sbuf_tensor("rs_qb", [128, 512], BF16))
            recipb = ph.enter_context(nc.sbuf_tensor("recipb", [128, 512], BF16))
            rowQ = ph.enter_context(nc.sbuf_tensor("rowQ", [1, 512], BF16))
            qtmp = ph.enter_context(nc.sbuf_tensor("qtmp", [128, 512], BF16))
            krot = ph.enter_context(nc.sbuf_tensor("krot", [128, 512], BF16))
            ktc = ph.enter_context(nc.sbuf_tensor("ktc", [128, 512], BF16))
            r2b3 = ph.enter_context(nc.sbuf_tensor("r2b3", [128, NC, 512], BF16))
            psQK = ph.enter_context(nc.psum_tensor("psQK", [128, 2, 512], F32))
            psSc = ph.enter_context(nc.psum_tensor("psSc", [128, 2, 512], F32))
            psO = ph.enter_context(nc.psum_tensor("psO", [128, 2, 512], F32))
            psMix = ph.enter_context(nc.psum_tensor("psMix", [128, 512], F32))
            psDen2 = ph.enter_context(nc.psum_tensor("psDen2", [128, 512], F32))

            def _(g):
                g.wait_ge(sV.h, b2_V); g.wait_ge(sT.h, b2_T); g.wait_ge(sS.h, b2_S)
                for h8 in range(8):
                    g.dma_start(out=woAll[:, h8, :], in_=dt["wo"][h8]).then_inc(dG.h, 16)
                dG.n += 8 * 16
            GQ(_)
            wo_ok = dG.n

            def _(y):
                y.wait_ge(sV.h, b2_V); y.wait_ge(sT.h, b2_T); y.wait_ge(sS.h, b2_S)
                y.dma_start(out=ropeC[:], in_=dt["ropeC"][:, :]).then_inc(dS.h, 16)
                y.dma_start(out=ropeS[:], in_=dt["ropeS"][:, :]).then_inc(dS.h, 16)
                y.dma_start(out=masks[:],
                            in_=dt["masks"].rearrange("m p n -> p m n")).then_inc(dS.h, 16)
                dS.n += 48
            YQ(_)
            rope_okS = dS.n

            qkv_done = {}
            wh_ok = {}

            def load_head(H):
                s_i, h8 = divmod(H, 8)
                def _(g):
                    if H >= 2:
                        g.wait_ge(sT.h, qkv_done[H - 2])
                    for dst, nm in [(wqh, "wq"), (wkh, "wk"), (wvh, "wv")]:
                        g.dma_start(out=dst[:, H % 2, :, :],
                                    in_=dt[nm][:, h8, :].rearrange("(c p) d -> p c d", p=128)
                                    ).then_inc(dG.h, 16)
                    dG.n += 48
                    wh_ok[H] = dG.n
                GQ(_)

            load_head(0)
            load_head(1)

            prev_exp_S = 0
            prev_onorm_last = 0
            prev_head_free = 0

            for s_i in range(NB):
                tb = s_i * T
                for h8 in range(8):
                    H = s_i * 8 + h8
                    slot = H % 2
                    if H + 1 < 16:
                        load_head(H + 1)

                    units = [("k", j) for j in range(4)] + [("q", j) for j in range(4)] \
                          + [("v", j) for j in range(16)]
                    cons_v = {}
                    mm_t = {}
                    ksq_s = {}; msk_t = {}
                    qsq_s = {}; msq_t = {}; qst_v = {}; qsq2_s = {}; qbc_t = {}
                    qev_v = {}
                    rsk_vv = {}

                    def emit_qchain_a(jq):
                        def _(t):
                            t.wait_ge(sS.h, qsq_s[jq])
                            if jq == 0:
                                t.wait_ge(sV.h, rsk_vv["v"])
                            else:
                                t.wait_ge(sV.h, qev_v[jq - 1])
                            ins = t.matmul(psMix[0:1, :], onesb_col[:, :],
                                           q2t[:, jq, :], start=True, stop=True)
                            sT.inc(ins)
                            msq_t[jq] = sT.n
                        TQ(_)

                        def _(v):
                            v.wait_ge(sT.h, msq_t[jq])
                            ins = v.tensor_scalar(statf[:, 0, :], psMix[0:1, :],
                                                  1.0 / 128.0, RMS_EPS, ALU.mult, op1=ALU.add)
                            sV.inc(ins); v.wait_ge(sV.h, sV.n)
                            ins = v.reciprocal(statf[:, 1, :], statf[:, 0, :])
                            sV.inc(ins)
                            qst_v[jq] = sV.n
                        VQ(_)

                        def _(s):
                            s.wait_ge(sV.h, qst_v[jq])
                            ins = s.activation(rowQ[:, :], statf[:, 1, :], AF.Sqrt)
                            sS.inc(ins)
                            qsq2_s[jq] = sS.n
                        SQ(_)

                    def emit_qchain_b(jq):
                        def _(t):
                            t.wait_ge(sS.h, qsq2_s[jq])
                            ins = t.matmul(psMix[:, :], onesb_row[:, :], rowQ[:, :],
                                           start=True, stop=True)
                            sT.inc(ins)
                            qbc_t[jq] = sT.n
                        TQ(_)

                        def _(v):
                            v.wait_ge(sT.h, qbc_t[jq])
                            v.tensor_copy(rs_qb[:, :], psMix[:, :])
                            v.tensor_tensor(qtmp[:, :], qraw[:, jq * 512:(jq + 1) * 512],
                                            rs_qb[:, :], ALU.mult)
                            v.tensor_tensor(ktc[:, :], qtmp[:, :],
                                            ropeC[:, jq * 512:(jq + 1) * 512], ALU.mult)
                            v.tensor_copy(krot[0:64, :], qtmp[64:128, :])
                            v.tensor_copy(krot[64:128, :], qtmp[0:64, :])
                            v.tensor_tensor(krot[:, :], krot[:, :],
                                            ropeS[:, jq * 512:(jq + 1) * 512], ALU.mult)
                            ins = v.tensor_tensor(qbuf[:, jq * 512:(jq + 1) * 512],
                                                  ktc[:, :], krot[:, :], ALU.add)
                            sV.inc(ins)
                            qev_v[jq] = sV.n
                        VQ(_)

                    for ui, (kind, j) in enumerate(units):
                        def _(t):
                            if ui == 0:
                                t.wait_ge(dG.h, wh_ok[H])
                                if H == 0:
                                    t.wait_ge(sV.h, b2_V)
                                    t.wait_ge(sS.h, b2_S)
                                    t.wait_ge(dS.h, rope_okS)
                                else:
                                    t.wait_ge(sV.h, prev_head_free)
                            if ui >= 2:
                                for sem, mk in cons_v[ui - 2]:
                                    t.wait_ge(sem.h, mk)
                            w = {"q": wqh, "k": wkh, "v": wvh}[kind]
                            if kind in ("q", "k"):
                                for k in range(NC):
                                    ins = t.matmul(psQK[:, ui % 2, :], w[:, slot, k, :],
                                                   X[:, k, tb + j * 512:tb + (j + 1) * 512],
                                                   start=(k == 0), stop=(k == NC - 1))
                            else:
                                for k in range(NC):
                                    ins = t.matmul(psQK[:, ui % 2, 0:128],
                                                   X[:, k, tb + j * 128:tb + (j + 1) * 128],
                                                   w[:, slot, k, :],
                                                   start=(k == 0), stop=(k == NC - 1))
                            sT.inc(ins)
                            mm_t[ui] = sT.n
                        TQ(_)

                        if kind == "v":
                            def _(v):
                                v.wait_ge(sT.h, mm_t[ui])
                                ins = v.tensor_copy(vbuf[:, j, :], psQK[:, ui % 2, 0:128])
                                sV.inc(ins)
                                cons_v[ui] = [(sV, sV.n)]
                            VQ(_)
                            if j == 0:
                                emit_qchain_a(0)
                            elif j % 4 == 2 and j // 4 < 4:
                                emit_qchain_b(j // 4)
                                if j // 4 + 1 < 4:
                                    emit_qchain_a(j // 4 + 1)
                        elif kind == "k":
                            def _(s):
                                s.wait_ge(sT.h, mm_t[ui])
                                if ui >= 2:
                                    s.wait_ge(sT.h, msk_t.get(ui - 2, 0))
                                ins = s.activation(k2t[:, j % 2, :], psQK[:, ui % 2, :],
                                                   AF.Square)
                                sS.inc(ins)
                                ksq_s[ui] = sS.n
                            SQ(_)

                            def _(t):
                                t.wait_ge(sS.h, ksq_s[ui])
                                for jj in range(4):
                                    ins = t.matmul(psMix[:, j * 4 + jj:j * 4 + jj + 1],
                                                   k2t[:, j % 2, jj * 128:(jj + 1) * 128],
                                                   onesb_col[:, :], start=True, stop=True)
                                sT.inc(ins)
                                msk_t[ui] = sT.n
                            TQ(_)

                            def _(v):
                                v.wait_ge(sS.h, ksq_s[ui])
                                v.tensor_tensor(ktc[:, :], psQK[:, ui % 2, :],
                                                ropeC[:, j * 512:(j + 1) * 512], ALU.mult)
                                v.tensor_copy(krot[0:64, :], psQK[64:128, ui % 2, :])
                                v.tensor_copy(krot[64:128, :], psQK[0:64, ui % 2, :])
                                v.tensor_tensor(krot[:, :], krot[:, :],
                                                ropeS[:, j * 512:(j + 1) * 512], ALU.mult)
                                ins = v.tensor_tensor(kbuf[:, j * 512:(j + 1) * 512],
                                                      ktc[:, :], krot[:, :], ALU.add)
                                sV.inc(ins)
                                cons_v[ui] = [(sV, sV.n)]
                            VQ(_)

                            if j == 3:
                                def _(v):
                                    v.wait_ge(sT.h, msk_t[ui])
                                    ins = v.tensor_scalar(rskt[:, :], psMix[:, 0:16],
                                                          1.0 / 128.0, RMS_EPS, ALU.mult,
                                                          op1=ALU.add)
                                    sV.inc(ins); v.wait_ge(sV.h, sV.n)
                                    ins = v.reciprocal(rskt[:, :], rskt[:, :])
                                    sV.inc(ins)
                                    rsk_vv["a"] = sV.n
                                VQ(_)

                                def _(s):
                                    s.wait_ge(sV.h, rsk_vv["a"])
                                    ins = s.activation(rsk[:, :], rskt[:, :], AF.Sqrt)
                                    sS.inc(ins)
                                    rsk_vv["s"] = sS.n
                                SQ(_)

                                def _(v):
                                    v.wait_ge(sS.h, rsk_vv["s"])
                                    ins = v.tensor_scalar(rsk[:, :], rsk[:, :],
                                                          1.0 / 128.0, None, ALU.mult)
                                    sV.inc(ins)
                                    rsk_vv["v"] = sV.n
                                VQ(_)
                        else:
                            def _(s):
                                s.wait_ge(sT.h, mm_t[ui])
                                ins = s.activation(q2t[:, j, :], psQK[:, ui % 2, :],
                                                   AF.Square)
                                sS.inc(ins)
                                qsq_s[j] = sS.n
                            SQ(_)

                            def _(v):
                                v.wait_ge(sS.h, qsq_s[j])
                                ins = v.tensor_copy(qraw[:, j * 512:(j + 1) * 512],
                                                    psQK[:, ui % 2, :])
                                sV.inc(ins)
                                cons_v[ui] = [(sV, sV.n), (sS, qsq_s[j])]
                            VQ(_)

                    qkv_done[H] = sT.n
                    qkv_last_v = sV.n

                    exp_s = {}; msk_v = {}; sc_t = {}; pv_t = {}
                    onorm_v = {}; den_v = {}; denb_t = {}
                    tiles = [(qtg, kt) for qtg in range(4) for kt in range(4 * qtg + 4)]
                    scslots = [psSc[:, 0, :], psSc[:, 1, :], psQK[:, 0, :], psQK[:, 1, :]]

                    def emit_sc(m):
                        qtg, kt = tiles[m]
                        def _(t):
                            if m == 0:
                                t.wait_ge(sV.h, qkv_last_v)
                                if prev_exp_S:
                                    t.wait_ge(sS.h, prev_exp_S)
                            if m >= 4:
                                t.wait_ge(sS.h, exp_s[m - 4])
                            ins = t.matmul(scslots[m % 4],
                                           kbuf[:, kt * 128:(kt + 1) * 128],
                                           qbuf[:, qtg * 512:(qtg + 1) * 512],
                                           start=True, stop=True)
                            sT.inc(ins)
                            sc_t[m] = sT.n
                        TQ(_)

                        def _(s):
                            s.wait_ge(sT.h, sc_t[m])
                            if m >= 4:
                                s.wait_ge(sT.h, pv_t[m - 4])
                            ins = s.activation(ptile[:, m % 4, :], scslots[m % 4],
                                               AF.Exp, scale=rsk[:, kt:kt + 1])
                            sS.inc(ins)
                            exp_s[m] = sS.n
                        SQ(_)

                        if kt >= 4 * qtg:
                            def _(v):
                                v.wait_ge(sS.h, exp_s[m])
                                ins = v.tensor_tensor(ptile[:, m % 4, :], ptile[:, m % 4, :],
                                                      masks[:, kt - 4 * qtg, :], ALU.mult)
                                sV.inc(ins)
                                msk_v[m] = sV.n
                            VQ(_)

                    def emit_pv(m):
                        qtg, kt = tiles[m]
                        nkt = 4 * qtg + 4
                        def _(t):
                            if m in msk_v:
                                t.wait_ge(sV.h, msk_v[m])
                            else:
                                t.wait_ge(sS.h, exp_s[m])
                            if kt == 0:
                                if qtg > 0:
                                    t.wait_ge(sV.h, den_v[qtg - 1])
                                if qtg >= 2:
                                    t.wait_ge(sV.h, onorm_v[qtg - 2])
                            t.matmul(psMix[0:1, :], onesb_col[:, :], ptile[:, m % 4, :],
                                     start=(kt == 0), stop=(kt == nkt - 1))
                            ins = t.matmul(psO[:, qtg % 2, :], vbuf[:, kt, :],
                                           ptile[:, m % 4, :],
                                           start=(kt == 0), stop=(kt == nkt - 1))
                            sT.inc(ins)
                            pv_t[m] = sT.n
                        TQ(_)

                        if kt == nkt - 1:
                            def _(v):
                                v.wait_ge(sT.h, pv_t[m])
                                with nc.allow_low_precision(reason="bf16 recip row; tile is bf16"):
                                    ins = v.reciprocal(rowQ[:, :], psMix[0:1, :])
                                sV.inc(ins)
                                den_v[qtg] = sV.n
                            VQ(_)

                            def _(t):
                                t.wait_ge(sV.h, den_v[qtg])
                                if qtg >= 2:
                                    t.wait_ge(sV.h, onorm_v[qtg - 2])
                                ins = t.matmul(psDen2[:, :], onesb_row[:, :], rowQ[:, :],
                                               start=True, stop=True)
                                sT.inc(ins)
                                denb_t[qtg] = sT.n
                            TQ(_)

                            def _(v):
                                v.wait_ge(sT.h, denb_t[qtg])
                                v.tensor_copy(recipb[:, :], psDen2[:, :])
                                ins = v.tensor_tensor(o_all[:, h8, qtg * 512:(qtg + 1) * 512],
                                                      psO[:, qtg % 2, :], recipb[:, :],
                                                      ALU.mult)
                                sV.inc(ins)
                                onorm_v[qtg] = sV.n
                            VQ(_)

                    for m in range(len(tiles)):
                        emit_sc(m)
                        if m >= 2:
                            emit_pv(m - 2)
                    emit_pv(len(tiles) - 2)
                    emit_pv(len(tiles) - 1)

                    prev_exp_S = exp_s[len(tiles) - 1]
                    prev_onorm_last = onorm_v[3]
                    prev_head_free = onorm_v[3]

                op_t = {}; opr_v = {}
                for i in range(32):
                    dc, g4 = divmod(i, 4)

                    def _(t):
                        if i == 0:
                            t.wait_ge(sV.h, prev_onorm_last)
                            t.wait_ge(dG.h, wo_ok)
                        if i >= 2:
                            t.wait_ge(sV.h, opr_v[i - 2])
                        for h8 in range(8):
                            ins = t.matmul(psQK[:, i % 2, :],
                                           woAll[:, h8, dc * 128:(dc + 1) * 128],
                                           o_all[:, h8, g4 * 512:(g4 + 1) * 512],
                                           start=(h8 == 0), stop=(h8 == 7))
                        sT.inc(ins)
                        op_t[i] = sT.n
                    TQ(_)

                    def _(v):
                        v.wait_ge(sT.h, op_t[i])
                        sl = slice(tb + g4 * 512, tb + (g4 + 1) * 512)
                        ins = v.tensor_tensor(X[:, dc, sl], X[:, dc, sl],
                                              psQK[:, i % 2, :], ALU.add)
                        sV.inc(ins)
                        opr_v[i] = sV.n
                    VQ(_)
                r3_ok = sV.n

                b3r2_s = {}; b3st_t = {}; b3rows = {}; b3bc_t = {}; b3ab_v = {}
                for g4 in range(4):
                    def _(s):
                        s.wait_ge(sV.h, r3_ok)
                        if g4 > 0:
                            s.wait_ge(sT.h, b3st_t[g4 - 1])
                        for c in range(NC):
                            ins = s.activation(r2b3[:, c, :],
                                               X[:, c, tb + g4 * 512:tb + (g4 + 1) * 512],
                                               AF.Square)
                        sS.inc(ins)
                        b3r2_s[g4] = sS.n
                    SQ(_)

                    def _(t):
                        t.wait_ge(sS.h, b3r2_s[g4])
                        if g4 == 0:
                            t.wait_ge(sV.h, prev_onorm_last)
                        else:
                            t.wait_ge(sV.h, b3rows[g4 - 1])
                        for c in range(NC):
                            t.matmul(psSc[0:1, 0, :], onesb_col[:, :],
                                     X[:, c, tb + g4 * 512:tb + (g4 + 1) * 512],
                                     start=(c == 0), stop=(c == NC - 1))
                            ins = t.matmul(psSc[0:1, 1, :], onesb_col[:, :],
                                           r2b3[:, c, :],
                                           start=(c == 0), stop=(c == NC - 1))
                        sT.inc(ins)
                        b3st_t[g4] = sT.n
                    TQ(_)

                    rows_ok = ln_rowmath(512, psSc[0:1, 0, :], psSc[0:1, 1, :], b3st_t[g4])
                    b3rows[g4] = rows_ok

                    def _(t):
                        t.wait_ge(sV.h, rows_ok)
                        if g4 >= 2:
                            t.wait_ge(sV.h, b3ab_v[g4 - 2])
                        ins = t.matmul(psO[:, 0, :], onesb_row[:, :], rowA[:, :],
                                       start=True, stop=True)
                        ins = t.matmul(psO[:, 1, :], onesb_row[:, :], rowB[:, :],
                                       start=True, stop=True)
                        sT.inc(ins)
                        b3bc_t[g4] = sT.n
                    TQ(_)

                    def _(v):
                        v.wait_ge(sT.h, b3bc_t[g4])
                        v.tensor_copy(Abuf[:, g4 * 512:(g4 + 1) * 512], psO[:, 0, :])
                        ins = v.tensor_copy(Bbuf[:, g4 * 512:(g4 + 1) * 512], psO[:, 1, :])
                        sV.inc(ins)
                        b3ab_v[g4] = sV.n
                    VQ(_)

                def _(v):
                    for c in range(NC):
                        xv = X[:, c, tb:tb + T]
                        v.tensor_tensor(xv, xv, Abuf[:, 0:T], ALU.mult)
                        ins = v.tensor_tensor(xv, xv, Bbuf[:, 0:T], ALU.add)
                    sV.inc(ins)
                VQ(_)
                prev_onorm_last = sV.n
                prev_head_free = sV.n

            b3_V, b3_T, b3_S = sV.n, sT.n, sS.n
        tap("b3", b3_V, b3_T, b3_S)

        with ExitStack() as ph:
            WinT = ph.enter_context(nc.sbuf_tensor("WinT", [128, NC, 128], BF16))
            WoutT = ph.enter_context(nc.sbuf_tensor("WoutT", [L1_M, D], BF16))
            W1T = ph.enter_context(nc.sbuf_tensor("W1T", [128, 8], F32))
            b1T = ph.enter_context(nc.sbuf_tensor("b1T", [128, 8], F32))
            W2T = ph.enter_context(nc.sbuf_tensor("W2T", [128, 8], F32))
            b2T = ph.enter_context(nc.sbuf_tensor("b2T", [L1_M, 1], F32))
            binT = ph.enter_context(nc.sbuf_tensor("binT", [128, 1], F32))
            bcomb = ph.enter_context(nc.sbuf_tensor("bcomb", [128, 8], F32))
            trelu = ph.enter_context(nc.sbuf_tensor("trelu", [128, 2, 512], BF16))
            gP = ph.enter_context(nc.sbuf_tensor("gP", [128, 2, 512], BF16))
            gbuf = ph.enter_context(nc.sbuf_tensor("gbuf", [L1_M, 2, 512], BF16))
            gtmp = ph.enter_context(nc.sbuf_tensor("gtmp", [L1_M, 2, 512], BF16))
            r2b4 = ph.enter_context(nc.sbuf_tensor("r2b4", [128, NC, 512], BF16))
            rOut = ph.enter_context(nc.sbuf_tensor("rOut", [128, NC, 512], F32))
            ostage = ph.enter_context(nc.sbuf_tensor("ostage", [128, 2, D], F32))
            psS4 = ph.enter_context(nc.psum_tensor("psS4", [128, 2, 512], F32))
            psO4 = ph.enter_context(nc.psum_tensor("psO4", [128, 2, 512], F32))
            psB4 = ph.enter_context(nc.psum_tensor("psB4", [128, 2, 512], F32))

            def _(g):
                g.wait_ge(sV.h, b3_V); g.wait_ge(sT.h, b3_T); g.wait_ge(sS.h, b3_S)
                g.dma_start(out=WinT[:],
                            in_=dt["Win2"].rearrange("(c p) m -> p c m", p=128)).then_inc(dG.h, 16)
                g.dma_start(out=WoutT[:], in_=dt["Wout"][:, :]).then_inc(dG.h, 16)
                dG.n += 32
            GQ(_)
            w4_okG = dG.n

            def _(y):
                y.wait_ge(sV.h, b3_V); y.wait_ge(sT.h, b3_T); y.wait_ge(sS.h, b3_S)
                for dst, nm in [(W1T, "W1P"), (b1T, "b1P"), (W2T, "W2P"),
                                (b2T, "b2"), (binT, "binP")]:
                    y.dma_start(out=dst[:], in_=dt[nm][:, :]).then_inc(dS.h, 16)
                dS.n += 5 * 16
            YQ(_)
            w4_okS = dS.n

            def _(v):
                v.wait_ge(dS.h, w4_okS)
                for i in range(8):
                    ins = v.tensor_scalar(bcomb[:, i:i + 1], W1T[:, i:i + 1],
                                          binT[:, :], b1T[:, i:i + 1], ALU.mult, op1=ALU.add)
                sV.inc(ins)
            VQ(_)
            bcomb_ok = sV.n

            smm_t = {}; relu_s = {}; gacc_v = {}; gfin_v = {}; omm_t = {}; ffr_v = {}
            for g in range(8):
                gb = g * 512

                def _(t):
                    if g == 0:
                        t.wait_ge(dG.h, w4_okG)
                        t.wait_ge(sV.h, b3_V)
                    elif g >= 2:
                        t.wait_ge(sS.h, relu_s[(g - 2, 7)])
                    for k in range(NC):
                        ins = t.matmul(psS4[:, g % 2, :], WinT[:, k, :], X[:, k, gb:gb + 512],
                                       start=(k == 0), stop=(k == NC - 1))
                    sT.inc(ins)
                    smm_t[g] = sT.n
                TQ(_)

                for i in range(8):
                    def _(s):
                        s.wait_ge(sT.h, smm_t[g])
                        if i == 0:
                            s.wait_ge(sV.h, bcomb_ok)
                            if g > 0:
                                s.wait_ge(sV.h, gacc_v[(g - 1, 7)])
                        if i >= 2:
                            s.wait_ge(sV.h, gacc_v[(g, i - 2)])
                        ins = s.activation(trelu[:, i % 2, :], psS4[:, g % 2, :], AF.Relu,
                                           bias=bcomb[:, i:i + 1], scale=W1T[:, i:i + 1])
                        sS.inc(ins)
                        relu_s[(g, i)] = sS.n
                    SQ(_)

                    def _(v):
                        v.wait_ge(sS.h, relu_s[(g, i)])
                        if i == 0:
                            ins = v.tensor_scalar(gP[:, g % 2, :], trelu[:, i % 2, :],
                                                  W2T[:, i:i + 1], None, ALU.mult)
                        else:
                            ins = v.scalar_tensor_tensor(gP[:, g % 2, :], trelu[:, i % 2, :],
                                                         W2T[:, i:i + 1], gP[:, g % 2, :],
                                                         ALU.mult, ALU.add)
                        sV.inc(ins)
                        gacc_v[(g, i)] = sV.n
                    VQ(_)

                def _(v):
                    v.tensor_copy(gtmp[:, g % 2, :], gP[64:128, g % 2, :])
                    ins = v.scalar_tensor_tensor(gbuf[:, g % 2, :], gP[0:64, g % 2, :],
                                                 b2T[:, :], gtmp[:, g % 2, :],
                                                 ALU.add, ALU.add)
                    sV.inc(ins)
                    gfin_v[g] = sV.n
                VQ(_)

                for dc in range(NC):
                    jj = g * NC + dc

                    def _(t):
                        t.wait_ge(sV.h, gfin_v[g])
                        if jj >= 2:
                            t.wait_ge(sV.h, ffr_v[jj - 2])
                        ins = t.matmul(psO4[:, jj % 2, :],
                                       WoutT[0:L1_M, dc * 128:(dc + 1) * 128],
                                       gbuf[0:L1_M, g % 2, :], start=True, stop=True)
                        sT.inc(ins)
                        omm_t[jj] = sT.n
                    TQ(_)

                    def _(v):
                        v.wait_ge(sT.h, omm_t[jj])
                        ins = v.scalar_tensor_tensor(X[:, dc, gb:gb + 512],
                                                     psO4[:, jj % 2, :], bout_sb[:, dc:dc + 1],
                                                     X[:, dc, gb:gb + 512], ALU.add, ALU.add)
                        sV.inc(ins)
                        ffr_v[jj] = sV.n
                    VQ(_)
            ffn_done_v = sV.n

            f4r2_s = {}; fst_t = {}; f4rows = {}; fbc_t = {}; fab_v = {}
            ftr_t = {}; fev_v = {}; grp_fev = {}; ostage_dma = {}
            for g in range(8):
                gb = g * 512

                def _(s):
                    s.wait_ge(sV.h, ffn_done_v)
                    if g > 0:
                        s.wait_ge(sT.h, fst_t[g - 1])
                    for c in range(NC):
                        ins = s.activation(r2b4[:, c, :], X[:, c, gb:gb + 512], AF.Square)
                    sS.inc(ins)
                    f4r2_s[g] = sS.n
                SQ(_)

                def _(t):
                    t.wait_ge(sS.h, f4r2_s[g])
                    if g == 0:
                        t.wait_ge(sV.h, ffn_done_v)
                    else:
                        t.wait_ge(sV.h, grp_fev[g - 1])
                    for c in range(NC):
                        t.matmul(psS4[0:1, 0, :], onesb_col[:, :], X[:, c, gb:gb + 512],
                                 start=(c == 0), stop=(c == NC - 1))
                        ins = t.matmul(psO4[0:1, 0, :], onesb_col[:, :], r2b4[:, c, :],
                                       start=(c == 0), stop=(c == NC - 1))
                    sT.inc(ins)
                    fst_t[g] = sT.n
                TQ(_)

                rows_ok = ln_rowmath(512, psS4[0:1, 0, :], psO4[0:1, 0, :], fst_t[g])
                f4rows[g] = rows_ok

                def _(t):
                    t.wait_ge(sV.h, rows_ok)
                    ins = t.matmul(psB4[:, 0, :], onesb_row[:, :], rowA[:, :],
                                   start=True, stop=True)
                    ins = t.matmul(psB4[:, 1, :], onesb_row[:, :], rowB[:, :],
                                   start=True, stop=True)
                    sT.inc(ins)
                    fbc_t[g] = sT.n
                TQ(_)

                def _(v):
                    v.wait_ge(sT.h, fbc_t[g])
                    v.tensor_copy(Abuf[:, 0:512], psB4[:, 0, :])
                    v.tensor_copy(Bbuf[:, 0:512], psB4[:, 1, :])
                    for c in range(NC):
                        v.tensor_tensor(rOut[:, c, :], X[:, c, gb:gb + 512],
                                        Abuf[:, 0:512], ALU.mult)
                        ins = v.tensor_tensor(rOut[:, c, :], rOut[:, c, :],
                                              Bbuf[:, 0:512], ALU.add)
                    sV.inc(ins)
                    fab_v[g] = sV.n
                VQ(_)

                for tt in range(4):
                    gtile = g * 4 + tt
                    jj0 = gtile * NC
                    for c in range(NC):
                        jj = jj0 + c

                        def _(t):
                            t.wait_ge(sV.h, fab_v[g])
                            if jj >= 2:
                                t.wait_ge(sV.h, fev_v[jj - 2])
                            ins = t.transpose(psO4[:, jj % 2, 0:128],
                                              rOut[:, c, tt * 128:(tt + 1) * 128],
                                              ident_f32[:])
                            sT.inc(ins)
                            ftr_t[jj] = sT.n
                        TQ(_)

                        def _(v):
                            v.wait_ge(sT.h, ftr_t[jj])
                            if c == 0 and gtile >= 2:
                                v.wait_ge(dS.h, ostage_dma[gtile - 2])
                            ins = v.tensor_copy(
                                ostage[:, gtile % 2, c * 128:(c + 1) * 128],
                                psO4[:, jj % 2, 0:128])
                            sV.inc(ins)
                            fev_v[jj] = sV.n
                        VQ(_)

                    def _(y):
                        y.wait_ge(sV.h, fev_v[jj0 + NC - 1])
                        tok0 = gb + tt * 128
                        s_ii, toff = divmod(tok0, T)
                        y.dma_start(out=out[s_ii, toff:toff + 128, :],
                                    in_=ostage[:, gtile % 2, :]).then_inc(dS.h, 16)
                        dS.n += 16
                        ostage_dma[gtile] = dS.n
                    YQ(_)
                grp_fev[g] = sV.n

            def _(y):
                y.wait_ge(dS.h, dS.n)
            YQ(_)

            def _(g):
                g.wait_ge(dG.h, dG.n)
            GQ(_)

    return nc, dt


_BUILD_CACHE = {}


def _get_nc(taps=()):
    key = tuple(taps)
    if key not in _BUILD_CACHE:
        _BUILD_CACHE[key] = build_nc(taps)
    return _BUILD_CACHE[key]


def _make_in_maps(inputs):
    consts = _host_consts()
    z = np.ascontiguousarray(np.asarray(inputs["L0_attn_Z"], np.float32))
    shared = {
        "l0_W": np.ascontiguousarray(inputs["L0_attn_W"], np.float32),
        "l0_b": _fm_vec(inputs["L0_attn_b"]),
        "l0_gamma": _fm_vec(inputs["L0_attn_gamma"]),
        "l0_beta": _fm_vec(inputs["L0_attn_beta"]),
        "decay": np.asarray(inputs["L0_attn_decay"], np.float32).reshape(1, 1),
        "gate_w": np.ascontiguousarray(inputs["L0_ffn_gate_w"], np.float32),
        "gate_b": np.asarray(inputs["L0_ffn_gate_b"], np.float32).reshape(MOE_E, 1),
        "sh_wg": np.ascontiguousarray(inputs["L0_ffn_shared_wg"], np.float32),
        "sh_wu": np.ascontiguousarray(inputs["L0_ffn_shared_wu"], np.float32),
        "sh_wd": np.ascontiguousarray(inputs["L0_ffn_shared_wd"], np.float32),
        "ex_wg": np.ascontiguousarray(inputs["L0_ffn_experts_wg"], np.float32),
        "ex_wu": np.ascontiguousarray(inputs["L0_ffn_experts_wu"], np.float32),
        "ex_wd": np.ascontiguousarray(inputs["L0_ffn_experts_wd"], np.float32),
        "wq": np.ascontiguousarray(inputs["L1_attn_wq"], np.float32),
        "wk": np.ascontiguousarray(inputs["L1_attn_wk"], np.float32),
        "wv": np.ascontiguousarray(inputs["L1_attn_wv"], np.float32),
        "wo": np.ascontiguousarray(inputs["L1_attn_wo"], np.float32),
        "Win2": np.ascontiguousarray(np.concatenate(
            [np.asarray(inputs["L1_ffn_Win"], np.float32)] * 2, axis=1)),
        "binP": np.ascontiguousarray(np.concatenate(
            [np.asarray(inputs["L1_ffn_bin_"], np.float32)] * 2).reshape(128, 1)),
        "W1P": _packpair(inputs["L1_ffn_W1"]),
        "b1P": _packpair(inputs["L1_ffn_b1"]),
        "W2P": _packpair(inputs["L1_ffn_W2"]),
        "b2": np.asarray(inputs["L1_ffn_b2"], np.float32).reshape(L1_M, 1),
        "Wout": np.ascontiguousarray(inputs["L1_ffn_Wout"], np.float32),
        "bout": _fm_vec(inputs["L1_ffn_bout"]),
    }
    shared.update(consts)
    maps = []
    for c in range(8):
        m = dict(shared)
        m["x"] = np.ascontiguousarray(z[2 * c:2 * c + 2])
        maps.append(m)
    return maps


def kernel(**inputs):
    nc, _ = _get_nc()
    in_maps = _make_in_maps(inputs)
    res = run_bass_kernel_spmd(nc, in_maps, core_ids=list(range(8)))
    return np.concatenate([r["out"] for r in res.results], axis=0)


# version 11
# speedup vs baseline: 1.1043x; 1.0502x over previous
import numpy as np
import ml_dtypes
from contextlib import ExitStack

import concourse.bass as bass
import concourse.mybir as mybir
from concourse.alu_op_type import AluOpType as ALU
from concourse.bass_utils import run_bass_kernel_spmd

F32 = mybir.dt.float32
BF16 = mybir.dt.bfloat16
AF = mybir.ActivationFunctionType
AX = mybir.AxisListType

NB = 2
T = 2048
NT = NB * T
D = 1024
NC = 8
LN_EPS = 1e-5
RMS_EPS = 1e-6
ROPE_BASE = 10000.0
MOE_I = 512
MOE_E = 16
L1_M = 64
L1_INNER = 16


def _host_consts():
    bf16 = ml_dtypes.bfloat16
    c = {}
    c["ident"] = np.eye(128, dtype=bf16)
    c["ident_f32"] = np.eye(128, dtype=np.float32)
    c["onesb_col"] = np.ones((128, 1), dtype=bf16)
    c["onesb_row"] = np.ones((1, 128), dtype=bf16)
    c["ones_col_f32"] = np.ones((1, 128), dtype=np.float32)
    c["ones_row_f32"] = np.ones((1, 512), dtype=np.float32)
    half = 64
    pos = np.arange(T, dtype=np.float64)[None, :]
    freqs = (ROPE_BASE ** (-np.arange(half, dtype=np.float64) / half))[:, None]
    rad = freqs * pos
    c["ropeC"] = np.ascontiguousarray(np.concatenate([np.cos(rad)] * 2, 0)).astype(bf16)
    c["ropeS"] = np.ascontiguousarray(
        np.concatenate([-np.sin(rad), np.sin(rad)], 0)).astype(bf16)
    m = np.zeros((4, 128, 512), dtype=bf16)
    for j in range(4):
        kk = j * 128 + np.arange(128)[:, None]
        qq = np.arange(512)[None, :]
        m[j] = (kk <= qq).astype(bf16)
    c["masks"] = m
    return c


def _fm_vec(v):
    return np.ascontiguousarray(np.asarray(v, np.float32).reshape(NC, 128).T)


def _packpair(w):
    w = np.asarray(w, np.float32)
    out = np.empty((128, 8), np.float32)
    for i in range(8):
        out[0:64, i] = w[:, 2 * i]
        out[64:128, i] = w[:, 2 * i + 1]
    return np.ascontiguousarray(out)


def _packpair(w):
    w = np.asarray(w, np.float32)
    out = np.empty((128, 8), np.float32)
    for i in range(8):
        out[0:64, i] = w[:, 2 * i]
        out[64:128, i] = w[:, 2 * i + 1]
    return np.ascontiguousarray(out)


class Sem:
    def __init__(self, stack, nc, name):
        self.h = stack.enter_context(nc.semaphore(name))
        self.n = 0

    def inc(self, instr, k=1):
        instr.then_inc(self.h, k)
        self.n += k


def build_nc(taps=()):
    nc = bass.Bass("TRN2", target_bir_lowering=False)
    dt = {}

    def din(name, shape, dtype=F32):
        dt[name] = nc.dram_tensor(name, list(shape), dtype, kind="ExternalInput").ap()
        return dt[name]

    x_in = din("x", [NB, T, D])
    out = nc.dram_tensor("out", [NB, T, D], F32, kind="ExternalOutput").ap()
    tap_out = {}
    for tn in taps:
        tap_out[tn] = nc.dram_tensor("tap_" + tn, [128, NC, NT], F32,
                                     kind="ExternalOutput").ap()

    din("l0_W", [D, D]); din("l0_b", [128, NC]); din("l0_gamma", [128, NC])
    din("l0_beta", [128, NC]); din("decay", [1, 1])
    din("gate_w", [D, MOE_E]); din("gate_b", [MOE_E, 1])
    din("sh_wg", [D, MOE_I]); din("sh_wu", [D, MOE_I]); din("sh_wd", [MOE_I, D])
    din("ex_wg", [MOE_E, D, MOE_I]); din("ex_wu", [MOE_E, D, MOE_I])
    din("ex_wd", [MOE_E, MOE_I, D])
    din("wq", [D, 8, 128]); din("wk", [D, 8, 128]); din("wv", [D, 8, 128])
    din("wo", [8, 128, D])
    din("Win2", [D, 128]); din("binP", [128, 1])
    din("W1P", [128, 8]); din("b1P", [128, 8])
    din("W2P", [128, 8]); din("b2", [L1_M, 1])
    din("Wout", [L1_M, D]); din("bout", [128, NC])
    din("ident", [128, 128], BF16); din("ident_f32", [128, 128])
    din("onesb_col", [128, 1], BF16)
    din("onesb_row", [1, 128], BF16); din("ones_col_f32", [1, 128])
    din("ones_row_f32", [1, 512])
    din("ropeC", [128, T], BF16); din("ropeS", [128, T], BF16)
    din("masks", [4, 128, 512], BF16)

    stack = ExitStack()
    with stack:
        dS = Sem(stack, nc, "dS")
        dG = Sem(stack, nc, "dG")
        sT = Sem(stack, nc, "sT")
        sV = Sem(stack, nc, "sV")
        sS = Sem(stack, nc, "sS")

        def sb(name, shape, dtype):
            return stack.enter_context(nc.sbuf_tensor(name, list(shape), dtype))

        X = sb("X", [128, NC, NT], BF16)
        ident = sb("c_ident", [128, 128], BF16)
        ident_f32 = sb("c_identf", [128, 128], F32)
        onesb_col = sb("c_obc", [128, 1], BF16)
        onesb_row = sb("c_obr", [1, 128], BF16)
        ones_col_f32 = sb("c_ocf", [1, 128], F32)
        ones_row_f32 = sb("c_orf", [1, 512], F32)
        l0_b = sb("c_l0b", [128, NC], F32)
        l0_gamma = sb("c_l0g", [128, NC], F32)
        l0_beta = sb("c_l0be", [128, NC], F32)
        bout_sb = sb("c_bout", [128, NC], F32)
        decay_sb = sb("c_decay", [1, 1], F32)
        rowA = sb("rowA", [1, 512], BF16)
        rowB = sb("rowB", [1, 512], BF16)
        Abuf = sb("Abuf", [128, T], BF16)
        Bbuf = sb("Bbuf", [128, T], BF16)
        statf = sb("statf", [1, 3, 512], F32)

        block = stack.enter_context(nc.Block())
        TQ, VQ, SQ, GQ, YQ = block.tensor, block.vector, block.scalar, block.gpsimd, block.sync

        def _(y):
            for sbuf_t, nm in [(ident, "ident"), (ident_f32, "ident_f32"),
                               (onesb_col, "onesb_col"),
                               (onesb_row, "onesb_row"), (ones_col_f32, "ones_col_f32"),
                               (ones_row_f32, "ones_row_f32"),
                               (l0_b, "l0_b"), (l0_gamma, "l0_gamma"),
                               (l0_beta, "l0_beta"), (bout_sb, "bout"),
                               (decay_sb, "decay")]:
                y.dma_start(out=sbuf_t[:], in_=dt[nm][:, :]).then_inc(dS.h, 16)
                dS.n += 16
        YQ(_)
        consts_ready = dS.n

        def ln_rowmath(n, psA_row, psB_row, t_mark):
            def _(v):
                v.wait_ge(sT.h, t_mark)
                mu = statf[:, 0, 0:n]; w = statf[:, 1, 0:n]; t2 = statf[:, 2, 0:n]
                v.tensor_scalar(mu, psA_row, 1.0 / D, None, ALU.mult)
                v.tensor_scalar(w, psB_row, 1.0 / D, None, ALU.mult)
                v.tensor_tensor(t2, mu, mu, ALU.mult)
                v.tensor_tensor(w, w, t2, ALU.subtract)
                v.tensor_scalar(w, w, 1.0, LN_EPS, ALU.mult, op1=ALU.add)
                ins = v.reciprocal(w, w)
                sV.inc(ins)
            VQ(_)
            m1 = sV.n

            def _(s):
                s.wait_ge(sV.h, m1)
                ins = s.activation(rowA[:, 0:n], statf[:, 1, 0:n], AF.Sqrt)
                sS.inc(ins)
            SQ(_)
            m2 = sS.n

            def _(v):
                v.wait_ge(sS.h, m2)
                v.tensor_tensor(statf[:, 2, 0:n], statf[:, 0, 0:n], rowA[:, 0:n], ALU.mult)
                ins = v.tensor_scalar(rowB[:, 0:n], statf[:, 2, 0:n], -1.0, None, ALU.mult)
                sV.inc(ins)
            VQ(_)
            return sV.n

        with ExitStack() as ph:
            psT = [ph.enter_context(nc.psum_tensor("psTa", [128, 1024], BF16)),
                   ph.enter_context(nc.psum_tensor("psTb", [128, 1024], BF16))]
            xstage = ph.enter_context(nc.sbuf_tensor("xstage", [128, 2, D], BF16))

            ntile = NB * (T // 128)
            ld_m = [0] * ntile; tr_m = [0] * ntile; ev_m = [0] * ntile
            for i in range(ntile):
                s_i, tt = divmod(i, T // 128)

                def _(g):
                    if i >= 2:
                        g.wait_ge(sT.h, tr_m[i - 2])
                    g.dma_start(out=xstage[:, i % 2, :],
                                in_=x_in[s_i, tt * 128:(tt + 1) * 128, :]).then_inc(dG.h, 16)
                    dG.n += 16
                    ld_m[i] = dG.n
                GQ(_)

                def _(t):
                    t.wait_ge(dG.h, ld_m[i])
                    if i == 0:
                        t.wait_ge(dS.h, consts_ready)
                    if i >= 2:
                        t.wait_ge(sV.h, ev_m[i - 2])
                    for c in range(NC):
                        ins = t.transpose(psT[i % 2][:, c * 128:(c + 1) * 128],
                                          xstage[:, i % 2, c * 128:(c + 1) * 128], ident[:])
                    sT.inc(ins)
                    tr_m[i] = sT.n
                TQ(_)

                def _(v):
                    v.wait_ge(sT.h, tr_m[i])
                    t0 = s_i * T + tt * 128
                    for c in range(NC):
                        ins = v.tensor_copy(X[:, c, t0:t0 + 128],
                                            psT[i % 2][:, c * 128:(c + 1) * 128])
                    sV.inc(ins)
                    ev_m[i] = sV.n
                VQ(_)
            in_T, in_V = sT.n, sV.n

        def tap(name, tV, tT, tS):
            if name not in tap_out:
                return
            def _(g):
                g.wait_ge(sV.h, tV); g.wait_ge(sT.h, tT); g.wait_ge(sS.h, tS)
                g.dma_start(out=tap_out[name][:, :, :], in_=X[:]).then_inc(dG.h, 16)
                dG.n += 16
            GQ(_)

        tap("xin", in_V, in_T, sS.n)

        with ExitStack() as ph:
            Wrow = ph.enter_context(nc.sbuf_tensor("b1Wrow", [128, NC, D], BF16))
            Wt = ph.enter_context(nc.sbuf_tensor("b1Wt", [128, NC, D], BF16))
            Sbuf = ph.enter_context(nc.sbuf_tensor("b1S", [128, NC, T], BF16))
            Ybuf = ph.enter_context(nc.sbuf_tensor("b1Y", [128, NC, T], BF16))
            a_tile = ph.enter_context(nc.sbuf_tensor("b1a", [128, T], F32))
            a_row = ph.enter_context(nc.sbuf_tensor("b1ar", [1, 512], F32))
            a_val = ph.enter_context(nc.sbuf_tensor("b1av", [1, 1], F32))
            y2t = ph.enter_context(nc.sbuf_tensor("b1y2", [128, 2, 512], BF16))
            psY = [ph.enter_context(nc.psum_tensor("b1psY0", [128, 512], F32)),
                   ph.enter_context(nc.psum_tensor("b1psY1", [128, 512], F32)),
                   None, None]
            psSA = ph.enter_context(nc.psum_tensor("b1psSA", [128, 512], F32))
            psSB = ph.enter_context(nc.psum_tensor("b1psSB", [128, 512], F32))
            psBa = ph.enter_context(nc.psum_tensor("b1psBa", [128, 512], F32))
            psBb = ph.enter_context(nc.psum_tensor("b1psBb", [128, 512], F32))
            psT = [ph.enter_context(nc.psum_tensor("b1psT0", [128, 1024], BF16)),
                   ph.enter_context(nc.psum_tensor("b1psT1", [128, 1024], BF16))]
            psY[2] = psBa
            psY[3] = psBb

            def _(g):
                g.wait_ge(sT.h, in_T)
                g.wait_ge(sV.h, in_V)
                g.dma_start(out=Wrow[:],
                            in_=dt["l0_W"].rearrange("(c p) d -> p c d", p=128)).then_inc(dG.h, 16)
                dG.n += 16
            GQ(_)
            wrow_ok = dG.n

            def _(s):
                s.wait_ge(dS.h, consts_ready)
                ins = s.activation(a_val[:], decay_sb[:], AF.Sigmoid)
                sS.inc(ins)
            SQ(_)
            sig_ok = sS.n

            def _(v):
                v.wait_ge(sS.h, sig_ok)
                ins = v.tensor_scalar(a_row[:], ones_row_f32[:], a_val[:], None, ALU.mult)
                sV.inc(ins)
            VQ(_)
            arow_okV = sV.n

            at_t = [0] * 4; at_v = [0] * 4
            for j in range(4):
                def _(t):
                    if j == 0:
                        t.wait_ge(sV.h, max(arow_okV, in_V))
                    if j >= 2:
                        t.wait_ge(sV.h, at_v[j - 2])
                    bank = psBa if j % 2 == 0 else psBb
                    ins = t.matmul(bank[:, :], ones_col_f32[:, :], a_row[:, :],
                                   start=True, stop=True)
                    sT.inc(ins)
                    at_t[j] = sT.n
                TQ(_)

                def _(v):
                    v.wait_ge(sT.h, at_t[j])
                    bank = psBa if j % 2 == 0 else psBb
                    ins = v.tensor_copy(a_tile[:, j * 512:(j + 1) * 512], bank[:, :])
                    sV.inc(ins)
                    at_v[j] = sV.n
                VQ(_)
            atile_ok = sV.n

            wt_t = [0] * 64; wt_v = [0] * 64
            for i in range(64):
                ce, cd = divmod(i, NC)

                def _(t):
                    if i == 0:
                        t.wait_ge(dG.h, wrow_ok)
                        t.wait_ge(sV.h, in_V)
                    if i >= 2:
                        t.wait_ge(sV.h, wt_v[i - 2])
                    ins = t.transpose(psT[i % 2][:, 0:128],
                                      Wrow[:, ce, cd * 128:(cd + 1) * 128], ident[:])
                    sT.inc(ins)
                    wt_t[i] = sT.n
                TQ(_)

                def _(v):
                    v.wait_ge(sT.h, wt_t[i])
                    ins = v.tensor_copy(Wt[:, cd, ce * 128:(ce + 1) * 128],
                                        psT[i % 2][:, 0:128])
                    sV.inc(ins)
                    wt_v[i] = sV.n
                VQ(_)
            wt_ok = sV.n

            prev_y_lastS = 0
            prev_y_lastV = 0
            prev_r2_T = 0
            for s_i in range(NB):
                tb = s_i * T

                def _(v):
                    if s_i > 0:
                        v.wait_ge(sT.h, prev_r2_T)
                    for c in range(NC):
                        ins = v.tensor_tensor_scan(Sbuf[:, c, :], a_tile[:],
                                                   X[:, c, tb:tb + T], 0.0,
                                                   ALU.mult, ALU.add)
                    sV.inc(ins)
                VQ(_)
                scan_ok = sV.n

                ymm = [0] * 32; yev = [0] * 32; yevV = [0] * 32; yst_t = [0] * 32
                ystat_v = [0] * 4; yab_v = [0] * 4; ybc_t = [0] * 4

                def emit_ymm(i):
                    g, eg = divmod(i, NC)
                    def _(t):
                        if i == 0:
                            t.wait_ge(sV.h, max(scan_ok, wt_ok))
                            if s_i > 0:
                                t.wait_ge(sS.h, prev_y_lastS)
                                t.wait_ge(sV.h, prev_y_lastV)
                        if i >= 4:
                            t.wait_ge(sS.h, yev[i - 4])
                            t.wait_ge(sV.h, yevV[i - 4])
                        if i == 0 and s_i == 0:
                            t.wait_ge(sV.h, atile_ok)
                        for k in range(NC):
                            ins = t.matmul(psY[i % 4][:, :], Wt[:, k, eg * 128:(eg + 1) * 128],
                                           Sbuf[:, k, g * 512:(g + 1) * 512],
                                           start=(k == 0), stop=(k == NC - 1))
                        sT.inc(ins)
                        ymm[i] = sT.n
                    TQ(_)

                def emit_yev(i):
                    g, eg = divmod(i, NC)
                    def _(s):
                        s.wait_ge(sT.h, ymm[i])
                        if i >= 2:
                            s.wait_ge(sT.h, yst_t[i - 2])
                        ins = s.activation(y2t[:, i % 2, :], psY[i % 4][:, :],
                                           AF.Square, bias=l0_b[:, eg:eg + 1])
                        sS.inc(ins)
                        yev[i] = sS.n
                    SQ(_)

                    def _(v):
                        v.wait_ge(sS.h, yev[i])
                        if i >= 2:
                            v.wait_ge(sT.h, yst_t[i - 2])
                        ins = v.tensor_scalar(Ybuf[:, eg, g * 512:(g + 1) * 512],
                                              psY[i % 4][:, :], l0_b[:, eg:eg + 1],
                                              None, ALU.add)
                        sV.inc(ins)
                        yevV[i] = sV.n
                    VQ(_)

                def emit_yst(i):
                    g, eg = divmod(i, NC)
                    def _(t):
                        t.wait_ge(sS.h, yev[i])
                        t.wait_ge(sV.h, yevV[i])
                        if eg == 0 and g > 0:
                            t.wait_ge(sV.h, yab_v[g - 1])
                        ins = t.matmul(psSA[0:1, :], onesb_col[:, :],
                                       Ybuf[:, eg, g * 512:(g + 1) * 512],
                                       start=(eg == 0), stop=(eg == NC - 1))
                        ins = t.matmul(psSB[0:1, :], onesb_col[:, :], y2t[:, i % 2, :],
                                       start=(eg == 0), stop=(eg == NC - 1))
                        sT.inc(ins)
                        yst_t[i] = sT.n
                    TQ(_)

                def emit_ygrp(g):
                    rows_ok = ln_rowmath(512, psSA[0:1, :], psSB[0:1, :], yst_t[g * 8 + 7])
                    ystat_v[g] = rows_ok

                    def _(t):
                        t.wait_ge(sV.h, rows_ok)
                        ins = t.matmul(psSA[:, :], onesb_row[:, :], rowA[:, :],
                                       start=True, stop=True)
                        ins = t.matmul(psSB[:, :], onesb_row[:, :], rowB[:, :],
                                       start=True, stop=True)
                        sT.inc(ins)
                        ybc_t[g] = sT.n
                    TQ(_)

                    def _(v):
                        v.wait_ge(sT.h, ybc_t[g])
                        v.tensor_copy(Abuf[:, g * 512:(g + 1) * 512], psSA[:, :])
                        ins = v.tensor_copy(Bbuf[:, g * 512:(g + 1) * 512], psSB[:, :])
                        sV.inc(ins)
                        yab_v[g] = sV.n
                    VQ(_)

                for i in range(32):
                    emit_ymm(i)
                    emit_yev(i)
                    if i >= 1:
                        emit_yst(i - 1)
                        if (i - 1) % 8 == 7:
                            emit_ygrp((i - 1) // 8)
                emit_yst(31)
                emit_ygrp(3)
                prev_y_lastS = yev[31]
                prev_y_lastV = yevV[31]

                def _(v):
                    for eg in range(NC):
                        yv = Ybuf[:, eg, :]
                        v.tensor_tensor(yv, yv, Abuf[:, 0:T], ALU.mult)
                        v.tensor_tensor(yv, yv, Bbuf[:, 0:T], ALU.add)
                        v.tensor_scalar(yv, yv, l0_gamma[:, eg:eg + 1],
                                        l0_beta[:, eg:eg + 1], ALU.mult, op1=ALU.add)
                        ins = v.tensor_tensor(X[:, eg, tb:tb + T], X[:, eg, tb:tb + T],
                                              yv, ALU.add)
                    sV.inc(ins)
                VQ(_)
                r_ok = sV.n

                def _(s):
                    s.wait_ge(sV.h, r_ok)
                    for c in range(NC):
                        ins = s.activation(Sbuf[:, c, :], X[:, c, tb:tb + T], AF.Square)
                    sS.inc(ins)
                SQ(_)
                r2_ok = sS.n

                ost_t = [0] * 4; ostat_v = [0] * 4; obc_t = [0] * 4; oab_v = [0] * 4
                for g in range(4):
                    def _(t):
                        t.wait_ge(sS.h, r2_ok)
                        if g > 0:
                            t.wait_ge(sV.h, ostat_v[g - 1])
                        else:
                            t.wait_ge(sV.h, yab_v[3])
                        for c in range(NC):
                            ins = t.matmul(psSA[0:1, :], onesb_col[:, :],
                                           X[:, c, tb + g * 512:tb + (g + 1) * 512],
                                           start=(c == 0), stop=(c == NC - 1))
                            ins = t.matmul(psSB[0:1, :], onesb_col[:, :],
                                           Sbuf[:, c, g * 512:(g + 1) * 512],
                                           start=(c == 0), stop=(c == NC - 1))
                        sT.inc(ins)
                        ost_t[g] = sT.n
                    TQ(_)
                    rows_ok = ln_rowmath(512, psSA[0:1, :], psSB[0:1, :], ost_t[g])
                    ostat_v[g] = rows_ok

                    def _(t):
                        t.wait_ge(sV.h, rows_ok)
                        if g >= 2:
                            t.wait_ge(sV.h, oab_v[g - 2])
                        ins = t.matmul(psBa[:, :], onesb_row[:, :], rowA[:, :],
                                       start=True, stop=True)
                        ins = t.matmul(psBb[:, :], onesb_row[:, :], rowB[:, :],
                                       start=True, stop=True)
                        sT.inc(ins)
                        obc_t[g] = sT.n
                    TQ(_)

                    def _(v):
                        v.wait_ge(sT.h, obc_t[g])
                        v.tensor_copy(Abuf[:, g * 512:(g + 1) * 512], psBa[:, :])
                        ins = v.tensor_copy(Bbuf[:, g * 512:(g + 1) * 512], psBb[:, :])
                        sV.inc(ins)
                        oab_v[g] = sV.n
                    VQ(_)

                def _(v):
                    for c in range(NC):
                        xv = X[:, c, tb:tb + T]
                        v.tensor_tensor(xv, xv, Abuf[:, 0:T], ALU.mult)
                        ins = v.tensor_tensor(xv, xv, Bbuf[:, 0:T], ALU.add)
                    sV.inc(ins)
                VQ(_)
                prev_r2_T = ost_t[3]

            b1_V, b1_T, b1_S = sV.n, sT.n, sS.n
        tap("b1", b1_V, b1_T, b1_S)

        with ExitStack() as ph:
            wgT = ph.enter_context(nc.sbuf_tensor("wgT", [128, 2, NC, MOE_I], BF16))
            wuT = ph.enter_context(nc.sbuf_tensor("wuT", [128, 2, NC, MOE_I], BF16))
            wdT = ph.enter_context(nc.sbuf_tensor("wdT", [128, 2, 4, D], BF16))
            gateT = ph.enter_context(nc.sbuf_tensor("gateT", [128, NC, MOE_E], BF16))
            gatebT = ph.enter_context(nc.sbuf_tensor("gatebT", [MOE_E, 1], F32))
            Rsb = ph.enter_context(nc.sbuf_tensor("Rsb", [MOE_E, T], F32))
            Rt = ph.enter_context(nc.sbuf_tensor("Rt", [128, 16, MOE_E], F32))
            ctok = ph.enter_context(nc.sbuf_tensor("ctok", [128, 2, 128], F32))
            cF = ph.enter_context(nc.sbuf_tensor("cF", [MOE_E, T], BF16))
            cRow = ph.enter_context(nc.sbuf_tensor("cRow", [1, 2, T], BF16))
            cb = ph.enter_context(nc.sbuf_tensor("cb", [128, T], BF16))
            accb = ph.enter_context(nc.sbuf_tensor("accb", [128, NC, T], BF16))
            slb = ph.enter_context(nc.sbuf_tensor("slb", [128, 2, 4, 256], BF16))
            hpb = ph.enter_context(nc.sbuf_tensor("hpb", [128, 2, 4, 256], BF16))
            tv1 = ph.enter_context(nc.sbuf_tensor("tv1", [128, 4], F32))
            tv2 = ph.enter_context(nc.sbuf_tensor("tv2", [128, MOE_E], F32))
            tv3 = ph.enter_context(nc.sbuf_tensor("tv3", [128, MOE_E], F32))
            psHG = ph.enter_context(nc.psum_tensor("psHG", [128, 4, 256], F32))
            psHU = ph.enter_context(nc.psum_tensor("psHU", [128, 4, 256], F32))
            psACC = ph.enter_context(nc.psum_tensor("psACC", [128, NC, 256], F32))

            def _(g):
                g.wait_ge(sV.h, b1_V); g.wait_ge(sT.h, b1_T); g.wait_ge(sS.h, b1_S)
                g.dma_start(out=gateT[:],
                            in_=dt["gate_w"].rearrange("(c p) e -> p c e", p=128)).then_inc(dG.h, 16)
                dG.n += 16
            GQ(_)
            gate_okG = dG.n

            def _(y):
                y.wait_ge(sV.h, b1_V); y.wait_ge(sT.h, b1_T); y.wait_ge(sS.h, b1_S)
                y.dma_start(out=gatebT[:], in_=dt["gate_b"][:, :]).then_inc(dS.h, 16)
                dS.n += 16
            YQ(_)
            gate_okS = dS.n

            prev_psacc_free_V = 0
            prev_unit16_T = 0
            prev_epB2_V = 0
            for h_i in range(NB):
                hb = h_i * T
                wload = {}
                hmm = {}; humm = {}; t2_m = {}; silu_m = {}; h_m = {}; dmm = {}; accev = {}
                crow_dma = {}; crow_t = {}; cb_ok = {}
                ep_r = {}; ep_r2 = {}; ep_st = {}; ep_bc = {}; ep_ab = {}

                def load_unit(u):
                    def _(g):
                        slot = u % 2
                        if u >= 2:
                            g.wait_ge(sT.h, dmm[(u - 2, 7)])
                        elif h_i > 0:
                            g.wait_ge(sT.h, prev_unit16_T)
                        else:
                            g.wait_ge(sV.h, b1_V); g.wait_ge(sT.h, b1_T)
                            g.wait_ge(sS.h, b1_S)
                        if u == 0:
                            srcs = [dt["sh_wg"], dt["sh_wu"], dt["sh_wd"]]
                        else:
                            e = u - 1
                            srcs = [dt["ex_wg"][e], dt["ex_wu"][e], dt["ex_wd"][e]]
                        g.dma_start(out=wgT[:, slot, :, :],
                                    in_=srcs[0].rearrange("(c p) i -> p c i", p=128)).then_inc(dG.h, 16)
                        g.dma_start(out=wuT[:, slot, :, :],
                                    in_=srcs[1].rearrange("(c p) i -> p c i", p=128)).then_inc(dG.h, 16)
                        g.dma_start(out=wdT[:, slot, :, :],
                                    in_=srcs[2].rearrange("(c p) i -> p c i", p=128)).then_inc(dG.h, 16)
                        dG.n += 48
                        wload[u] = dG.n
                    GQ(_)

                load_unit(0)
                load_unit(1)

                rt_t = [0] * 4; rsig = [0] * 4
                for g4 in range(4):
                    def _(t):
                        if g4 == 0:
                            t.wait_ge(dG.h, gate_okG)
                            t.wait_ge(sV.h, max(b1_V, prev_psacc_free_V))
                        else:
                            t.wait_ge(sS.h, rsig[g4 - 1])
                        for k in range(NC):
                            ins = t.matmul(psACC[0:16, 0:2, :], gateT[:, k, :],
                                           X[:, k, hb + g4 * 512:hb + (g4 + 1) * 512],
                                           start=(k == 0), stop=(k == NC - 1))
                        sT.inc(ins)
                        rt_t[g4] = sT.n
                    TQ(_)

                    def _(s):
                        s.wait_ge(sT.h, rt_t[g4])
                        if g4 == 0:
                            s.wait_ge(dS.h, gate_okS)
                        ins = s.activation(Rsb[:, g4 * 512:(g4 + 1) * 512],
                                           psACC[0:16, 0:2, :], AF.Sigmoid,
                                           bias=gatebT[:, :])
                        sS.inc(ins)
                        rsig[g4] = sS.n
                    SQ(_)

                rtr_t = [0] * 16; rte_v = [0] * 16
                for i in range(16):
                    def _(t):
                        if i == 0:
                            t.wait_ge(sS.h, rsig[3])
                            t.wait_ge(sV.h, prev_epB2_V)
                        if i >= 2:
                            t.wait_ge(sV.h, rte_v[i - 2])
                        reg = psHG[:, 0, 0:16] if i % 2 == 0 else psHG[:, 2, 0:16]
                        ins = t.transpose(reg, Rsb[:, i * 128:(i + 1) * 128], ident_f32[0:16, 0:16])
                        sT.inc(ins)
                        rtr_t[i] = sT.n
                    TQ(_)

                    def _(v):
                        v.wait_ge(sT.h, rtr_t[i])
                        reg = psHG[:, 0, 0:16] if i % 2 == 0 else psHG[:, 2, 0:16]
                        ins = v.tensor_copy(Rt[:, i, :], reg)
                        sV.inc(ins)
                        rte_v[i] = sV.n
                    VQ(_)

                top_v = [0] * 16; ctr_t = [0] * 16; cte_v = [0] * 16
                def _(v):
                    v.memset(ctok[:, 0, :], 0.0)
                    ins = v.memset(ctok[:, 1, :], 0.0)
                    sV.inc(ins)
                VQ(_)
                for i in range(16):
                    def _(v):
                        v.wait_ge(sV.h, rte_v[i])
                        if i >= 2:
                            v.wait_ge(sT.h, ctr_t[i - 2])
                        r = Rt[:, i, :]
                        m1 = tv1[:, 0:1]; m2 = tv1[:, 1:2]
                        den = tv1[:, 2:3]; rec = tv1[:, 3:4]
                        def bump(ins):
                            sV.inc(ins); v.wait_ge(sV.h, sV.n)
                        bump(v.reduce_max(m1, r, AX.X))
                        bump(v.tensor_scalar(tv2[:, :], r, m1, None, ALU.is_ge))
                        bump(v.scalar_tensor_tensor(tv3[:, :], tv2[:, :], -1e30, r,
                                                    ALU.mult, ALU.add))
                        bump(v.reduce_max(m2, tv3[:, :], AX.X))
                        bump(v.tensor_tensor(den, m1, m2, ALU.add))
                        bump(v.reciprocal(rec, den))
                        bump(v.tensor_scalar(tv2[:, :], r, m2, None, ALU.is_ge))
                        ins = v.scalar_tensor_tensor(ctok[:, i % 2, 0:16], tv2[:, :], rec,
                                                     r, ALU.mult, ALU.mult)
                        sV.inc(ins)
                        top_v[i] = sV.n
                    VQ(_)

                    def _(t):
                        t.wait_ge(sV.h, top_v[i])
                        if i >= 2:
                            t.wait_ge(sV.h, cte_v[i - 2])
                        reg = psHG[:, 0, 0:128] if i % 2 == 0 else psHG[:, 2, 0:128]
                        ins = t.transpose(reg, ctok[:, i % 2, :], ident_f32[:])
                        sT.inc(ins)
                        ctr_t[i] = sT.n
                    TQ(_)

                    def _(v):
                        v.wait_ge(sT.h, ctr_t[i])
                        reg = psHG[0:16, 0, 0:128] if i % 2 == 0 else psHG[0:16, 2, 0:128]
                        ins = v.tensor_copy(cF[:, i * 128:(i + 1) * 128], reg)
                        sV.inc(ins)
                        cte_v[i] = sV.n
                    VQ(_)
                cf_ok = sV.n

                def crow_dma_issue(uu):
                    def _(y):
                        if uu == 1:
                            y.wait_ge(sV.h, cf_ok)
                        if uu >= 3:
                            y.wait_ge(sT.h, crow_t[uu - 2])
                        y.dma_start(out=cRow[0:1, uu % 2, :],
                                    in_=cF[uu - 1:uu, :]).then_inc(dS.h, 16)
                        dS.n += 16
                        crow_dma[uu] = dS.n
                    YQ(_)

                crow_dma_issue(1)
                for u in range(17):
                    slot = u % 2
                    if u + 1 <= 16:
                        load_unit(u + 1)
                    if 2 <= u + 1 <= 16:
                        crow_dma_issue(u + 1)

                    if u >= 1:
                        e = u - 1
                        cbm = {}

                        def _(t):
                            t.wait_ge(dS.h, crow_dma[u])
                            if (u - 1, 7) in accev:
                                t.wait_ge(sV.h, accev[(u - 1, 7)])
                            cr = cRow[0:1, u % 2, :]
                            t.matmul(psACC[:, 0:2, :], onesb_row[:, :],
                                     cr[:, 0:512], start=True, stop=True)
                            t.matmul(psACC[:, 2:4, :], onesb_row[:, :],
                                     cr[:, 512:1024], start=True, stop=True)
                            t.matmul(psACC[:, 4:6, :], onesb_row[:, :],
                                     cr[:, 1024:1536], start=True, stop=True)
                            ins = t.matmul(psACC[:, 6:8, :], onesb_row[:, :],
                                           cr[:, 1536:2048], start=True, stop=True)
                            sT.inc(ins)
                            cbm["t1"] = sT.n
                            crow_t[u] = sT.n
                        TQ(_)

                        def _(v):
                            v.wait_ge(sT.h, cbm["t1"])
                            ins = v.tensor_copy(cb[:, :],
                                                psACC[:, :, :].rearrange("p a b -> p (a b)"))
                            sV.inc(ins)
                            cbm["v1"] = sV.n
                        VQ(_)
                        cb_ok[u] = cbm["v1"]

                    def emit_down(u, g):
                        def _(t):
                            t.wait_ge(sV.h, h_m[(u, g)])
                            key = (u, g - 1) if g > 0 else (u - 1, 7)
                            if key in accev:
                                t.wait_ge(sV.h, accev[key])
                            if u == 0 and g == 0:
                                t.wait_ge(sS.h, rsig[3])
                            if u > 0:
                                for b in range(4):
                                    t.matmul(psACC[:, 2 * b:2 * b + 2, :], ident[:],
                                             accb[:, 2 * b:2 * b + 2, g * 256:(g + 1) * 256],
                                             start=True, stop=False)
                            for dc in range(NC):
                                for ic in range(4):
                                    ins = t.matmul(psACC[:, dc, :],
                                                   wdT[:, u % 2, ic, dc * 128:(dc + 1) * 128],
                                                   hpb[:, g % 2, ic, :],
                                                   start=(u == 0 and ic == 0 and dc % 2 == 0),
                                                   stop=(ic == 3 and dc % 2 == 1))
                            sT.inc(ins)
                            dmm[(u, g)] = sT.n
                        TQ(_)

                        def _(v):
                            v.wait_ge(sT.h, dmm[(u, g)])
                            ins = v.tensor_copy(accb[:, :, g * 256:(g + 1) * 256],
                                                psACC[:, :, :])
                            sV.inc(ins)
                            accev[(u, g)] = sV.n
                        VQ(_)

                    for g in range(8):
                        gb = hb + g * 256

                        def _(t):
                            if g == 0:
                                t.wait_ge(dG.h, wload[u])
                                if u == 0:
                                    t.wait_ge(sV.h, cf_ok)
                            key = (u, g - 1) if g > 0 else (u - 1, 7)
                            if key in h_m:
                                t.wait_ge(sV.h, h_m[key])
                            for ic in range(4):
                                for k in range(NC):
                                    ins = t.matmul(psHU[:, ic, :],
                                                   wuT[:, slot, k, ic * 128:(ic + 1) * 128],
                                                   X[:, k, gb:gb + 256],
                                                   start=(k == 0), stop=(k == NC - 1))
                            sT.inc(ins)
                            humm[(u, g)] = sT.n
                            for ic in range(4):
                                for k in range(NC):
                                    ins = t.matmul(psHG[:, ic, :],
                                                   wgT[:, slot, k, ic * 128:(ic + 1) * 128],
                                                   X[:, k, gb:gb + 256],
                                                   start=(k == 0), stop=(k == NC - 1))
                            sT.inc(ins)
                            hmm[(u, g)] = sT.n
                        TQ(_)

                        def _(v):
                            v.wait_ge(sT.h, humm[(u, g)])
                            hp = hpb[:, g % 2, :, :]
                            if u >= 1:
                                v.wait_ge(sV.h, cb_ok[u])
                                ins = v.tensor_tensor(hp, psHU[:, :, :],
                                                      cb[:, g * 256:g * 256 + 256][:, None, :]
                                                      .to_broadcast((128, 4, 256)), ALU.mult)
                            else:
                                ins = v.tensor_copy(hp, psHU[:, :, :])
                            sV.inc(ins)
                            t2_m[(u, g)] = sV.n
                        VQ(_)

                        def _(s):
                            s.wait_ge(sT.h, hmm[(u, g)])
                            ins = s.activation(slb[:, g % 2, :, :], psHG[:, :, :], AF.Silu)
                            sS.inc(ins)
                            silu_m[(u, g)] = sS.n
                        SQ(_)

                        def _(v):
                            v.wait_ge(sS.h, silu_m[(u, g)])
                            hp = hpb[:, g % 2, :, :]
                            ins = v.tensor_tensor(hp, slb[:, g % 2, :, :], hp, ALU.mult)
                            sV.inc(ins)
                            h_m[(u, g)] = sV.n
                        VQ(_)

                        if g >= 1:
                            emit_down(u, g - 1)
                    emit_down(u, 7)

                slbF = slb[:, :, :, :].rearrange("p a b c -> p (a b c)")
                hpbF = hpb[:, :, :, :].rearrange("p a b c -> p (a b c)")
                def r2reg(dc):
                    return (slbF[:, dc * 512:(dc + 1) * 512] if dc < 4
                            else hpbF[:, (dc - 4) * 512:(dc - 3) * 512])
                for gg in range(4):
                    gb = hb + gg * 512

                    def _(v):
                        v.wait_ge(sV.h, accev[(16, 2 * gg + 1)])
                        for dc in range(NC):
                            ins = v.scalar_tensor_tensor(
                                X[:, dc, gb:gb + 512], X[:, dc, gb:gb + 512],
                                2.0, accb[:, dc, gg * 512:(gg + 1) * 512],
                                ALU.mult, ALU.add)
                        sV.inc(ins)
                        ep_r[gg] = sV.n
                    VQ(_)

                    def _(s):
                        s.wait_ge(sV.h, ep_r[gg])
                        s.wait_ge(sT.h, dmm[(16, 7)])
                        for dc in range(NC):
                            ins = s.activation(r2reg(dc),
                                               X[:, dc, gb:gb + 512], AF.Square)
                        sS.inc(ins)
                        ep_r2[gg] = sS.n
                    SQ(_)

                    if gg % 2 == 0:
                        stA = psHG[0:1, 0:2, :]; stB = psHU[0:1, 0:2, :]
                        bcA = psHG[:, 0:2, :]; bcB = psHG[:, 2:4, :]
                    else:
                        stA = psACC[0:1, 0:2, :]; stB = psACC[0:1, 2:4, :]
                        bcA = psACC[:, 0:2, :]; bcB = psACC[:, 2:4, :]

                    def _(t, stA=stA, stB=stB):
                        t.wait_ge(sS.h, ep_r2[gg])
                        if gg == 0:
                            t.wait_ge(sV.h, h_m[(16, 7)])
                        elif gg == 1:
                            t.wait_ge(sV.h, accev[(16, 7)])
                        else:
                            t.wait_ge(sV.h, ep_ab[gg - 2])
                        for c in range(NC):
                            t.matmul(stA, onesb_col[:, :], X[:, c, gb:gb + 512],
                                     start=(c == 0), stop=(c == NC - 1))
                            ins = t.matmul(stB, onesb_col[:, :], r2reg(c),
                                           start=(c == 0), stop=(c == NC - 1))
                        sT.inc(ins)
                        ep_st[gg] = sT.n
                    TQ(_)

                    rows_ok = ln_rowmath(512, stA, stB, ep_st[gg])

                    def _(t, bcA=bcA, bcB=bcB):
                        t.wait_ge(sV.h, rows_ok)
                        ins = t.matmul(bcA, onesb_row[:, :], rowA[:, 0:512],
                                       start=True, stop=True)
                        ins = t.matmul(bcB, onesb_row[:, :], rowB[:, 0:512],
                                       start=True, stop=True)
                        sT.inc(ins)
                        ep_bc[gg] = sT.n
                    TQ(_)

                    def _(v, bcA=bcA, bcB=bcB):
                        v.wait_ge(sT.h, ep_bc[gg])
                        ab = Abuf[:, (gg % 2) * 512:(gg % 2) * 512 + 512]
                        bb = Bbuf[:, (gg % 2) * 512:(gg % 2) * 512 + 512]
                        v.tensor_copy(ab, bcA)
                        v.tensor_copy(bb, bcB)
                        for c in range(NC):
                            xv = X[:, c, gb:gb + 512]
                            v.tensor_tensor(xv, xv, ab, ALU.mult)
                            ins = v.tensor_tensor(xv, xv, bb, ALU.add)
                        sV.inc(ins)
                        ep_ab[gg] = sV.n
                    VQ(_)

                prev_psacc_free_V = ep_ab[3]
                prev_unit16_T = dmm[(16, 7)]
                prev_epB2_V = ep_ab[3]

            b2_V, b2_T, b2_S = sV.n, sT.n, sS.n
        tap("b2", b2_V, b2_T, b2_S)

        with ExitStack() as ph:
            ropeC = ph.enter_context(nc.sbuf_tensor("c_ropeC", [128, T], BF16))
            ropeS = ph.enter_context(nc.sbuf_tensor("c_ropeS", [128, T], BF16))
            masks = ph.enter_context(nc.sbuf_tensor("c_masks", [128, 4, 512], BF16))
            woAll = ph.enter_context(nc.sbuf_tensor("woAll", [128, 8, D], BF16))
            wqh = ph.enter_context(nc.sbuf_tensor("wqh", [128, 2, NC, 128], BF16))
            wkh = ph.enter_context(nc.sbuf_tensor("wkh", [128, 2, NC, 128], BF16))
            wvh = ph.enter_context(nc.sbuf_tensor("wvh", [128, 2, NC, 128], BF16))
            qbuf = ph.enter_context(nc.sbuf_tensor("qbuf", [128, T], BF16))
            kbuf = ph.enter_context(nc.sbuf_tensor("kbuf", [128, T], BF16))
            vbuf = ph.enter_context(nc.sbuf_tensor("vbuf", [128, 16, 128], BF16))
            ptile = ph.enter_context(nc.sbuf_tensor("ptile", [128, 4, 512], BF16))
            q2t = ph.enter_context(nc.sbuf_tensor("q2t", [128, 4, 512], BF16))
            qraw = ph.enter_context(nc.sbuf_tensor("qraw", [128, T], BF16))
            k2t = ph.enter_context(nc.sbuf_tensor("k2t", [128, 2, 512], BF16))
            o_all = ph.enter_context(nc.sbuf_tensor("o_all", [128, 8, T], BF16))
            rsk = ph.enter_context(nc.sbuf_tensor("rsk", [128, 16], F32))
            rskt = ph.enter_context(nc.sbuf_tensor("rskt", [128, 16], F32))
            rs_qb = ph.enter_context(nc.sbuf_tensor("rs_qb", [128, 512], BF16))
            recipb = ph.enter_context(nc.sbuf_tensor("recipb", [128, 512], BF16))
            rowQ = ph.enter_context(nc.sbuf_tensor("rowQ", [1, 512], BF16))
            qtmp = ph.enter_context(nc.sbuf_tensor("qtmp", [128, 512], BF16))
            krot = ph.enter_context(nc.sbuf_tensor("krot", [128, 512], BF16))
            ktc = ph.enter_context(nc.sbuf_tensor("ktc", [128, 512], BF16))
            r2b3 = ph.enter_context(nc.sbuf_tensor("r2b3", [128, NC, 512], BF16))
            psQK = ph.enter_context(nc.psum_tensor("psQK", [128, 2, 512], F32))
            psSc = ph.enter_context(nc.psum_tensor("psSc", [128, 2, 512], F32))
            psO = ph.enter_context(nc.psum_tensor("psO", [128, 2, 512], F32))
            psMix = ph.enter_context(nc.psum_tensor("psMix", [128, 512], F32))
            psDen2 = ph.enter_context(nc.psum_tensor("psDen2", [128, 512], F32))

            def _(g):
                g.wait_ge(sV.h, b2_V); g.wait_ge(sT.h, b2_T); g.wait_ge(sS.h, b2_S)
                for h8 in range(8):
                    g.dma_start(out=woAll[:, h8, :], in_=dt["wo"][h8]).then_inc(dG.h, 16)
                dG.n += 8 * 16
            GQ(_)
            wo_ok = dG.n

            def _(y):
                y.wait_ge(sV.h, b2_V); y.wait_ge(sT.h, b2_T); y.wait_ge(sS.h, b2_S)
                y.dma_start(out=ropeC[:], in_=dt["ropeC"][:, :]).then_inc(dS.h, 16)
                y.dma_start(out=ropeS[:], in_=dt["ropeS"][:, :]).then_inc(dS.h, 16)
                y.dma_start(out=masks[:],
                            in_=dt["masks"].rearrange("m p n -> p m n")).then_inc(dS.h, 16)
                dS.n += 48
            YQ(_)
            rope_okS = dS.n

            qkv_done = {}
            wh_ok = {}

            def load_head(H):
                s_i, h8 = divmod(H, 8)
                def _(g):
                    if H >= 2:
                        g.wait_ge(sT.h, qkv_done[H - 2])
                    for dst, nm in [(wqh, "wq"), (wkh, "wk"), (wvh, "wv")]:
                        g.dma_start(out=dst[:, H % 2, :, :],
                                    in_=dt[nm][:, h8, :].rearrange("(c p) d -> p c d", p=128)
                                    ).then_inc(dG.h, 16)
                    dG.n += 48
                    wh_ok[H] = dG.n
                GQ(_)

            load_head(0)
            load_head(1)

            prev_exp_S = 0
            prev_onorm_last = 0
            prev_head_free = 0

            for s_i in range(NB):
                tb = s_i * T
                for h8 in range(8):
                    H = s_i * 8 + h8
                    slot = H % 2
                    if H + 1 < 16:
                        load_head(H + 1)

                    units = [("k", j) for j in range(4)] + [("q", j) for j in range(4)] \
                          + [("v", j) for j in range(16)]
                    cons_v = {}
                    mm_t = {}
                    ksq_s = {}; msk_t = {}
                    qsq_s = {}; msq_t = {}; qst_v = {}; qsq2_s = {}; qbc_t = {}
                    qev_v = {}
                    rsk_vv = {}

                    def emit_qchain_a(jq):
                        def _(t):
                            t.wait_ge(sS.h, qsq_s[jq])
                            if jq == 0:
                                t.wait_ge(sV.h, rsk_vv["v"])
                            else:
                                t.wait_ge(sV.h, qev_v[jq - 1])
                            ins = t.matmul(psMix[0:1, :], onesb_col[:, :],
                                           q2t[:, jq, :], start=True, stop=True)
                            sT.inc(ins)
                            msq_t[jq] = sT.n
                        TQ(_)

                        def _(v):
                            v.wait_ge(sT.h, msq_t[jq])
                            ins = v.tensor_scalar(statf[:, 0, :], psMix[0:1, :],
                                                  1.0 / 128.0, RMS_EPS, ALU.mult, op1=ALU.add)
                            sV.inc(ins); v.wait_ge(sV.h, sV.n)
                            ins = v.reciprocal(statf[:, 1, :], statf[:, 0, :])
                            sV.inc(ins)
                            qst_v[jq] = sV.n
                        VQ(_)

                        def _(s):
                            s.wait_ge(sV.h, qst_v[jq])
                            ins = s.activation(rowQ[:, :], statf[:, 1, :], AF.Sqrt)
                            sS.inc(ins)
                            qsq2_s[jq] = sS.n
                        SQ(_)

                    def emit_qchain_b(jq):
                        def _(t):
                            t.wait_ge(sS.h, qsq2_s[jq])
                            ins = t.matmul(psMix[:, :], onesb_row[:, :], rowQ[:, :],
                                           start=True, stop=True)
                            sT.inc(ins)
                            qbc_t[jq] = sT.n
                        TQ(_)

                        def _(v):
                            v.wait_ge(sT.h, qbc_t[jq])
                            v.tensor_copy(rs_qb[:, :], psMix[:, :])
                            v.tensor_tensor(qtmp[:, :], qraw[:, jq * 512:(jq + 1) * 512],
                                            rs_qb[:, :], ALU.mult)
                            v.tensor_tensor(ktc[:, :], qtmp[:, :],
                                            ropeC[:, jq * 512:(jq + 1) * 512], ALU.mult)
                            v.tensor_copy(krot[0:64, :], qtmp[64:128, :])
                            v.tensor_copy(krot[64:128, :], qtmp[0:64, :])
                            v.tensor_tensor(krot[:, :], krot[:, :],
                                            ropeS[:, jq * 512:(jq + 1) * 512], ALU.mult)
                            ins = v.tensor_tensor(qbuf[:, jq * 512:(jq + 1) * 512],
                                                  ktc[:, :], krot[:, :], ALU.add)
                            sV.inc(ins)
                            qev_v[jq] = sV.n
                        VQ(_)

                    qkslots = [psQK[:, 0, :], psQK[:, 1, :], psSc[:, 0, :], psSc[:, 1, :]]

                    def emit_mskmm(jk):
                        def _(t):
                            t.wait_ge(sS.h, ksq_s[jk])
                            for jj in range(4):
                                ins = t.matmul(psMix[:, jk * 4 + jj:jk * 4 + jj + 1],
                                               k2t[:, jk % 2, jj * 128:(jj + 1) * 128],
                                               onesb_col[:, :], start=True, stop=True)
                            sT.inc(ins)
                            msk_t[jk] = sT.n
                        TQ(_)

                    def emit_rskchain():
                        def _(v):
                            v.wait_ge(sT.h, msk_t[3])
                            ins = v.tensor_scalar(rskt[:, :], psMix[:, 0:16],
                                                  1.0 / 128.0, RMS_EPS, ALU.mult,
                                                  op1=ALU.add)
                            sV.inc(ins); v.wait_ge(sV.h, sV.n)
                            ins = v.reciprocal(rskt[:, :], rskt[:, :])
                            sV.inc(ins)
                            rsk_vv["a"] = sV.n
                        VQ(_)

                        def _(s):
                            s.wait_ge(sV.h, rsk_vv["a"])
                            ins = s.activation(rsk[:, :], rskt[:, :], AF.Sqrt)
                            sS.inc(ins)
                            rsk_vv["s"] = sS.n
                        SQ(_)

                        def _(v):
                            v.wait_ge(sS.h, rsk_vv["s"])
                            ins = v.tensor_scalar(rsk[:, :], rsk[:, :],
                                                  1.0 / 128.0, None, ALU.mult)
                            sV.inc(ins)
                            rsk_vv["v"] = sV.n
                        VQ(_)

                    for ui, (kind, j) in enumerate(units):
                        def _(t):
                            if ui == 0:
                                t.wait_ge(dG.h, wh_ok[H])
                                if H == 0:
                                    t.wait_ge(sV.h, b2_V)
                                    t.wait_ge(sS.h, b2_S)
                                    t.wait_ge(dS.h, rope_okS)
                                else:
                                    t.wait_ge(sV.h, prev_head_free)
                                    t.wait_ge(sS.h, prev_exp_S)
                            if ui >= 4:
                                for sem, mk in cons_v[ui - 4]:
                                    t.wait_ge(sem.h, mk)
                            w = {"q": wqh, "k": wkh, "v": wvh}[kind]
                            if kind in ("q", "k"):
                                for k in range(NC):
                                    ins = t.matmul(qkslots[ui % 4], w[:, slot, k, :],
                                                   X[:, k, tb + j * 512:tb + (j + 1) * 512],
                                                   start=(k == 0), stop=(k == NC - 1))
                            else:
                                for k in range(NC):
                                    ins = t.matmul(qkslots[ui % 4][:, 0:128],
                                                   X[:, k, tb + j * 128:tb + (j + 1) * 128],
                                                   w[:, slot, k, :],
                                                   start=(k == 0), stop=(k == NC - 1))
                            sT.inc(ins)
                            mm_t[ui] = sT.n
                        TQ(_)
                        if kind == "k" and j >= 1:
                            emit_mskmm(j - 1)
                        elif kind == "q" and j == 0:
                            emit_mskmm(3)
                            emit_rskchain()

                        if kind == "v":
                            def _(v):
                                v.wait_ge(sT.h, mm_t[ui])
                                ins = v.tensor_copy(vbuf[:, j, :], qkslots[ui % 4][:, 0:128])
                                sV.inc(ins)
                                cons_v[ui] = [(sV, sV.n)]
                            VQ(_)
                            if j == 0:
                                emit_qchain_a(0)
                            elif j % 4 == 2 and j // 4 < 4:
                                emit_qchain_b(j // 4)
                                if j // 4 + 1 < 4:
                                    emit_qchain_a(j // 4 + 1)
                        elif kind == "k":
                            def _(s):
                                s.wait_ge(sT.h, mm_t[ui])
                                if j >= 2:
                                    s.wait_ge(sT.h, msk_t.get(j - 2, 0))
                                ins = s.activation(k2t[:, j % 2, :], qkslots[ui % 4],
                                                   AF.Square)
                                sS.inc(ins)
                                ksq_s[j] = sS.n
                            SQ(_)

                            def _(v):
                                v.wait_ge(sS.h, ksq_s[j])
                                v.tensor_tensor(ktc[:, :], qkslots[ui % 4],
                                                ropeC[:, j * 512:(j + 1) * 512], ALU.mult)
                                v.tensor_copy(krot[0:64, :], qkslots[ui % 4][64:128, :])
                                v.tensor_copy(krot[64:128, :], qkslots[ui % 4][0:64, :])
                                v.tensor_tensor(krot[:, :], krot[:, :],
                                                ropeS[:, j * 512:(j + 1) * 512], ALU.mult)
                                ins = v.tensor_tensor(kbuf[:, j * 512:(j + 1) * 512],
                                                      ktc[:, :], krot[:, :], ALU.add)
                                sV.inc(ins)
                                cons_v[ui] = [(sV, sV.n)]
                            VQ(_)

                        else:
                            def _(s):
                                s.wait_ge(sT.h, mm_t[ui])
                                ins = s.activation(q2t[:, j, :], qkslots[ui % 4],
                                                   AF.Square)
                                sS.inc(ins)
                                qsq_s[j] = sS.n
                            SQ(_)

                            def _(v):
                                v.wait_ge(sS.h, qsq_s[j])
                                ins = v.tensor_copy(qraw[:, j * 512:(j + 1) * 512],
                                                    qkslots[ui % 4])
                                sV.inc(ins)
                                cons_v[ui] = [(sV, sV.n), (sS, qsq_s[j])]
                            VQ(_)

                    qkv_done[H] = sT.n
                    qkv_last_v = sV.n

                    exp_s = {}; msk_v = {}; sc_t = {}; pv_t = {}
                    onorm_v = {}; den_v = {}; denb_t = {}
                    tiles = [(qtg, kt) for qtg in range(4) for kt in range(4 * qtg + 4)]
                    scslots = [psSc[:, 0, :], psSc[:, 1, :], psQK[:, 0, :], psQK[:, 1, :]]

                    def emit_sc(m):
                        qtg, kt = tiles[m]
                        def _(t):
                            if m == 0:
                                t.wait_ge(sV.h, qkv_last_v)
                                if prev_exp_S:
                                    t.wait_ge(sS.h, prev_exp_S)
                            if m >= 4:
                                t.wait_ge(sS.h, exp_s[m - 4])
                            ins = t.matmul(scslots[m % 4],
                                           kbuf[:, kt * 128:(kt + 1) * 128],
                                           qbuf[:, qtg * 512:(qtg + 1) * 512],
                                           start=True, stop=True)
                            sT.inc(ins)
                            sc_t[m] = sT.n
                        TQ(_)

                        def _(s):
                            s.wait_ge(sT.h, sc_t[m])
                            if m >= 4:
                                s.wait_ge(sT.h, pv_t[m - 4])
                            ins = s.activation(ptile[:, m % 4, :], scslots[m % 4],
                                               AF.Exp, scale=rsk[:, kt:kt + 1])
                            sS.inc(ins)
                            exp_s[m] = sS.n
                        SQ(_)

                        if kt >= 4 * qtg:
                            def _(v):
                                v.wait_ge(sS.h, exp_s[m])
                                ins = v.tensor_tensor(ptile[:, m % 4, :], ptile[:, m % 4, :],
                                                      masks[:, kt - 4 * qtg, :], ALU.mult)
                                sV.inc(ins)
                                msk_v[m] = sV.n
                            VQ(_)

                    def emit_pv(m):
                        qtg, kt = tiles[m]
                        nkt = 4 * qtg + 4
                        def _(t):
                            if m in msk_v:
                                t.wait_ge(sV.h, msk_v[m])
                            else:
                                t.wait_ge(sS.h, exp_s[m])
                            if kt == 0:
                                if qtg > 0:
                                    t.wait_ge(sV.h, den_v[qtg - 1])
                                if qtg >= 2:
                                    t.wait_ge(sV.h, onorm_v[qtg - 2])
                            t.matmul(psMix[0:1, :], onesb_col[:, :], ptile[:, m % 4, :],
                                     start=(kt == 0), stop=(kt == nkt - 1))
                            ins = t.matmul(psO[:, qtg % 2, :], vbuf[:, kt, :],
                                           ptile[:, m % 4, :],
                                           start=(kt == 0), stop=(kt == nkt - 1))
                            sT.inc(ins)
                            pv_t[m] = sT.n
                        TQ(_)

                        if kt == nkt - 1:
                            def _(v):
                                v.wait_ge(sT.h, pv_t[m])
                                with nc.allow_low_precision(reason="bf16 recip row; tile is bf16"):
                                    ins = v.reciprocal(rowQ[:, :], psMix[0:1, :])
                                sV.inc(ins)
                                den_v[qtg] = sV.n
                            VQ(_)

                            def _(t):
                                t.wait_ge(sV.h, den_v[qtg])
                                if qtg >= 2:
                                    t.wait_ge(sV.h, onorm_v[qtg - 2])
                                ins = t.matmul(psDen2[:, :], onesb_row[:, :], rowQ[:, :],
                                               start=True, stop=True)
                                sT.inc(ins)
                                denb_t[qtg] = sT.n
                            TQ(_)

                            def _(v):
                                v.wait_ge(sT.h, denb_t[qtg])
                                v.tensor_copy(recipb[:, :], psDen2[:, :])
                                ins = v.tensor_tensor(o_all[:, h8, qtg * 512:(qtg + 1) * 512],
                                                      psO[:, qtg % 2, :], recipb[:, :],
                                                      ALU.mult)
                                sV.inc(ins)
                                onorm_v[qtg] = sV.n
                            VQ(_)

                    for m in range(len(tiles)):
                        emit_sc(m)
                        if m >= 2:
                            emit_pv(m - 2)
                    emit_pv(len(tiles) - 2)
                    emit_pv(len(tiles) - 1)

                    prev_exp_S = exp_s[len(tiles) - 1]
                    prev_onorm_last = onorm_v[3]
                    prev_head_free = onorm_v[3]

                op_t = {}; opr_v = {}
                for i in range(32):
                    dc, g4 = divmod(i, 4)

                    def _(t):
                        if i == 0:
                            t.wait_ge(sV.h, prev_onorm_last)
                            t.wait_ge(dG.h, wo_ok)
                        if i >= 2:
                            t.wait_ge(sV.h, opr_v[i - 2])
                        for h8 in range(8):
                            ins = t.matmul(psQK[:, i % 2, :],
                                           woAll[:, h8, dc * 128:(dc + 1) * 128],
                                           o_all[:, h8, g4 * 512:(g4 + 1) * 512],
                                           start=(h8 == 0), stop=(h8 == 7))
                        sT.inc(ins)
                        op_t[i] = sT.n
                    TQ(_)

                    def _(v):
                        v.wait_ge(sT.h, op_t[i])
                        sl = slice(tb + g4 * 512, tb + (g4 + 1) * 512)
                        ins = v.tensor_tensor(X[:, dc, sl], X[:, dc, sl],
                                              psQK[:, i % 2, :], ALU.add)
                        sV.inc(ins)
                        opr_v[i] = sV.n
                    VQ(_)
                r3_ok = sV.n

                b3r2_s = {}; b3st_t = {}; b3rows = {}; b3bc_t = {}; b3ab_v = {}
                for g4 in range(4):
                    def _(s):
                        s.wait_ge(sV.h, r3_ok)
                        if g4 > 0:
                            s.wait_ge(sT.h, b3st_t[g4 - 1])
                        for c in range(NC):
                            ins = s.activation(r2b3[:, c, :],
                                               X[:, c, tb + g4 * 512:tb + (g4 + 1) * 512],
                                               AF.Square)
                        sS.inc(ins)
                        b3r2_s[g4] = sS.n
                    SQ(_)

                    def _(t):
                        t.wait_ge(sS.h, b3r2_s[g4])
                        if g4 == 0:
                            t.wait_ge(sV.h, prev_onorm_last)
                        else:
                            t.wait_ge(sV.h, b3rows[g4 - 1])
                        for c in range(NC):
                            t.matmul(psSc[0:1, 0, :], onesb_col[:, :],
                                     X[:, c, tb + g4 * 512:tb + (g4 + 1) * 512],
                                     start=(c == 0), stop=(c == NC - 1))
                            ins = t.matmul(psSc[0:1, 1, :], onesb_col[:, :],
                                           r2b3[:, c, :],
                                           start=(c == 0), stop=(c == NC - 1))
                        sT.inc(ins)
                        b3st_t[g4] = sT.n
                    TQ(_)

                    rows_ok = ln_rowmath(512, psSc[0:1, 0, :], psSc[0:1, 1, :], b3st_t[g4])
                    b3rows[g4] = rows_ok

                    def _(t):
                        t.wait_ge(sV.h, rows_ok)
                        if g4 >= 2:
                            t.wait_ge(sV.h, b3ab_v[g4 - 2])
                        ins = t.matmul(psO[:, 0, :], onesb_row[:, :], rowA[:, :],
                                       start=True, stop=True)
                        ins = t.matmul(psO[:, 1, :], onesb_row[:, :], rowB[:, :],
                                       start=True, stop=True)
                        sT.inc(ins)
                        b3bc_t[g4] = sT.n
                    TQ(_)

                    def _(v):
                        v.wait_ge(sT.h, b3bc_t[g4])
                        v.tensor_copy(Abuf[:, g4 * 512:(g4 + 1) * 512], psO[:, 0, :])
                        ins = v.tensor_copy(Bbuf[:, g4 * 512:(g4 + 1) * 512], psO[:, 1, :])
                        sV.inc(ins)
                        b3ab_v[g4] = sV.n
                    VQ(_)

                def _(v):
                    for c in range(NC):
                        xv = X[:, c, tb:tb + T]
                        v.tensor_tensor(xv, xv, Abuf[:, 0:T], ALU.mult)
                        ins = v.tensor_tensor(xv, xv, Bbuf[:, 0:T], ALU.add)
                    sV.inc(ins)
                VQ(_)
                prev_onorm_last = sV.n
                prev_head_free = sV.n

            b3_V, b3_T, b3_S = sV.n, sT.n, sS.n
        tap("b3", b3_V, b3_T, b3_S)

        with ExitStack() as ph:
            WinT = ph.enter_context(nc.sbuf_tensor("WinT", [128, NC, 128], BF16))
            WoutT = ph.enter_context(nc.sbuf_tensor("WoutT", [L1_M, D], BF16))
            W1T = ph.enter_context(nc.sbuf_tensor("W1T", [128, 8], F32))
            b1T = ph.enter_context(nc.sbuf_tensor("b1T", [128, 8], F32))
            W2T = ph.enter_context(nc.sbuf_tensor("W2T", [128, 8], F32))
            b2T = ph.enter_context(nc.sbuf_tensor("b2T", [L1_M, 1], F32))
            binT = ph.enter_context(nc.sbuf_tensor("binT", [128, 1], F32))
            bcomb = ph.enter_context(nc.sbuf_tensor("bcomb", [128, 8], F32))
            trelu = ph.enter_context(nc.sbuf_tensor("trelu", [128, 2, 512], BF16))
            gP = ph.enter_context(nc.sbuf_tensor("gP", [128, 2, 512], BF16))
            gbuf = ph.enter_context(nc.sbuf_tensor("gbuf", [L1_M, 2, 512], BF16))
            gtmp = ph.enter_context(nc.sbuf_tensor("gtmp", [L1_M, 2, 512], BF16))
            r2b4 = ph.enter_context(nc.sbuf_tensor("r2b4", [128, NC, 512], BF16))
            rOut = ph.enter_context(nc.sbuf_tensor("rOut", [128, NC, 512], F32))
            ostage = ph.enter_context(nc.sbuf_tensor("ostage", [128, 2, D], F32))
            psS4 = ph.enter_context(nc.psum_tensor("psS4", [128, 2, 512], F32))
            psO4 = ph.enter_context(nc.psum_tensor("psO4", [128, 2, 512], F32))
            psB4 = ph.enter_context(nc.psum_tensor("psB4", [128, 2, 512], F32))

            def _(g):
                g.wait_ge(sV.h, b3_V); g.wait_ge(sT.h, b3_T); g.wait_ge(sS.h, b3_S)
                g.dma_start(out=WinT[:],
                            in_=dt["Win2"].rearrange("(c p) m -> p c m", p=128)).then_inc(dG.h, 16)
                g.dma_start(out=WoutT[:], in_=dt["Wout"][:, :]).then_inc(dG.h, 16)
                dG.n += 32
            GQ(_)
            w4_okG = dG.n

            def _(y):
                y.wait_ge(sV.h, b3_V); y.wait_ge(sT.h, b3_T); y.wait_ge(sS.h, b3_S)
                for dst, nm in [(W1T, "W1P"), (b1T, "b1P"), (W2T, "W2P"),
                                (b2T, "b2"), (binT, "binP")]:
                    y.dma_start(out=dst[:], in_=dt[nm][:, :]).then_inc(dS.h, 16)
                dS.n += 5 * 16
            YQ(_)
            w4_okS = dS.n

            def _(v):
                v.wait_ge(dS.h, w4_okS)
                for i in range(8):
                    ins = v.tensor_scalar(bcomb[:, i:i + 1], W1T[:, i:i + 1],
                                          binT[:, :], b1T[:, i:i + 1], ALU.mult, op1=ALU.add)
                sV.inc(ins)
            VQ(_)
            bcomb_ok = sV.n

            smm_t = {}; relu_s = {}; gacc_v = {}; gfin_v = {}; omm_t = {}; ffr_v = {}
            for g in range(8):
                gb = g * 512

                def _(t):
                    if g == 0:
                        t.wait_ge(dG.h, w4_okG)
                        t.wait_ge(sV.h, b3_V)
                    elif g >= 2:
                        t.wait_ge(sS.h, relu_s[(g - 2, 7)])
                    for k in range(NC):
                        ins = t.matmul(psS4[:, g % 2, :], WinT[:, k, :], X[:, k, gb:gb + 512],
                                       start=(k == 0), stop=(k == NC - 1))
                    sT.inc(ins)
                    smm_t[g] = sT.n
                TQ(_)

                for i in range(8):
                    def _(s):
                        s.wait_ge(sT.h, smm_t[g])
                        if i == 0:
                            s.wait_ge(sV.h, bcomb_ok)
                            if g > 0:
                                s.wait_ge(sV.h, gacc_v[(g - 1, 7)])
                        if i >= 2:
                            s.wait_ge(sV.h, gacc_v[(g, i - 2)])
                        ins = s.activation(trelu[:, i % 2, :], psS4[:, g % 2, :], AF.Relu,
                                           bias=bcomb[:, i:i + 1], scale=W1T[:, i:i + 1])
                        sS.inc(ins)
                        relu_s[(g, i)] = sS.n
                    SQ(_)

                    def _(v):
                        v.wait_ge(sS.h, relu_s[(g, i)])
                        if i == 0:
                            ins = v.tensor_scalar(gP[:, g % 2, :], trelu[:, i % 2, :],
                                                  W2T[:, i:i + 1], None, ALU.mult)
                        else:
                            ins = v.scalar_tensor_tensor(gP[:, g % 2, :], trelu[:, i % 2, :],
                                                         W2T[:, i:i + 1], gP[:, g % 2, :],
                                                         ALU.mult, ALU.add)
                        sV.inc(ins)
                        gacc_v[(g, i)] = sV.n
                    VQ(_)

                def _(v):
                    v.tensor_copy(gtmp[:, g % 2, :], gP[64:128, g % 2, :])
                    ins = v.scalar_tensor_tensor(gbuf[:, g % 2, :], gP[0:64, g % 2, :],
                                                 b2T[:, :], gtmp[:, g % 2, :],
                                                 ALU.add, ALU.add)
                    sV.inc(ins)
                    gfin_v[g] = sV.n
                VQ(_)

                for dc in range(NC):
                    jj = g * NC + dc

                    def _(t):
                        t.wait_ge(sV.h, gfin_v[g])
                        if jj >= 2:
                            t.wait_ge(sV.h, ffr_v[jj - 2])
                        ins = t.matmul(psO4[:, jj % 2, :],
                                       WoutT[0:L1_M, dc * 128:(dc + 1) * 128],
                                       gbuf[0:L1_M, g % 2, :], start=True, stop=True)
                        sT.inc(ins)
                        omm_t[jj] = sT.n
                    TQ(_)

                    def _(v):
                        v.wait_ge(sT.h, omm_t[jj])
                        ins = v.scalar_tensor_tensor(X[:, dc, gb:gb + 512],
                                                     psO4[:, jj % 2, :], bout_sb[:, dc:dc + 1],
                                                     X[:, dc, gb:gb + 512], ALU.add, ALU.add)
                        sV.inc(ins)
                        ffr_v[jj] = sV.n
                    VQ(_)
            ffn_done_v = sV.n

            f4r2_s = {}; fst_t = {}; f4rows = {}; fbc_t = {}; fab_v = {}
            ftr_t = {}; fev_v = {}; grp_fev = {}; ostage_dma = {}
            for g in range(8):
                gb = g * 512

                def _(s):
                    s.wait_ge(sV.h, ffn_done_v)
                    if g > 0:
                        s.wait_ge(sT.h, fst_t[g - 1])
                    for c in range(NC):
                        ins = s.activation(r2b4[:, c, :], X[:, c, gb:gb + 512], AF.Square)
                    sS.inc(ins)
                    f4r2_s[g] = sS.n
                SQ(_)

                def _(t):
                    t.wait_ge(sS.h, f4r2_s[g])
                    if g == 0:
                        t.wait_ge(sV.h, ffn_done_v)
                    else:
                        t.wait_ge(sV.h, grp_fev[g - 1])
                    for c in range(NC):
                        t.matmul(psS4[0:1, 0, :], onesb_col[:, :], X[:, c, gb:gb + 512],
                                 start=(c == 0), stop=(c == NC - 1))
                        ins = t.matmul(psO4[0:1, 0, :], onesb_col[:, :], r2b4[:, c, :],
                                       start=(c == 0), stop=(c == NC - 1))
                    sT.inc(ins)
                    fst_t[g] = sT.n
                TQ(_)

                rows_ok = ln_rowmath(512, psS4[0:1, 0, :], psO4[0:1, 0, :], fst_t[g])
                f4rows[g] = rows_ok

                def _(t):
                    t.wait_ge(sV.h, rows_ok)
                    ins = t.matmul(psB4[:, 0, :], onesb_row[:, :], rowA[:, :],
                                   start=True, stop=True)
                    ins = t.matmul(psB4[:, 1, :], onesb_row[:, :], rowB[:, :],
                                   start=True, stop=True)
                    sT.inc(ins)
                    fbc_t[g] = sT.n
                TQ(_)

                def _(v):
                    v.wait_ge(sT.h, fbc_t[g])
                    v.tensor_copy(Abuf[:, 0:512], psB4[:, 0, :])
                    v.tensor_copy(Bbuf[:, 0:512], psB4[:, 1, :])
                    for c in range(NC):
                        v.tensor_tensor(rOut[:, c, :], X[:, c, gb:gb + 512],
                                        Abuf[:, 0:512], ALU.mult)
                        ins = v.tensor_tensor(rOut[:, c, :], rOut[:, c, :],
                                              Bbuf[:, 0:512], ALU.add)
                    sV.inc(ins)
                    fab_v[g] = sV.n
                VQ(_)

                for tt in range(4):
                    gtile = g * 4 + tt
                    jj0 = gtile * NC
                    for c in range(NC):
                        jj = jj0 + c

                        def _(t):
                            t.wait_ge(sV.h, fab_v[g])
                            if jj >= 2:
                                t.wait_ge(sV.h, fev_v[jj - 2])
                            ins = t.transpose(psO4[:, jj % 2, 0:128],
                                              rOut[:, c, tt * 128:(tt + 1) * 128],
                                              ident_f32[:])
                            sT.inc(ins)
                            ftr_t[jj] = sT.n
                        TQ(_)

                        def _(v):
                            v.wait_ge(sT.h, ftr_t[jj])
                            if c == 0 and gtile >= 2:
                                v.wait_ge(dS.h, ostage_dma[gtile - 2])
                            ins = v.tensor_copy(
                                ostage[:, gtile % 2, c * 128:(c + 1) * 128],
                                psO4[:, jj % 2, 0:128])
                            sV.inc(ins)
                            fev_v[jj] = sV.n
                        VQ(_)

                    def _(y):
                        y.wait_ge(sV.h, fev_v[jj0 + NC - 1])
                        tok0 = gb + tt * 128
                        s_ii, toff = divmod(tok0, T)
                        y.dma_start(out=out[s_ii, toff:toff + 128, :],
                                    in_=ostage[:, gtile % 2, :]).then_inc(dS.h, 16)
                        dS.n += 16
                        ostage_dma[gtile] = dS.n
                    YQ(_)
                grp_fev[g] = sV.n

            def _(y):
                y.wait_ge(dS.h, dS.n)
            YQ(_)

            def _(g):
                g.wait_ge(dG.h, dG.n)
            GQ(_)

    return nc, dt


_BUILD_CACHE = {}


def _get_nc(taps=()):
    key = tuple(taps)
    if key not in _BUILD_CACHE:
        _BUILD_CACHE[key] = build_nc(taps)
    return _BUILD_CACHE[key]


def _make_in_maps(inputs):
    consts = _host_consts()
    z = np.ascontiguousarray(np.asarray(inputs["L0_attn_Z"], np.float32))
    shared = {
        "l0_W": np.ascontiguousarray(inputs["L0_attn_W"], np.float32),
        "l0_b": _fm_vec(inputs["L0_attn_b"]),
        "l0_gamma": _fm_vec(inputs["L0_attn_gamma"]),
        "l0_beta": _fm_vec(inputs["L0_attn_beta"]),
        "decay": np.asarray(inputs["L0_attn_decay"], np.float32).reshape(1, 1),
        "gate_w": np.ascontiguousarray(inputs["L0_ffn_gate_w"], np.float32),
        "gate_b": np.asarray(inputs["L0_ffn_gate_b"], np.float32).reshape(MOE_E, 1),
        "sh_wg": np.ascontiguousarray(inputs["L0_ffn_shared_wg"], np.float32),
        "sh_wu": np.ascontiguousarray(inputs["L0_ffn_shared_wu"], np.float32),
        "sh_wd": np.ascontiguousarray(inputs["L0_ffn_shared_wd"], np.float32),
        "ex_wg": np.ascontiguousarray(inputs["L0_ffn_experts_wg"], np.float32),
        "ex_wu": np.ascontiguousarray(inputs["L0_ffn_experts_wu"], np.float32),
        "ex_wd": np.ascontiguousarray(inputs["L0_ffn_experts_wd"], np.float32),
        "wq": np.ascontiguousarray(inputs["L1_attn_wq"], np.float32),
        "wk": np.ascontiguousarray(inputs["L1_attn_wk"], np.float32),
        "wv": np.ascontiguousarray(inputs["L1_attn_wv"], np.float32),
        "wo": np.ascontiguousarray(inputs["L1_attn_wo"], np.float32),
        "Win2": np.ascontiguousarray(np.concatenate(
            [np.asarray(inputs["L1_ffn_Win"], np.float32)] * 2, axis=1)),
        "binP": np.ascontiguousarray(np.concatenate(
            [np.asarray(inputs["L1_ffn_bin_"], np.float32)] * 2).reshape(128, 1)),
        "W1P": _packpair(inputs["L1_ffn_W1"]),
        "b1P": _packpair(inputs["L1_ffn_b1"]),
        "W2P": _packpair(inputs["L1_ffn_W2"]),
        "b2": np.asarray(inputs["L1_ffn_b2"], np.float32).reshape(L1_M, 1),
        "Wout": np.ascontiguousarray(inputs["L1_ffn_Wout"], np.float32),
        "bout": _fm_vec(inputs["L1_ffn_bout"]),
    }
    shared.update(consts)
    maps = []
    for c in range(8):
        m = dict(shared)
        m["x"] = np.ascontiguousarray(z[2 * c:2 * c + 2])
        maps.append(m)
    return maps


def kernel(**inputs):
    nc, _ = _get_nc()
    in_maps = _make_in_maps(inputs)
    res = run_bass_kernel_spmd(nc, in_maps, core_ids=list(range(8)))
    return np.concatenate([r["out"] for r in res.results], axis=0)


# version 12 (final)
# speedup vs baseline: 1.1085x; 1.0038x over previous
import numpy as np
import ml_dtypes
from contextlib import ExitStack

import concourse.bass as bass
import concourse.mybir as mybir
from concourse.alu_op_type import AluOpType as ALU
from concourse.bass_utils import run_bass_kernel_spmd

F32 = mybir.dt.float32
BF16 = mybir.dt.bfloat16
AF = mybir.ActivationFunctionType
AX = mybir.AxisListType

NB = 2
T = 2048
NT = NB * T
D = 1024
NC = 8
LN_EPS = 1e-5
RMS_EPS = 1e-6
ROPE_BASE = 10000.0
MOE_I = 512
MOE_E = 16
L1_M = 64
L1_INNER = 16


def _host_consts():
    bf16 = ml_dtypes.bfloat16
    c = {}
    c["ident"] = np.eye(128, dtype=bf16)
    c["ident_f32"] = np.eye(128, dtype=np.float32)
    c["onesb_col"] = np.ones((128, 1), dtype=bf16)
    c["onesb_row"] = np.ones((1, 128), dtype=bf16)
    c["ones_col_f32"] = np.ones((1, 128), dtype=np.float32)
    c["ones_row_f32"] = np.ones((1, 512), dtype=np.float32)
    half = 64
    pos = np.arange(T, dtype=np.float64)[None, :]
    freqs = (ROPE_BASE ** (-np.arange(half, dtype=np.float64) / half))[:, None]
    rad = freqs * pos
    c["ropeC"] = np.ascontiguousarray(np.concatenate([np.cos(rad)] * 2, 0)).astype(bf16)
    c["ropeS"] = np.ascontiguousarray(
        np.concatenate([-np.sin(rad), np.sin(rad)], 0)).astype(bf16)
    m = np.zeros((4, 128, 512), dtype=bf16)
    for j in range(4):
        kk = j * 128 + np.arange(128)[:, None]
        qq = np.arange(512)[None, :]
        m[j] = (kk <= qq).astype(bf16)
    c["masks"] = m
    return c


def _fm_vec(v):
    return np.ascontiguousarray(np.asarray(v, np.float32).reshape(NC, 128).T)


def _packpair(w):
    w = np.asarray(w, np.float32)
    out = np.empty((128, 8), np.float32)
    for i in range(8):
        out[0:64, i] = w[:, 2 * i]
        out[64:128, i] = w[:, 2 * i + 1]
    return np.ascontiguousarray(out)


def _packpair(w):
    w = np.asarray(w, np.float32)
    out = np.empty((128, 8), np.float32)
    for i in range(8):
        out[0:64, i] = w[:, 2 * i]
        out[64:128, i] = w[:, 2 * i + 1]
    return np.ascontiguousarray(out)


class Sem:
    def __init__(self, stack, nc, name):
        self.h = stack.enter_context(nc.semaphore(name))
        self.n = 0

    def inc(self, instr, k=1):
        instr.then_inc(self.h, k)
        self.n += k


def build_nc(taps=()):
    nc = bass.Bass("TRN2", target_bir_lowering=False)
    dt = {}

    def din(name, shape, dtype=F32):
        dt[name] = nc.dram_tensor(name, list(shape), dtype, kind="ExternalInput").ap()
        return dt[name]

    x_in = din("x", [NB, T, D])
    out = nc.dram_tensor("out", [NB, T, D], F32, kind="ExternalOutput").ap()
    tap_out = {}
    for tn in taps:
        tap_out[tn] = nc.dram_tensor("tap_" + tn, [128, NC, NT], F32,
                                     kind="ExternalOutput").ap()

    din("l0_W", [D, D]); din("l0_b", [128, NC]); din("l0_gamma", [128, NC])
    din("l0_beta", [128, NC]); din("decay", [1, 1])
    din("gate_w", [D, MOE_E]); din("gate_b", [MOE_E, 1])
    din("sh_wg", [D, MOE_I]); din("sh_wu", [D, MOE_I]); din("sh_wd", [MOE_I, D])
    din("ex_wg", [MOE_E, D, MOE_I]); din("ex_wu", [MOE_E, D, MOE_I])
    din("ex_wd", [MOE_E, MOE_I, D])
    din("wq", [D, 8, 128]); din("wk", [D, 8, 128]); din("wv", [D, 8, 128])
    din("wo", [8, 128, D])
    din("Win2", [D, 128]); din("binP", [128, 1])
    din("W1P", [128, 8]); din("b1P", [128, 8])
    din("W2P", [128, 8]); din("b2", [L1_M, 1])
    din("Wout", [L1_M, D]); din("bout", [128, NC])
    din("ident", [128, 128], BF16); din("ident_f32", [128, 128])
    din("onesb_col", [128, 1], BF16)
    din("onesb_row", [1, 128], BF16); din("ones_col_f32", [1, 128])
    din("ones_row_f32", [1, 512])
    din("ropeC", [128, T], BF16); din("ropeS", [128, T], BF16)
    din("masks", [4, 128, 512], BF16)

    stack = ExitStack()
    with stack:
        dS = Sem(stack, nc, "dS")
        dG = Sem(stack, nc, "dG")
        sT = Sem(stack, nc, "sT")
        sV = Sem(stack, nc, "sV")
        sS = Sem(stack, nc, "sS")

        def sb(name, shape, dtype):
            return stack.enter_context(nc.sbuf_tensor(name, list(shape), dtype))

        X = sb("X", [128, NC, NT], BF16)
        ident = sb("c_ident", [128, 128], BF16)
        ident_f32 = sb("c_identf", [128, 128], F32)
        onesb_col = sb("c_obc", [128, 1], BF16)
        onesb_row = sb("c_obr", [1, 128], BF16)
        ones_col_f32 = sb("c_ocf", [1, 128], F32)
        ones_row_f32 = sb("c_orf", [1, 512], F32)
        l0_b = sb("c_l0b", [128, NC], F32)
        l0_gamma = sb("c_l0g", [128, NC], F32)
        l0_beta = sb("c_l0be", [128, NC], F32)
        bout_sb = sb("c_bout", [128, NC], F32)
        decay_sb = sb("c_decay", [1, 1], F32)
        rowA = sb("rowA", [1, 512], BF16)
        rowB = sb("rowB", [1, 512], BF16)
        Abuf = sb("Abuf", [128, T], BF16)
        Bbuf = sb("Bbuf", [128, T], BF16)
        statf = sb("statf", [1, 3, 512], F32)

        block = stack.enter_context(nc.Block())
        TQ, VQ, SQ, GQ, YQ = block.tensor, block.vector, block.scalar, block.gpsimd, block.sync

        def _(y):
            for sbuf_t, nm in [(ident, "ident"), (ident_f32, "ident_f32"),
                               (onesb_col, "onesb_col"),
                               (onesb_row, "onesb_row"), (ones_col_f32, "ones_col_f32"),
                               (ones_row_f32, "ones_row_f32"),
                               (l0_b, "l0_b"), (l0_gamma, "l0_gamma"),
                               (l0_beta, "l0_beta"), (bout_sb, "bout"),
                               (decay_sb, "decay")]:
                y.dma_start(out=sbuf_t[:], in_=dt[nm][:, :]).then_inc(dS.h, 16)
                dS.n += 16
        YQ(_)
        consts_ready = dS.n

        def ln_rowmath(n, psA_row, psB_row, t_mark):
            def _(v):
                v.wait_ge(sT.h, t_mark)
                mu = statf[:, 0, 0:n]; w = statf[:, 1, 0:n]; t2 = statf[:, 2, 0:n]
                v.tensor_scalar(mu, psA_row, 1.0 / D, None, ALU.mult)
                v.tensor_scalar(w, psB_row, 1.0 / D, None, ALU.mult)
                v.tensor_tensor(t2, mu, mu, ALU.mult)
                v.tensor_tensor(w, w, t2, ALU.subtract)
                v.tensor_scalar(w, w, 1.0, LN_EPS, ALU.mult, op1=ALU.add)
                ins = v.reciprocal(w, w)
                sV.inc(ins)
            VQ(_)
            m1 = sV.n

            def _(s):
                s.wait_ge(sV.h, m1)
                ins = s.activation(rowA[:, 0:n], statf[:, 1, 0:n], AF.Sqrt)
                sS.inc(ins)
            SQ(_)
            m2 = sS.n

            def _(v):
                v.wait_ge(sS.h, m2)
                v.tensor_tensor(statf[:, 2, 0:n], statf[:, 0, 0:n], rowA[:, 0:n], ALU.mult)
                ins = v.tensor_scalar(rowB[:, 0:n], statf[:, 2, 0:n], -1.0, None, ALU.mult)
                sV.inc(ins)
            VQ(_)
            return sV.n

        with ExitStack() as ph:
            psT = [ph.enter_context(nc.psum_tensor("psTa", [128, 1024], BF16)),
                   ph.enter_context(nc.psum_tensor("psTb", [128, 1024], BF16))]
            xstage = ph.enter_context(nc.sbuf_tensor("xstage", [128, 2, D], BF16))

            ntile = NB * (T // 128)
            ld_m = [0] * ntile; tr_m = [0] * ntile; ev_m = [0] * ntile
            for i in range(ntile):
                s_i, tt = divmod(i, T // 128)

                def _(g):
                    if i >= 2:
                        g.wait_ge(sT.h, tr_m[i - 2])
                    g.dma_start(out=xstage[:, i % 2, :],
                                in_=x_in[s_i, tt * 128:(tt + 1) * 128, :]).then_inc(dG.h, 16)
                    dG.n += 16
                    ld_m[i] = dG.n
                GQ(_)

                def _(t):
                    t.wait_ge(dG.h, ld_m[i])
                    if i == 0:
                        t.wait_ge(dS.h, consts_ready)
                    if i >= 2:
                        t.wait_ge(sV.h, ev_m[i - 2])
                    for c in range(NC):
                        ins = t.transpose(psT[i % 2][:, c * 128:(c + 1) * 128],
                                          xstage[:, i % 2, c * 128:(c + 1) * 128], ident[:])
                    sT.inc(ins)
                    tr_m[i] = sT.n
                TQ(_)

                def _(v):
                    v.wait_ge(sT.h, tr_m[i])
                    t0 = s_i * T + tt * 128
                    for c in range(NC):
                        ins = v.tensor_copy(X[:, c, t0:t0 + 128],
                                            psT[i % 2][:, c * 128:(c + 1) * 128])
                    sV.inc(ins)
                    ev_m[i] = sV.n
                VQ(_)
            in_T, in_V = sT.n, sV.n

        def tap(name, tV, tT, tS):
            if name not in tap_out:
                return
            def _(g):
                g.wait_ge(sV.h, tV); g.wait_ge(sT.h, tT); g.wait_ge(sS.h, tS)
                g.dma_start(out=tap_out[name][:, :, :], in_=X[:]).then_inc(dG.h, 16)
                dG.n += 16
            GQ(_)

        tap("xin", in_V, in_T, sS.n)

        with ExitStack() as ph:
            Wrow = ph.enter_context(nc.sbuf_tensor("b1Wrow", [128, NC, D], BF16))
            Wt = ph.enter_context(nc.sbuf_tensor("b1Wt", [128, NC, D], BF16))
            Sbuf = ph.enter_context(nc.sbuf_tensor("b1S", [128, NC, T], BF16))
            Ybuf = ph.enter_context(nc.sbuf_tensor("b1Y", [128, NC, T], BF16))
            a_tile = ph.enter_context(nc.sbuf_tensor("b1a", [128, T], F32))
            a_row = ph.enter_context(nc.sbuf_tensor("b1ar", [1, 512], F32))
            a_val = ph.enter_context(nc.sbuf_tensor("b1av", [1, 1], F32))
            y2t = ph.enter_context(nc.sbuf_tensor("b1y2", [128, 2, 512], BF16))
            psY = [ph.enter_context(nc.psum_tensor("b1psY0", [128, 512], F32)),
                   ph.enter_context(nc.psum_tensor("b1psY1", [128, 512], F32)),
                   None, None]
            psSA = ph.enter_context(nc.psum_tensor("b1psSA", [128, 512], F32))
            psSB = ph.enter_context(nc.psum_tensor("b1psSB", [128, 512], F32))
            psBa = ph.enter_context(nc.psum_tensor("b1psBa", [128, 512], F32))
            psBb = ph.enter_context(nc.psum_tensor("b1psBb", [128, 512], F32))
            psT = [ph.enter_context(nc.psum_tensor("b1psT0", [128, 1024], BF16)),
                   ph.enter_context(nc.psum_tensor("b1psT1", [128, 1024], BF16))]
            psY[2] = psBa
            psY[3] = psBb

            def _(g):
                g.wait_ge(sT.h, in_T)
                g.wait_ge(sV.h, in_V)
                g.dma_start(out=Wrow[:],
                            in_=dt["l0_W"].rearrange("(c p) d -> p c d", p=128)).then_inc(dG.h, 16)
                dG.n += 16
            GQ(_)
            wrow_ok = dG.n

            def _(s):
                s.wait_ge(dS.h, consts_ready)
                ins = s.activation(a_val[:], decay_sb[:], AF.Sigmoid)
                sS.inc(ins)
            SQ(_)
            sig_ok = sS.n

            def _(v):
                v.wait_ge(sS.h, sig_ok)
                ins = v.tensor_scalar(a_row[:], ones_row_f32[:], a_val[:], None, ALU.mult)
                sV.inc(ins)
            VQ(_)
            arow_okV = sV.n

            at_t = [0] * 4; at_v = [0] * 4
            for j in range(4):
                def _(t):
                    if j == 0:
                        t.wait_ge(sV.h, max(arow_okV, in_V))
                    if j >= 2:
                        t.wait_ge(sV.h, at_v[j - 2])
                    bank = psBa if j % 2 == 0 else psBb
                    ins = t.matmul(bank[:, :], ones_col_f32[:, :], a_row[:, :],
                                   start=True, stop=True)
                    sT.inc(ins)
                    at_t[j] = sT.n
                TQ(_)

                def _(v):
                    v.wait_ge(sT.h, at_t[j])
                    bank = psBa if j % 2 == 0 else psBb
                    ins = v.tensor_copy(a_tile[:, j * 512:(j + 1) * 512], bank[:, :])
                    sV.inc(ins)
                    at_v[j] = sV.n
                VQ(_)
            atile_ok = sV.n

            wt_t = [0] * 64; wt_v = [0] * 64
            for i in range(64):
                ce, cd = divmod(i, NC)

                def _(t):
                    if i == 0:
                        t.wait_ge(dG.h, wrow_ok)
                        t.wait_ge(sV.h, in_V)
                    if i >= 2:
                        t.wait_ge(sV.h, wt_v[i - 2])
                    ins = t.transpose(psT[i % 2][:, 0:128],
                                      Wrow[:, ce, cd * 128:(cd + 1) * 128], ident[:])
                    sT.inc(ins)
                    wt_t[i] = sT.n
                TQ(_)

                def _(v):
                    v.wait_ge(sT.h, wt_t[i])
                    ins = v.tensor_copy(Wt[:, cd, ce * 128:(ce + 1) * 128],
                                        psT[i % 2][:, 0:128])
                    sV.inc(ins)
                    wt_v[i] = sV.n
                VQ(_)
            wt_ok = sV.n

            prev_y_lastS = 0
            prev_y_lastV = 0
            prev_r2_T = 0
            for s_i in range(NB):
                tb = s_i * T

                def _(v):
                    if s_i > 0:
                        v.wait_ge(sT.h, prev_r2_T)
                    for c in range(NC):
                        ins = v.tensor_tensor_scan(Sbuf[:, c, :], a_tile[:],
                                                   X[:, c, tb:tb + T], 0.0,
                                                   ALU.mult, ALU.add)
                    sV.inc(ins)
                VQ(_)
                scan_ok = sV.n

                ymm = [0] * 32; yev = [0] * 32; yevV = [0] * 32; yst_t = [0] * 32
                ystat_v = [0] * 4; yab_v = [0] * 4; ybc_t = [0] * 4

                def emit_ymm(i):
                    g, eg = divmod(i, NC)
                    def _(t):
                        if i == 0:
                            t.wait_ge(sV.h, max(scan_ok, wt_ok))
                            if s_i > 0:
                                t.wait_ge(sS.h, prev_y_lastS)
                                t.wait_ge(sV.h, prev_y_lastV)
                        if i >= 4:
                            t.wait_ge(sS.h, yev[i - 4])
                            t.wait_ge(sV.h, yevV[i - 4])
                        if i == 0 and s_i == 0:
                            t.wait_ge(sV.h, atile_ok)
                        for k in range(NC):
                            ins = t.matmul(psY[i % 4][:, :], Wt[:, k, eg * 128:(eg + 1) * 128],
                                           Sbuf[:, k, g * 512:(g + 1) * 512],
                                           start=(k == 0), stop=(k == NC - 1))
                        sT.inc(ins)
                        ymm[i] = sT.n
                    TQ(_)

                def emit_yev(i):
                    g, eg = divmod(i, NC)
                    def _(s):
                        s.wait_ge(sT.h, ymm[i])
                        if i >= 2:
                            s.wait_ge(sT.h, yst_t[i - 2])
                        ins = s.activation(y2t[:, i % 2, :], psY[i % 4][:, :],
                                           AF.Square, bias=l0_b[:, eg:eg + 1])
                        sS.inc(ins)
                        yev[i] = sS.n
                    SQ(_)

                    def _(v):
                        v.wait_ge(sS.h, yev[i])
                        if i >= 2:
                            v.wait_ge(sT.h, yst_t[i - 2])
                        ins = v.tensor_scalar(Ybuf[:, eg, g * 512:(g + 1) * 512],
                                              psY[i % 4][:, :], l0_b[:, eg:eg + 1],
                                              None, ALU.add)
                        sV.inc(ins)
                        yevV[i] = sV.n
                    VQ(_)

                def emit_yst(i):
                    g, eg = divmod(i, NC)
                    def _(t):
                        t.wait_ge(sS.h, yev[i])
                        t.wait_ge(sV.h, yevV[i])
                        if eg == 0 and g > 0:
                            t.wait_ge(sV.h, yab_v[g - 1])
                        ins = t.matmul(psSA[0:1, :], onesb_col[:, :],
                                       Ybuf[:, eg, g * 512:(g + 1) * 512],
                                       start=(eg == 0), stop=(eg == NC - 1))
                        ins = t.matmul(psSB[0:1, :], onesb_col[:, :], y2t[:, i % 2, :],
                                       start=(eg == 0), stop=(eg == NC - 1))
                        sT.inc(ins)
                        yst_t[i] = sT.n
                    TQ(_)

                def emit_ygrp(g):
                    rows_ok = ln_rowmath(512, psSA[0:1, :], psSB[0:1, :], yst_t[g * 8 + 7])
                    ystat_v[g] = rows_ok

                    def _(t):
                        t.wait_ge(sV.h, rows_ok)
                        ins = t.matmul(psSA[:, :], onesb_row[:, :], rowA[:, :],
                                       start=True, stop=True)
                        ins = t.matmul(psSB[:, :], onesb_row[:, :], rowB[:, :],
                                       start=True, stop=True)
                        sT.inc(ins)
                        ybc_t[g] = sT.n
                    TQ(_)

                    def _(v):
                        v.wait_ge(sT.h, ybc_t[g])
                        v.tensor_copy(Abuf[:, g * 512:(g + 1) * 512], psSA[:, :])
                        ins = v.tensor_copy(Bbuf[:, g * 512:(g + 1) * 512], psSB[:, :])
                        sV.inc(ins)
                        yab_v[g] = sV.n
                    VQ(_)

                for i in range(32):
                    emit_ymm(i)
                    emit_yev(i)
                    if i >= 1:
                        emit_yst(i - 1)
                        if (i - 1) % 8 == 7:
                            emit_ygrp((i - 1) // 8)
                emit_yst(31)
                emit_ygrp(3)
                prev_y_lastS = yev[31]
                prev_y_lastV = yevV[31]

                def _(v):
                    for eg in range(NC):
                        yv = Ybuf[:, eg, :]
                        v.tensor_tensor(yv, yv, Abuf[:, 0:T], ALU.mult)
                        v.tensor_tensor(yv, yv, Bbuf[:, 0:T], ALU.add)
                        v.tensor_scalar(yv, yv, l0_gamma[:, eg:eg + 1],
                                        l0_beta[:, eg:eg + 1], ALU.mult, op1=ALU.add)
                        ins = v.tensor_tensor(X[:, eg, tb:tb + T], X[:, eg, tb:tb + T],
                                              yv, ALU.add)
                    sV.inc(ins)
                VQ(_)
                r_ok = sV.n

                def _(s):
                    s.wait_ge(sV.h, r_ok)
                    for c in range(NC):
                        ins = s.activation(Sbuf[:, c, :], X[:, c, tb:tb + T], AF.Square)
                    sS.inc(ins)
                SQ(_)
                r2_ok = sS.n

                ost_t = [0] * 4; ostat_v = [0] * 4; obc_t = [0] * 4; oab_v = [0] * 4
                for g in range(4):
                    def _(t):
                        t.wait_ge(sS.h, r2_ok)
                        if g > 0:
                            t.wait_ge(sV.h, ostat_v[g - 1])
                        else:
                            t.wait_ge(sV.h, yab_v[3])
                        for c in range(NC):
                            ins = t.matmul(psSA[0:1, :], onesb_col[:, :],
                                           X[:, c, tb + g * 512:tb + (g + 1) * 512],
                                           start=(c == 0), stop=(c == NC - 1))
                            ins = t.matmul(psSB[0:1, :], onesb_col[:, :],
                                           Sbuf[:, c, g * 512:(g + 1) * 512],
                                           start=(c == 0), stop=(c == NC - 1))
                        sT.inc(ins)
                        ost_t[g] = sT.n
                    TQ(_)
                    rows_ok = ln_rowmath(512, psSA[0:1, :], psSB[0:1, :], ost_t[g])
                    ostat_v[g] = rows_ok

                    def _(t):
                        t.wait_ge(sV.h, rows_ok)
                        if g >= 2:
                            t.wait_ge(sV.h, oab_v[g - 2])
                        ins = t.matmul(psBa[:, :], onesb_row[:, :], rowA[:, :],
                                       start=True, stop=True)
                        ins = t.matmul(psBb[:, :], onesb_row[:, :], rowB[:, :],
                                       start=True, stop=True)
                        sT.inc(ins)
                        obc_t[g] = sT.n
                    TQ(_)

                    def _(v):
                        v.wait_ge(sT.h, obc_t[g])
                        v.tensor_copy(Abuf[:, g * 512:(g + 1) * 512], psBa[:, :])
                        ins = v.tensor_copy(Bbuf[:, g * 512:(g + 1) * 512], psBb[:, :])
                        sV.inc(ins)
                        oab_v[g] = sV.n
                    VQ(_)

                def _(v):
                    for c in range(NC):
                        xv = X[:, c, tb:tb + T]
                        v.tensor_tensor(xv, xv, Abuf[:, 0:T], ALU.mult)
                        ins = v.tensor_tensor(xv, xv, Bbuf[:, 0:T], ALU.add)
                    sV.inc(ins)
                VQ(_)
                prev_r2_T = ost_t[3]

            b1_V, b1_T, b1_S = sV.n, sT.n, sS.n
        tap("b1", b1_V, b1_T, b1_S)

        with ExitStack() as ph:
            wgT = ph.enter_context(nc.sbuf_tensor("wgT", [128, 2, NC, MOE_I], BF16))
            wuT = ph.enter_context(nc.sbuf_tensor("wuT", [128, 2, NC, MOE_I], BF16))
            wdT = ph.enter_context(nc.sbuf_tensor("wdT", [128, 2, 4, D], BF16))
            gateT = ph.enter_context(nc.sbuf_tensor("gateT", [128, NC, MOE_E], BF16))
            gatebT = ph.enter_context(nc.sbuf_tensor("gatebT", [MOE_E, 1], F32))
            Rsb = ph.enter_context(nc.sbuf_tensor("Rsb", [MOE_E, T], F32))
            Rt = ph.enter_context(nc.sbuf_tensor("Rt", [128, 16, MOE_E], F32))
            ctok = ph.enter_context(nc.sbuf_tensor("ctok", [128, 2, 128], F32))
            cF = ph.enter_context(nc.sbuf_tensor("cF", [MOE_E, T], BF16))
            cRow = ph.enter_context(nc.sbuf_tensor("cRow", [1, 2, T], BF16))
            cb = ph.enter_context(nc.sbuf_tensor("cb", [128, T], BF16))
            accb = ph.enter_context(nc.sbuf_tensor("accb", [128, NC, T], BF16))
            slb = ph.enter_context(nc.sbuf_tensor("slb", [128, 2, 4, 256], BF16))
            hpb = ph.enter_context(nc.sbuf_tensor("hpb", [128, 2, 4, 256], BF16))
            tv1 = ph.enter_context(nc.sbuf_tensor("tv1", [128, 4], F32))
            tv2 = ph.enter_context(nc.sbuf_tensor("tv2", [128, MOE_E], F32))
            tv3 = ph.enter_context(nc.sbuf_tensor("tv3", [128, MOE_E], F32))
            psHG = ph.enter_context(nc.psum_tensor("psHG", [128, 4, 256], F32))
            psHU = ph.enter_context(nc.psum_tensor("psHU", [128, 4, 256], F32))
            psACC = ph.enter_context(nc.psum_tensor("psACC", [128, NC, 256], F32))

            def _(g):
                g.wait_ge(sV.h, b1_V); g.wait_ge(sT.h, b1_T); g.wait_ge(sS.h, b1_S)
                g.dma_start(out=gateT[:],
                            in_=dt["gate_w"].rearrange("(c p) e -> p c e", p=128)).then_inc(dG.h, 16)
                dG.n += 16
            GQ(_)
            gate_okG = dG.n

            def _(y):
                y.wait_ge(sV.h, b1_V); y.wait_ge(sT.h, b1_T); y.wait_ge(sS.h, b1_S)
                y.dma_start(out=gatebT[:], in_=dt["gate_b"][:, :]).then_inc(dS.h, 16)
                dS.n += 16
            YQ(_)
            gate_okS = dS.n

            prev_psacc_free_V = 0
            prev_unit16_T = 0
            prev_epB2_V = 0
            for h_i in range(NB):
                hb = h_i * T
                wload = {}
                hmm = {}; humm = {}; hmm_lo = {}; t2_m = {}; silu_lo = {}; silu_m = {}; h_m = {}; dmm = {}; accev = {}
                crow_dma = {}; crow_t = {}; cb_ok = {}
                ep_r = {}; ep_r2 = {}; ep_st = {}; ep_bc = {}; ep_ab = {}

                def load_unit(u):
                    def _(g):
                        slot = u % 2
                        if u >= 2:
                            g.wait_ge(sT.h, dmm[(u - 2, 7)])
                        elif h_i > 0:
                            g.wait_ge(sT.h, prev_unit16_T)
                        else:
                            g.wait_ge(sV.h, b1_V); g.wait_ge(sT.h, b1_T)
                            g.wait_ge(sS.h, b1_S)
                        if u == 0:
                            srcs = [dt["sh_wg"], dt["sh_wu"], dt["sh_wd"]]
                        else:
                            e = u - 1
                            srcs = [dt["ex_wg"][e], dt["ex_wu"][e], dt["ex_wd"][e]]
                        g.dma_start(out=wgT[:, slot, :, :],
                                    in_=srcs[0].rearrange("(c p) i -> p c i", p=128)).then_inc(dG.h, 16)
                        g.dma_start(out=wuT[:, slot, :, :],
                                    in_=srcs[1].rearrange("(c p) i -> p c i", p=128)).then_inc(dG.h, 16)
                        g.dma_start(out=wdT[:, slot, :, :],
                                    in_=srcs[2].rearrange("(c p) i -> p c i", p=128)).then_inc(dG.h, 16)
                        dG.n += 48
                        wload[u] = dG.n
                    GQ(_)

                load_unit(0)
                load_unit(1)

                rt_t = [0] * 4; rsig = [0] * 4
                for g4 in range(4):
                    def _(t):
                        if g4 == 0:
                            t.wait_ge(dG.h, gate_okG)
                            t.wait_ge(sV.h, max(b1_V, prev_psacc_free_V))
                        else:
                            t.wait_ge(sS.h, rsig[g4 - 1])
                        for k in range(NC):
                            ins = t.matmul(psACC[0:16, 0:2, :], gateT[:, k, :],
                                           X[:, k, hb + g4 * 512:hb + (g4 + 1) * 512],
                                           start=(k == 0), stop=(k == NC - 1))
                        sT.inc(ins)
                        rt_t[g4] = sT.n
                    TQ(_)

                    def _(s):
                        s.wait_ge(sT.h, rt_t[g4])
                        if g4 == 0:
                            s.wait_ge(dS.h, gate_okS)
                        ins = s.activation(Rsb[:, g4 * 512:(g4 + 1) * 512],
                                           psACC[0:16, 0:2, :], AF.Sigmoid,
                                           bias=gatebT[:, :])
                        sS.inc(ins)
                        rsig[g4] = sS.n
                    SQ(_)

                rtr_t = [0] * 16; rte_v = [0] * 16
                for i in range(16):
                    def _(t):
                        if i == 0:
                            t.wait_ge(sS.h, rsig[3])
                            t.wait_ge(sV.h, prev_epB2_V)
                        if i >= 2:
                            t.wait_ge(sV.h, rte_v[i - 2])
                        reg = psHG[:, 0, 0:16] if i % 2 == 0 else psHG[:, 2, 0:16]
                        ins = t.transpose(reg, Rsb[:, i * 128:(i + 1) * 128], ident_f32[0:16, 0:16])
                        sT.inc(ins)
                        rtr_t[i] = sT.n
                    TQ(_)

                    def _(v):
                        v.wait_ge(sT.h, rtr_t[i])
                        reg = psHG[:, 0, 0:16] if i % 2 == 0 else psHG[:, 2, 0:16]
                        ins = v.tensor_copy(Rt[:, i, :], reg)
                        sV.inc(ins)
                        rte_v[i] = sV.n
                    VQ(_)

                top_v = [0] * 16; ctr_t = [0] * 16; cte_v = [0] * 16
                def _(v):
                    v.memset(ctok[:, 0, :], 0.0)
                    ins = v.memset(ctok[:, 1, :], 0.0)
                    sV.inc(ins)
                VQ(_)
                for i in range(16):
                    def _(v):
                        v.wait_ge(sV.h, rte_v[i])
                        if i >= 2:
                            v.wait_ge(sT.h, ctr_t[i - 2])
                        r = Rt[:, i, :]
                        m1 = tv1[:, 0:1]; m2 = tv1[:, 1:2]
                        den = tv1[:, 2:3]; rec = tv1[:, 3:4]
                        def bump(ins):
                            sV.inc(ins); v.wait_ge(sV.h, sV.n)
                        bump(v.reduce_max(m1, r, AX.X))
                        bump(v.tensor_scalar(tv2[:, :], r, m1, None, ALU.is_ge))
                        bump(v.scalar_tensor_tensor(tv3[:, :], tv2[:, :], -1e30, r,
                                                    ALU.mult, ALU.add))
                        bump(v.reduce_max(m2, tv3[:, :], AX.X))
                        bump(v.tensor_tensor(den, m1, m2, ALU.add))
                        bump(v.reciprocal(rec, den))
                        bump(v.tensor_scalar(tv2[:, :], r, m2, None, ALU.is_ge))
                        ins = v.scalar_tensor_tensor(ctok[:, i % 2, 0:16], tv2[:, :], rec,
                                                     r, ALU.mult, ALU.mult)
                        sV.inc(ins)
                        top_v[i] = sV.n
                    VQ(_)

                    def _(t):
                        t.wait_ge(sV.h, top_v[i])
                        if i >= 2:
                            t.wait_ge(sV.h, cte_v[i - 2])
                        reg = psHG[:, 0, 0:128] if i % 2 == 0 else psHG[:, 2, 0:128]
                        ins = t.transpose(reg, ctok[:, i % 2, :], ident_f32[:])
                        sT.inc(ins)
                        ctr_t[i] = sT.n
                    TQ(_)

                    def _(v):
                        v.wait_ge(sT.h, ctr_t[i])
                        reg = psHG[0:16, 0, 0:128] if i % 2 == 0 else psHG[0:16, 2, 0:128]
                        ins = v.tensor_copy(cF[:, i * 128:(i + 1) * 128], reg)
                        sV.inc(ins)
                        cte_v[i] = sV.n
                    VQ(_)
                cf_ok = sV.n

                def crow_dma_issue(uu):
                    def _(y):
                        if uu == 1:
                            y.wait_ge(sV.h, cf_ok)
                        if uu >= 3:
                            y.wait_ge(sT.h, crow_t[uu - 2])
                        y.dma_start(out=cRow[0:1, uu % 2, :],
                                    in_=cF[uu - 1:uu, :]).then_inc(dS.h, 16)
                        dS.n += 16
                        crow_dma[uu] = dS.n
                    YQ(_)

                crow_dma_issue(1)
                for u in range(17):
                    slot = u % 2
                    if u + 1 <= 16:
                        load_unit(u + 1)
                    if 2 <= u + 1 <= 16:
                        crow_dma_issue(u + 1)

                    if u >= 1:
                        e = u - 1
                        cbm = {}

                        def _(t):
                            t.wait_ge(dS.h, crow_dma[u])
                            if (u - 1, 7) in accev:
                                t.wait_ge(sV.h, accev[(u - 1, 7)])
                            cr = cRow[0:1, u % 2, :]
                            t.matmul(psACC[:, 0:2, :], onesb_row[:, :],
                                     cr[:, 0:512], start=True, stop=True)
                            t.matmul(psACC[:, 2:4, :], onesb_row[:, :],
                                     cr[:, 512:1024], start=True, stop=True)
                            t.matmul(psACC[:, 4:6, :], onesb_row[:, :],
                                     cr[:, 1024:1536], start=True, stop=True)
                            ins = t.matmul(psACC[:, 6:8, :], onesb_row[:, :],
                                           cr[:, 1536:2048], start=True, stop=True)
                            sT.inc(ins)
                            cbm["t1"] = sT.n
                            crow_t[u] = sT.n
                        TQ(_)

                        def _(v):
                            v.wait_ge(sT.h, cbm["t1"])
                            ins = v.tensor_copy(cb[:, :],
                                                psACC[:, :, :].rearrange("p a b -> p (a b)"))
                            sV.inc(ins)
                            cbm["v1"] = sV.n
                        VQ(_)
                        cb_ok[u] = cbm["v1"]

                    def emit_down(u, g):
                        def _(t):
                            t.wait_ge(sV.h, h_m[(u, g)])
                            key = (u, g - 1) if g > 0 else (u - 1, 7)
                            if key in accev:
                                t.wait_ge(sV.h, accev[key])
                            if u == 0 and g == 0:
                                t.wait_ge(sS.h, rsig[3])
                            if u > 0:
                                for b in range(4):
                                    t.matmul(psACC[:, 2 * b:2 * b + 2, :], ident[:],
                                             accb[:, 2 * b:2 * b + 2, g * 256:(g + 1) * 256],
                                             start=True, stop=False)
                            for dc in range(NC):
                                for ic in range(4):
                                    ins = t.matmul(psACC[:, dc, :],
                                                   wdT[:, u % 2, ic, dc * 128:(dc + 1) * 128],
                                                   hpb[:, g % 2, ic, :],
                                                   start=(u == 0 and ic == 0 and dc % 2 == 0),
                                                   stop=(ic == 3 and dc % 2 == 1))
                            sT.inc(ins)
                            dmm[(u, g)] = sT.n
                        TQ(_)

                        def _(v):
                            v.wait_ge(sT.h, dmm[(u, g)])
                            ins = v.tensor_copy(accb[:, :, g * 256:(g + 1) * 256],
                                                psACC[:, :, :])
                            sV.inc(ins)
                            accev[(u, g)] = sV.n
                        VQ(_)

                    for g in range(8):
                        gb = hb + g * 256

                        def _(t):
                            if g == 0:
                                t.wait_ge(dG.h, wload[u])
                                if u == 0:
                                    t.wait_ge(sV.h, cf_ok)
                            key = (u, g - 1) if g > 0 else (u - 1, 7)
                            if key in h_m:
                                t.wait_ge(sV.h, h_m[key])
                            for ic in range(4):
                                for k in range(NC):
                                    ins = t.matmul(psHU[:, ic, :],
                                                   wuT[:, slot, k, ic * 128:(ic + 1) * 128],
                                                   X[:, k, gb:gb + 256],
                                                   start=(k == 0), stop=(k == NC - 1))
                            sT.inc(ins)
                            humm[(u, g)] = sT.n
                            for ic in range(2):
                                for k in range(NC):
                                    ins = t.matmul(psHG[:, ic, :],
                                                   wgT[:, slot, k, ic * 128:(ic + 1) * 128],
                                                   X[:, k, gb:gb + 256],
                                                   start=(k == 0), stop=(k == NC - 1))
                            sT.inc(ins)
                            hmm_lo[(u, g)] = sT.n
                            for ic in range(2, 4):
                                for k in range(NC):
                                    ins = t.matmul(psHG[:, ic, :],
                                                   wgT[:, slot, k, ic * 128:(ic + 1) * 128],
                                                   X[:, k, gb:gb + 256],
                                                   start=(k == 0), stop=(k == NC - 1))
                            sT.inc(ins)
                            hmm[(u, g)] = sT.n
                        TQ(_)

                        def _(v):
                            v.wait_ge(sT.h, humm[(u, g)])
                            hp = hpb[:, g % 2, :, :]
                            if u >= 1:
                                v.wait_ge(sV.h, cb_ok[u])
                                ins = v.tensor_tensor(hp, psHU[:, :, :],
                                                      cb[:, g * 256:g * 256 + 256][:, None, :]
                                                      .to_broadcast((128, 4, 256)), ALU.mult)
                            else:
                                ins = v.tensor_copy(hp, psHU[:, :, :])
                            sV.inc(ins)
                            t2_m[(u, g)] = sV.n
                        VQ(_)

                        def _(s):
                            s.wait_ge(sT.h, hmm_lo[(u, g)])
                            ins = s.activation(slb[:, g % 2, 0:2, :], psHG[:, 0:2, :],
                                               AF.Silu)
                            sS.inc(ins)
                            silu_lo[(u, g)] = sS.n
                        SQ(_)

                        def _(v):
                            v.wait_ge(sS.h, silu_lo[(u, g)])
                            hp = hpb[:, g % 2, 0:2, :]
                            ins = v.tensor_tensor(hp, slb[:, g % 2, 0:2, :], hp, ALU.mult)
                            sV.inc(ins)
                        VQ(_)

                        def _(s):
                            s.wait_ge(sT.h, hmm[(u, g)])
                            ins = s.activation(slb[:, g % 2, 2:4, :], psHG[:, 2:4, :],
                                               AF.Silu)
                            sS.inc(ins)
                            silu_m[(u, g)] = sS.n
                        SQ(_)

                        def _(v):
                            v.wait_ge(sS.h, silu_m[(u, g)])
                            hp = hpb[:, g % 2, 2:4, :]
                            ins = v.tensor_tensor(hp, slb[:, g % 2, 2:4, :], hp, ALU.mult)
                            sV.inc(ins)
                            h_m[(u, g)] = sV.n
                        VQ(_)

                        if g >= 1:
                            emit_down(u, g - 1)
                    emit_down(u, 7)

                slbF = slb[:, :, :, :].rearrange("p a b c -> p (a b c)")
                hpbF = hpb[:, :, :, :].rearrange("p a b c -> p (a b c)")
                def r2reg(dc):
                    return (slbF[:, dc * 512:(dc + 1) * 512] if dc < 4
                            else hpbF[:, (dc - 4) * 512:(dc - 3) * 512])
                for gg in range(4):
                    gb = hb + gg * 512

                    def _(v):
                        v.wait_ge(sV.h, accev[(16, 2 * gg + 1)])
                        for dc in range(NC):
                            ins = v.scalar_tensor_tensor(
                                X[:, dc, gb:gb + 512], X[:, dc, gb:gb + 512],
                                2.0, accb[:, dc, gg * 512:(gg + 1) * 512],
                                ALU.mult, ALU.add)
                        sV.inc(ins)
                        ep_r[gg] = sV.n
                    VQ(_)

                    def _(s):
                        s.wait_ge(sV.h, ep_r[gg])
                        s.wait_ge(sT.h, dmm[(16, 7)])
                        for dc in range(NC):
                            ins = s.activation(r2reg(dc),
                                               X[:, dc, gb:gb + 512], AF.Square)
                        sS.inc(ins)
                        ep_r2[gg] = sS.n
                    SQ(_)

                    if gg % 2 == 0:
                        stA = psHG[0:1, 0:2, :]; stB = psHU[0:1, 0:2, :]
                        bcA = psHG[:, 0:2, :]; bcB = psHG[:, 2:4, :]
                    else:
                        stA = psACC[0:1, 0:2, :]; stB = psACC[0:1, 2:4, :]
                        bcA = psACC[:, 0:2, :]; bcB = psACC[:, 2:4, :]

                    def _(t, stA=stA, stB=stB):
                        t.wait_ge(sS.h, ep_r2[gg])
                        if gg == 0:
                            t.wait_ge(sV.h, h_m[(16, 7)])
                        elif gg == 1:
                            t.wait_ge(sV.h, accev[(16, 7)])
                        else:
                            t.wait_ge(sV.h, ep_ab[gg - 2])
                        for c in range(NC):
                            t.matmul(stA, onesb_col[:, :], X[:, c, gb:gb + 512],
                                     start=(c == 0), stop=(c == NC - 1))
                            ins = t.matmul(stB, onesb_col[:, :], r2reg(c),
                                           start=(c == 0), stop=(c == NC - 1))
                        sT.inc(ins)
                        ep_st[gg] = sT.n
                    TQ(_)

                    rows_ok = ln_rowmath(512, stA, stB, ep_st[gg])

                    def _(t, bcA=bcA, bcB=bcB):
                        t.wait_ge(sV.h, rows_ok)
                        ins = t.matmul(bcA, onesb_row[:, :], rowA[:, 0:512],
                                       start=True, stop=True)
                        ins = t.matmul(bcB, onesb_row[:, :], rowB[:, 0:512],
                                       start=True, stop=True)
                        sT.inc(ins)
                        ep_bc[gg] = sT.n
                    TQ(_)

                    def _(v, bcA=bcA, bcB=bcB):
                        v.wait_ge(sT.h, ep_bc[gg])
                        ab = Abuf[:, (gg % 2) * 512:(gg % 2) * 512 + 512]
                        bb = Bbuf[:, (gg % 2) * 512:(gg % 2) * 512 + 512]
                        v.tensor_copy(ab, bcA)
                        v.tensor_copy(bb, bcB)
                        for c in range(NC):
                            xv = X[:, c, gb:gb + 512]
                            v.tensor_tensor(xv, xv, ab, ALU.mult)
                            ins = v.tensor_tensor(xv, xv, bb, ALU.add)
                        sV.inc(ins)
                        ep_ab[gg] = sV.n
                    VQ(_)

                prev_psacc_free_V = ep_ab[3]
                prev_unit16_T = dmm[(16, 7)]
                prev_epB2_V = ep_ab[3]

            b2_V, b2_T, b2_S = sV.n, sT.n, sS.n
        tap("b2", b2_V, b2_T, b2_S)

        with ExitStack() as ph:
            ropeC = ph.enter_context(nc.sbuf_tensor("c_ropeC", [128, T], BF16))
            ropeS = ph.enter_context(nc.sbuf_tensor("c_ropeS", [128, T], BF16))
            masks = ph.enter_context(nc.sbuf_tensor("c_masks", [128, 4, 512], BF16))
            woAll = ph.enter_context(nc.sbuf_tensor("woAll", [128, 8, D], BF16))
            wqh = ph.enter_context(nc.sbuf_tensor("wqh", [128, 2, NC, 128], BF16))
            wkh = ph.enter_context(nc.sbuf_tensor("wkh", [128, 2, NC, 128], BF16))
            wvh = ph.enter_context(nc.sbuf_tensor("wvh", [128, 2, NC, 128], BF16))
            qbuf = ph.enter_context(nc.sbuf_tensor("qbuf", [128, T], BF16))
            kbuf = ph.enter_context(nc.sbuf_tensor("kbuf", [128, T], BF16))
            vbuf = ph.enter_context(nc.sbuf_tensor("vbuf", [128, 16, 128], BF16))
            ptile = ph.enter_context(nc.sbuf_tensor("ptile", [128, 4, 512], BF16))
            q2t = ph.enter_context(nc.sbuf_tensor("q2t", [128, 4, 512], BF16))
            qraw = ph.enter_context(nc.sbuf_tensor("qraw", [128, T], BF16))
            k2t = ph.enter_context(nc.sbuf_tensor("k2t", [128, 2, 512], BF16))
            o_all = ph.enter_context(nc.sbuf_tensor("o_all", [128, 8, T], BF16))
            rsk = ph.enter_context(nc.sbuf_tensor("rsk", [128, 16], F32))
            rskt = ph.enter_context(nc.sbuf_tensor("rskt", [128, 16], F32))
            rs_qb = ph.enter_context(nc.sbuf_tensor("rs_qb", [128, 512], BF16))
            recipb = ph.enter_context(nc.sbuf_tensor("recipb", [128, 512], BF16))
            rowQ = ph.enter_context(nc.sbuf_tensor("rowQ", [1, 512], BF16))
            qtmp = ph.enter_context(nc.sbuf_tensor("qtmp", [128, 512], BF16))
            krot = ph.enter_context(nc.sbuf_tensor("krot", [128, 512], BF16))
            ktc = ph.enter_context(nc.sbuf_tensor("ktc", [128, 512], BF16))
            r2b3 = ph.enter_context(nc.sbuf_tensor("r2b3", [128, NC, 512], BF16))
            psQK = ph.enter_context(nc.psum_tensor("psQK", [128, 2, 512], F32))
            psSc = ph.enter_context(nc.psum_tensor("psSc", [128, 2, 512], F32))
            psO = ph.enter_context(nc.psum_tensor("psO", [128, 2, 512], F32))
            psMix = ph.enter_context(nc.psum_tensor("psMix", [128, 512], F32))
            psDen2 = ph.enter_context(nc.psum_tensor("psDen2", [128, 512], F32))

            def _(g):
                g.wait_ge(sV.h, b2_V); g.wait_ge(sT.h, b2_T); g.wait_ge(sS.h, b2_S)
                for h8 in range(8):
                    g.dma_start(out=woAll[:, h8, :], in_=dt["wo"][h8]).then_inc(dG.h, 16)
                dG.n += 8 * 16
            GQ(_)
            wo_ok = dG.n

            def _(y):
                y.wait_ge(sV.h, b2_V); y.wait_ge(sT.h, b2_T); y.wait_ge(sS.h, b2_S)
                y.dma_start(out=ropeC[:], in_=dt["ropeC"][:, :]).then_inc(dS.h, 16)
                y.dma_start(out=ropeS[:], in_=dt["ropeS"][:, :]).then_inc(dS.h, 16)
                y.dma_start(out=masks[:],
                            in_=dt["masks"].rearrange("m p n -> p m n")).then_inc(dS.h, 16)
                dS.n += 48
            YQ(_)
            rope_okS = dS.n

            qkv_done = {}
            wh_ok = {}

            def load_head(H):
                s_i, h8 = divmod(H, 8)
                def _(g):
                    if H >= 2:
                        g.wait_ge(sT.h, qkv_done[H - 2])
                    for dst, nm in [(wqh, "wq"), (wkh, "wk"), (wvh, "wv")]:
                        g.dma_start(out=dst[:, H % 2, :, :],
                                    in_=dt[nm][:, h8, :].rearrange("(c p) d -> p c d", p=128)
                                    ).then_inc(dG.h, 16)
                    dG.n += 48
                    wh_ok[H] = dG.n
                GQ(_)

            load_head(0)
            load_head(1)

            prev_exp_S = 0
            prev_onorm_last = 0
            prev_head_free = 0

            for s_i in range(NB):
                tb = s_i * T
                for h8 in range(8):
                    H = s_i * 8 + h8
                    slot = H % 2
                    if H + 1 < 16:
                        load_head(H + 1)

                    units = [("k", j) for j in range(4)] + [("q", j) for j in range(4)] \
                          + [("v", j) for j in range(16)]
                    cons_v = {}
                    mm_t = {}
                    ksq_s = {}; msk_t = {}
                    qsq_s = {}; msq_t = {}; qst_v = {}; qsq2_s = {}; qbc_t = {}
                    qev_v = {}
                    rsk_vv = {}

                    def emit_qchain_a(jq):
                        def _(t):
                            t.wait_ge(sS.h, qsq_s[jq])
                            if jq == 0:
                                t.wait_ge(sV.h, rsk_vv["v"])
                            else:
                                t.wait_ge(sV.h, qev_v[jq - 1])
                            ins = t.matmul(psMix[0:1, :], onesb_col[:, :],
                                           q2t[:, jq, :], start=True, stop=True)
                            sT.inc(ins)
                            msq_t[jq] = sT.n
                        TQ(_)

                        def _(v):
                            v.wait_ge(sT.h, msq_t[jq])
                            ins = v.tensor_scalar(statf[:, 0, :], psMix[0:1, :],
                                                  1.0 / 128.0, RMS_EPS, ALU.mult, op1=ALU.add)
                            sV.inc(ins); v.wait_ge(sV.h, sV.n)
                            ins = v.reciprocal(statf[:, 1, :], statf[:, 0, :])
                            sV.inc(ins)
                            qst_v[jq] = sV.n
                        VQ(_)

                        def _(s):
                            s.wait_ge(sV.h, qst_v[jq])
                            ins = s.activation(rowQ[:, :], statf[:, 1, :], AF.Sqrt)
                            sS.inc(ins)
                            qsq2_s[jq] = sS.n
                        SQ(_)

                    def emit_qchain_b(jq):
                        def _(t):
                            t.wait_ge(sS.h, qsq2_s[jq])
                            ins = t.matmul(psMix[:, :], onesb_row[:, :], rowQ[:, :],
                                           start=True, stop=True)
                            sT.inc(ins)
                            qbc_t[jq] = sT.n
                        TQ(_)

                        def _(v):
                            v.wait_ge(sT.h, qbc_t[jq])
                            v.tensor_copy(rs_qb[:, :], psMix[:, :])
                            v.tensor_tensor(qtmp[:, :], qraw[:, jq * 512:(jq + 1) * 512],
                                            rs_qb[:, :], ALU.mult)
                            v.tensor_tensor(ktc[:, :], qtmp[:, :],
                                            ropeC[:, jq * 512:(jq + 1) * 512], ALU.mult)
                            v.tensor_copy(krot[0:64, :], qtmp[64:128, :])
                            v.tensor_copy(krot[64:128, :], qtmp[0:64, :])
                            v.tensor_tensor(krot[:, :], krot[:, :],
                                            ropeS[:, jq * 512:(jq + 1) * 512], ALU.mult)
                            ins = v.tensor_tensor(qbuf[:, jq * 512:(jq + 1) * 512],
                                                  ktc[:, :], krot[:, :], ALU.add)
                            sV.inc(ins)
                            qev_v[jq] = sV.n
                        VQ(_)

                    qkslots = [psQK[:, 0, :], psQK[:, 1, :], psSc[:, 0, :], psSc[:, 1, :]]

                    def emit_mskmm(jk):
                        def _(t):
                            t.wait_ge(sS.h, ksq_s[jk])
                            for jj in range(4):
                                ins = t.matmul(psMix[:, jk * 4 + jj:jk * 4 + jj + 1],
                                               k2t[:, jk % 2, jj * 128:(jj + 1) * 128],
                                               onesb_col[:, :], start=True, stop=True)
                            sT.inc(ins)
                            msk_t[jk] = sT.n
                        TQ(_)

                    def emit_rskchain():
                        def _(v):
                            v.wait_ge(sT.h, msk_t[3])
                            ins = v.tensor_scalar(rskt[:, :], psMix[:, 0:16],
                                                  1.0 / 128.0, RMS_EPS, ALU.mult,
                                                  op1=ALU.add)
                            sV.inc(ins); v.wait_ge(sV.h, sV.n)
                            ins = v.reciprocal(rskt[:, :], rskt[:, :])
                            sV.inc(ins)
                            rsk_vv["a"] = sV.n
                        VQ(_)

                        def _(s):
                            s.wait_ge(sV.h, rsk_vv["a"])
                            ins = s.activation(rsk[:, :], rskt[:, :], AF.Sqrt)
                            sS.inc(ins)
                            rsk_vv["s"] = sS.n
                        SQ(_)

                        def _(v):
                            v.wait_ge(sS.h, rsk_vv["s"])
                            ins = v.tensor_scalar(rsk[:, :], rsk[:, :],
                                                  1.0 / 128.0, None, ALU.mult)
                            sV.inc(ins)
                            rsk_vv["v"] = sV.n
                        VQ(_)

                    for ui, (kind, j) in enumerate(units):
                        def _(t):
                            if ui == 0:
                                t.wait_ge(dG.h, wh_ok[H])
                                if H == 0:
                                    t.wait_ge(sV.h, b2_V)
                                    t.wait_ge(sS.h, b2_S)
                                    t.wait_ge(dS.h, rope_okS)
                                else:
                                    t.wait_ge(sV.h, prev_head_free)
                                    t.wait_ge(sS.h, prev_exp_S)
                            if ui >= 4:
                                for sem, mk in cons_v[ui - 4]:
                                    t.wait_ge(sem.h, mk)
                            w = {"q": wqh, "k": wkh, "v": wvh}[kind]
                            if kind in ("q", "k"):
                                for k in range(NC):
                                    ins = t.matmul(qkslots[ui % 4], w[:, slot, k, :],
                                                   X[:, k, tb + j * 512:tb + (j + 1) * 512],
                                                   start=(k == 0), stop=(k == NC - 1))
                            else:
                                for k in range(NC):
                                    ins = t.matmul(qkslots[ui % 4][:, 0:128],
                                                   X[:, k, tb + j * 128:tb + (j + 1) * 128],
                                                   w[:, slot, k, :],
                                                   start=(k == 0), stop=(k == NC - 1))
                            sT.inc(ins)
                            mm_t[ui] = sT.n
                        TQ(_)
                        if kind == "k" and j >= 1:
                            emit_mskmm(j - 1)
                        elif kind == "q" and j == 0:
                            emit_mskmm(3)
                            emit_rskchain()

                        if kind == "v":
                            def _(v):
                                v.wait_ge(sT.h, mm_t[ui])
                                ins = v.tensor_copy(vbuf[:, j, :], qkslots[ui % 4][:, 0:128])
                                sV.inc(ins)
                                cons_v[ui] = [(sV, sV.n)]
                            VQ(_)
                            if j == 0:
                                emit_qchain_a(0)
                            elif j % 4 == 2 and j // 4 < 4:
                                emit_qchain_b(j // 4)
                                if j // 4 + 1 < 4:
                                    emit_qchain_a(j // 4 + 1)
                        elif kind == "k":
                            def _(s):
                                s.wait_ge(sT.h, mm_t[ui])
                                if j >= 2:
                                    s.wait_ge(sT.h, msk_t.get(j - 2, 0))
                                ins = s.activation(k2t[:, j % 2, :], qkslots[ui % 4],
                                                   AF.Square)
                                sS.inc(ins)
                                ksq_s[j] = sS.n
                            SQ(_)

                            def _(v):
                                v.wait_ge(sS.h, ksq_s[j])
                                v.tensor_tensor(ktc[:, :], qkslots[ui % 4],
                                                ropeC[:, j * 512:(j + 1) * 512], ALU.mult)
                                v.tensor_copy(krot[0:64, :], qkslots[ui % 4][64:128, :])
                                v.tensor_copy(krot[64:128, :], qkslots[ui % 4][0:64, :])
                                v.tensor_tensor(krot[:, :], krot[:, :],
                                                ropeS[:, j * 512:(j + 1) * 512], ALU.mult)
                                ins = v.tensor_tensor(kbuf[:, j * 512:(j + 1) * 512],
                                                      ktc[:, :], krot[:, :], ALU.add)
                                sV.inc(ins)
                                cons_v[ui] = [(sV, sV.n)]
                            VQ(_)

                        else:
                            def _(s):
                                s.wait_ge(sT.h, mm_t[ui])
                                ins = s.activation(q2t[:, j, :], qkslots[ui % 4],
                                                   AF.Square)
                                sS.inc(ins)
                                qsq_s[j] = sS.n
                            SQ(_)

                            def _(v):
                                v.wait_ge(sS.h, qsq_s[j])
                                ins = v.tensor_copy(qraw[:, j * 512:(j + 1) * 512],
                                                    qkslots[ui % 4])
                                sV.inc(ins)
                                cons_v[ui] = [(sV, sV.n), (sS, qsq_s[j])]
                            VQ(_)

                    qkv_done[H] = sT.n
                    qkv_last_v = sV.n

                    exp_s = {}; msk_v = {}; sc_t = {}; pv_t = {}
                    onorm_v = {}; den_v = {}; denb_t = {}
                    tiles = [(qtg, kt) for qtg in range(4) for kt in range(4 * qtg + 4)]
                    scslots = [psSc[:, 0, :], psSc[:, 1, :], psQK[:, 0, :], psQK[:, 1, :]]

                    def emit_sc(m):
                        qtg, kt = tiles[m]
                        def _(t):
                            if m == 0:
                                t.wait_ge(sV.h, qkv_last_v)
                                if prev_exp_S:
                                    t.wait_ge(sS.h, prev_exp_S)
                            if m >= 4:
                                t.wait_ge(sS.h, exp_s[m - 4])
                            ins = t.matmul(scslots[m % 4],
                                           kbuf[:, kt * 128:(kt + 1) * 128],
                                           qbuf[:, qtg * 512:(qtg + 1) * 512],
                                           start=True, stop=True)
                            sT.inc(ins)
                            sc_t[m] = sT.n
                        TQ(_)

                        def _(s):
                            s.wait_ge(sT.h, sc_t[m])
                            if m >= 4:
                                s.wait_ge(sT.h, pv_t[m - 4])
                            ins = s.activation(ptile[:, m % 4, :], scslots[m % 4],
                                               AF.Exp, scale=rsk[:, kt:kt + 1])
                            sS.inc(ins)
                            exp_s[m] = sS.n
                        SQ(_)

                        if kt >= 4 * qtg:
                            def _(v):
                                v.wait_ge(sS.h, exp_s[m])
                                ins = v.tensor_tensor(ptile[:, m % 4, :], ptile[:, m % 4, :],
                                                      masks[:, kt - 4 * qtg, :], ALU.mult)
                                sV.inc(ins)
                                msk_v[m] = sV.n
                            VQ(_)

                    def emit_pv(m):
                        qtg, kt = tiles[m]
                        nkt = 4 * qtg + 4
                        def _(t):
                            if m in msk_v:
                                t.wait_ge(sV.h, msk_v[m])
                            else:
                                t.wait_ge(sS.h, exp_s[m])
                            if kt == 0:
                                if qtg > 0:
                                    t.wait_ge(sV.h, den_v[qtg - 1])
                                if qtg >= 2:
                                    t.wait_ge(sV.h, onorm_v[qtg - 2])
                            t.matmul(psMix[0:1, :], onesb_col[:, :], ptile[:, m % 4, :],
                                     start=(kt == 0), stop=(kt == nkt - 1))
                            ins = t.matmul(psO[:, qtg % 2, :], vbuf[:, kt, :],
                                           ptile[:, m % 4, :],
                                           start=(kt == 0), stop=(kt == nkt - 1))
                            sT.inc(ins)
                            pv_t[m] = sT.n
                        TQ(_)

                        if kt == nkt - 1:
                            def _(v):
                                v.wait_ge(sT.h, pv_t[m])
                                with nc.allow_low_precision(reason="bf16 recip row; tile is bf16"):
                                    ins = v.reciprocal(rowQ[:, :], psMix[0:1, :])
                                sV.inc(ins)
                                den_v[qtg] = sV.n
                            VQ(_)

                            def _(t):
                                t.wait_ge(sV.h, den_v[qtg])
                                if qtg >= 2:
                                    t.wait_ge(sV.h, onorm_v[qtg - 2])
                                ins = t.matmul(psDen2[:, :], onesb_row[:, :], rowQ[:, :],
                                               start=True, stop=True)
                                sT.inc(ins)
                                denb_t[qtg] = sT.n
                            TQ(_)

                            def _(v):
                                v.wait_ge(sT.h, denb_t[qtg])
                                v.tensor_copy(recipb[:, :], psDen2[:, :])
                                ins = v.tensor_tensor(o_all[:, h8, qtg * 512:(qtg + 1) * 512],
                                                      psO[:, qtg % 2, :], recipb[:, :],
                                                      ALU.mult)
                                sV.inc(ins)
                                onorm_v[qtg] = sV.n
                            VQ(_)

                    for m in range(len(tiles)):
                        emit_sc(m)
                        if m >= 2:
                            emit_pv(m - 2)
                    emit_pv(len(tiles) - 2)
                    emit_pv(len(tiles) - 1)

                    prev_exp_S = exp_s[len(tiles) - 1]
                    prev_onorm_last = onorm_v[3]
                    prev_head_free = onorm_v[3]

                op_t = {}; opr_v = {}
                for i in range(32):
                    dc, g4 = divmod(i, 4)

                    def _(t):
                        if i == 0:
                            t.wait_ge(sV.h, prev_onorm_last)
                            t.wait_ge(dG.h, wo_ok)
                        if i >= 2:
                            t.wait_ge(sV.h, opr_v[i - 2])
                        for h8 in range(8):
                            ins = t.matmul(psQK[:, i % 2, :],
                                           woAll[:, h8, dc * 128:(dc + 1) * 128],
                                           o_all[:, h8, g4 * 512:(g4 + 1) * 512],
                                           start=(h8 == 0), stop=(h8 == 7))
                        sT.inc(ins)
                        op_t[i] = sT.n
                    TQ(_)

                    def _(v):
                        v.wait_ge(sT.h, op_t[i])
                        sl = slice(tb + g4 * 512, tb + (g4 + 1) * 512)
                        ins = v.tensor_tensor(X[:, dc, sl], X[:, dc, sl],
                                              psQK[:, i % 2, :], ALU.add)
                        sV.inc(ins)
                        opr_v[i] = sV.n
                    VQ(_)
                r3_ok = sV.n

                b3r2_s = {}; b3st_t = {}; b3rows = {}; b3bc_t = {}; b3ab_v = {}
                for g4 in range(4):
                    def _(s):
                        s.wait_ge(sV.h, r3_ok)
                        if g4 > 0:
                            s.wait_ge(sT.h, b3st_t[g4 - 1])
                        for c in range(NC):
                            ins = s.activation(r2b3[:, c, :],
                                               X[:, c, tb + g4 * 512:tb + (g4 + 1) * 512],
                                               AF.Square)
                        sS.inc(ins)
                        b3r2_s[g4] = sS.n
                    SQ(_)

                    def _(t):
                        t.wait_ge(sS.h, b3r2_s[g4])
                        if g4 == 0:
                            t.wait_ge(sV.h, prev_onorm_last)
                        else:
                            t.wait_ge(sV.h, b3rows[g4 - 1])
                        for c in range(NC):
                            t.matmul(psSc[0:1, 0, :], onesb_col[:, :],
                                     X[:, c, tb + g4 * 512:tb + (g4 + 1) * 512],
                                     start=(c == 0), stop=(c == NC - 1))
                            ins = t.matmul(psSc[0:1, 1, :], onesb_col[:, :],
                                           r2b3[:, c, :],
                                           start=(c == 0), stop=(c == NC - 1))
                        sT.inc(ins)
                        b3st_t[g4] = sT.n
                    TQ(_)

                    rows_ok = ln_rowmath(512, psSc[0:1, 0, :], psSc[0:1, 1, :], b3st_t[g4])
                    b3rows[g4] = rows_ok

                    def _(t):
                        t.wait_ge(sV.h, rows_ok)
                        if g4 >= 2:
                            t.wait_ge(sV.h, b3ab_v[g4 - 2])
                        ins = t.matmul(psO[:, 0, :], onesb_row[:, :], rowA[:, :],
                                       start=True, stop=True)
                        ins = t.matmul(psO[:, 1, :], onesb_row[:, :], rowB[:, :],
                                       start=True, stop=True)
                        sT.inc(ins)
                        b3bc_t[g4] = sT.n
                    TQ(_)

                    def _(v):
                        v.wait_ge(sT.h, b3bc_t[g4])
                        v.tensor_copy(Abuf[:, g4 * 512:(g4 + 1) * 512], psO[:, 0, :])
                        ins = v.tensor_copy(Bbuf[:, g4 * 512:(g4 + 1) * 512], psO[:, 1, :])
                        sV.inc(ins)
                        b3ab_v[g4] = sV.n
                    VQ(_)

                def _(v):
                    for c in range(NC):
                        xv = X[:, c, tb:tb + T]
                        v.tensor_tensor(xv, xv, Abuf[:, 0:T], ALU.mult)
                        ins = v.tensor_tensor(xv, xv, Bbuf[:, 0:T], ALU.add)
                    sV.inc(ins)
                VQ(_)
                prev_onorm_last = sV.n
                prev_head_free = sV.n

            b3_V, b3_T, b3_S = sV.n, sT.n, sS.n
        tap("b3", b3_V, b3_T, b3_S)

        with ExitStack() as ph:
            WinT = ph.enter_context(nc.sbuf_tensor("WinT", [128, NC, 128], BF16))
            WoutT = ph.enter_context(nc.sbuf_tensor("WoutT", [L1_M, D], BF16))
            W1T = ph.enter_context(nc.sbuf_tensor("W1T", [128, 8], F32))
            b1T = ph.enter_context(nc.sbuf_tensor("b1T", [128, 8], F32))
            W2T = ph.enter_context(nc.sbuf_tensor("W2T", [128, 8], F32))
            b2T = ph.enter_context(nc.sbuf_tensor("b2T", [L1_M, 1], F32))
            binT = ph.enter_context(nc.sbuf_tensor("binT", [128, 1], F32))
            bcomb = ph.enter_context(nc.sbuf_tensor("bcomb", [128, 8], F32))
            trelu = ph.enter_context(nc.sbuf_tensor("trelu", [128, 2, 512], BF16))
            gP = ph.enter_context(nc.sbuf_tensor("gP", [128, 2, 512], BF16))
            gbuf = ph.enter_context(nc.sbuf_tensor("gbuf", [L1_M, 2, 512], BF16))
            gtmp = ph.enter_context(nc.sbuf_tensor("gtmp", [L1_M, 2, 512], BF16))
            r2b4 = ph.enter_context(nc.sbuf_tensor("r2b4", [128, NC, 512], BF16))
            rOut = ph.enter_context(nc.sbuf_tensor("rOut", [128, NC, 512], F32))
            ostage = ph.enter_context(nc.sbuf_tensor("ostage", [128, 2, D], F32))
            psS4 = ph.enter_context(nc.psum_tensor("psS4", [128, 2, 512], F32))
            psO4 = ph.enter_context(nc.psum_tensor("psO4", [128, 2, 512], F32))
            psB4 = ph.enter_context(nc.psum_tensor("psB4", [128, 2, 512], F32))

            def _(g):
                g.wait_ge(sV.h, b3_V); g.wait_ge(sT.h, b3_T); g.wait_ge(sS.h, b3_S)
                g.dma_start(out=WinT[:],
                            in_=dt["Win2"].rearrange("(c p) m -> p c m", p=128)).then_inc(dG.h, 16)
                g.dma_start(out=WoutT[:], in_=dt["Wout"][:, :]).then_inc(dG.h, 16)
                dG.n += 32
            GQ(_)
            w4_okG = dG.n

            def _(y):
                y.wait_ge(sV.h, b3_V); y.wait_ge(sT.h, b3_T); y.wait_ge(sS.h, b3_S)
                for dst, nm in [(W1T, "W1P"), (b1T, "b1P"), (W2T, "W2P"),
                                (b2T, "b2"), (binT, "binP")]:
                    y.dma_start(out=dst[:], in_=dt[nm][:, :]).then_inc(dS.h, 16)
                dS.n += 5 * 16
            YQ(_)
            w4_okS = dS.n

            def _(v):
                v.wait_ge(dS.h, w4_okS)
                for i in range(8):
                    ins = v.tensor_scalar(bcomb[:, i:i + 1], W1T[:, i:i + 1],
                                          binT[:, :], b1T[:, i:i + 1], ALU.mult, op1=ALU.add)
                sV.inc(ins)
            VQ(_)
            bcomb_ok = sV.n

            smm_t = {}; relu_s = {}; gacc_v = {}; gfin_v = {}; omm_t = {}; ffr_v = {}
            for g in range(8):
                gb = g * 512

                def _(t):
                    if g == 0:
                        t.wait_ge(dG.h, w4_okG)
                        t.wait_ge(sV.h, b3_V)
                    elif g >= 2:
                        t.wait_ge(sS.h, relu_s[(g - 2, 7)])
                    for k in range(NC):
                        ins = t.matmul(psS4[:, g % 2, :], WinT[:, k, :], X[:, k, gb:gb + 512],
                                       start=(k == 0), stop=(k == NC - 1))
                    sT.inc(ins)
                    smm_t[g] = sT.n
                TQ(_)

                for i in range(8):
                    def _(s):
                        s.wait_ge(sT.h, smm_t[g])
                        if i == 0:
                            s.wait_ge(sV.h, bcomb_ok)
                            if g > 0:
                                s.wait_ge(sV.h, gacc_v[(g - 1, 7)])
                        if i >= 2:
                            s.wait_ge(sV.h, gacc_v[(g, i - 2)])
                        ins = s.activation(trelu[:, i % 2, :], psS4[:, g % 2, :], AF.Relu,
                                           bias=bcomb[:, i:i + 1], scale=W1T[:, i:i + 1])
                        sS.inc(ins)
                        relu_s[(g, i)] = sS.n
                    SQ(_)

                    def _(v):
                        v.wait_ge(sS.h, relu_s[(g, i)])
                        if i == 0:
                            ins = v.tensor_scalar(gP[:, g % 2, :], trelu[:, i % 2, :],
                                                  W2T[:, i:i + 1], None, ALU.mult)
                        else:
                            ins = v.scalar_tensor_tensor(gP[:, g % 2, :], trelu[:, i % 2, :],
                                                         W2T[:, i:i + 1], gP[:, g % 2, :],
                                                         ALU.mult, ALU.add)
                        sV.inc(ins)
                        gacc_v[(g, i)] = sV.n
                    VQ(_)

                def _(v):
                    v.tensor_copy(gtmp[:, g % 2, :], gP[64:128, g % 2, :])
                    ins = v.scalar_tensor_tensor(gbuf[:, g % 2, :], gP[0:64, g % 2, :],
                                                 b2T[:, :], gtmp[:, g % 2, :],
                                                 ALU.add, ALU.add)
                    sV.inc(ins)
                    gfin_v[g] = sV.n
                VQ(_)

                for dc in range(NC):
                    jj = g * NC + dc

                    def _(t):
                        t.wait_ge(sV.h, gfin_v[g])
                        if jj >= 2:
                            t.wait_ge(sV.h, ffr_v[jj - 2])
                        ins = t.matmul(psO4[:, jj % 2, :],
                                       WoutT[0:L1_M, dc * 128:(dc + 1) * 128],
                                       gbuf[0:L1_M, g % 2, :], start=True, stop=True)
                        sT.inc(ins)
                        omm_t[jj] = sT.n
                    TQ(_)

                    def _(v):
                        v.wait_ge(sT.h, omm_t[jj])
                        ins = v.scalar_tensor_tensor(X[:, dc, gb:gb + 512],
                                                     psO4[:, jj % 2, :], bout_sb[:, dc:dc + 1],
                                                     X[:, dc, gb:gb + 512], ALU.add, ALU.add)
                        sV.inc(ins)
                        ffr_v[jj] = sV.n
                    VQ(_)
            ffn_done_v = sV.n

            f4r2_s = {}; fst_t = {}; f4rows = {}; fbc_t = {}; fab_v = {}
            ftr_t = {}; fev_v = {}; grp_fev = {}; ostage_dma = {}
            for g in range(8):
                gb = g * 512

                def _(s):
                    s.wait_ge(sV.h, ffn_done_v)
                    if g > 0:
                        s.wait_ge(sT.h, fst_t[g - 1])
                    for c in range(NC):
                        ins = s.activation(r2b4[:, c, :], X[:, c, gb:gb + 512], AF.Square)
                    sS.inc(ins)
                    f4r2_s[g] = sS.n
                SQ(_)

                def _(t):
                    t.wait_ge(sS.h, f4r2_s[g])
                    if g == 0:
                        t.wait_ge(sV.h, ffn_done_v)
                    else:
                        t.wait_ge(sV.h, grp_fev[g - 1])
                    for c in range(NC):
                        t.matmul(psS4[0:1, 0, :], onesb_col[:, :], X[:, c, gb:gb + 512],
                                 start=(c == 0), stop=(c == NC - 1))
                        ins = t.matmul(psO4[0:1, 0, :], onesb_col[:, :], r2b4[:, c, :],
                                       start=(c == 0), stop=(c == NC - 1))
                    sT.inc(ins)
                    fst_t[g] = sT.n
                TQ(_)

                rows_ok = ln_rowmath(512, psS4[0:1, 0, :], psO4[0:1, 0, :], fst_t[g])
                f4rows[g] = rows_ok

                def _(t):
                    t.wait_ge(sV.h, rows_ok)
                    ins = t.matmul(psB4[:, 0, :], onesb_row[:, :], rowA[:, :],
                                   start=True, stop=True)
                    ins = t.matmul(psB4[:, 1, :], onesb_row[:, :], rowB[:, :],
                                   start=True, stop=True)
                    sT.inc(ins)
                    fbc_t[g] = sT.n
                TQ(_)

                def _(v):
                    v.wait_ge(sT.h, fbc_t[g])
                    v.tensor_copy(Abuf[:, 0:512], psB4[:, 0, :])
                    v.tensor_copy(Bbuf[:, 0:512], psB4[:, 1, :])
                    for c in range(NC):
                        v.tensor_tensor(rOut[:, c, :], X[:, c, gb:gb + 512],
                                        Abuf[:, 0:512], ALU.mult)
                        ins = v.tensor_tensor(rOut[:, c, :], rOut[:, c, :],
                                              Bbuf[:, 0:512], ALU.add)
                    sV.inc(ins)
                    fab_v[g] = sV.n
                VQ(_)

                for tt in range(4):
                    gtile = g * 4 + tt
                    jj0 = gtile * NC
                    for c in range(NC):
                        jj = jj0 + c

                        def _(t):
                            t.wait_ge(sV.h, fab_v[g])
                            if jj >= 2:
                                t.wait_ge(sV.h, fev_v[jj - 2])
                            ins = t.transpose(psO4[:, jj % 2, 0:128],
                                              rOut[:, c, tt * 128:(tt + 1) * 128],
                                              ident_f32[:])
                            sT.inc(ins)
                            ftr_t[jj] = sT.n
                        TQ(_)

                        def _(v):
                            v.wait_ge(sT.h, ftr_t[jj])
                            if c == 0 and gtile >= 2:
                                v.wait_ge(dS.h, ostage_dma[gtile - 2])
                            ins = v.tensor_copy(
                                ostage[:, gtile % 2, c * 128:(c + 1) * 128],
                                psO4[:, jj % 2, 0:128])
                            sV.inc(ins)
                            fev_v[jj] = sV.n
                        VQ(_)

                    def _(y):
                        y.wait_ge(sV.h, fev_v[jj0 + NC - 1])
                        tok0 = gb + tt * 128
                        s_ii, toff = divmod(tok0, T)
                        y.dma_start(out=out[s_ii, toff:toff + 128, :],
                                    in_=ostage[:, gtile % 2, :]).then_inc(dS.h, 16)
                        dS.n += 16
                        ostage_dma[gtile] = dS.n
                    YQ(_)
                grp_fev[g] = sV.n

            def _(y):
                y.wait_ge(dS.h, dS.n)
            YQ(_)

            def _(g):
                g.wait_ge(dG.h, dG.n)
            GQ(_)

    return nc, dt


_BUILD_CACHE = {}


def _get_nc(taps=()):
    key = tuple(taps)
    if key not in _BUILD_CACHE:
        _BUILD_CACHE[key] = build_nc(taps)
    return _BUILD_CACHE[key]


def _make_in_maps(inputs):
    consts = _host_consts()
    z = np.ascontiguousarray(np.asarray(inputs["L0_attn_Z"], np.float32))
    shared = {
        "l0_W": np.ascontiguousarray(inputs["L0_attn_W"], np.float32),
        "l0_b": _fm_vec(inputs["L0_attn_b"]),
        "l0_gamma": _fm_vec(inputs["L0_attn_gamma"]),
        "l0_beta": _fm_vec(inputs["L0_attn_beta"]),
        "decay": np.asarray(inputs["L0_attn_decay"], np.float32).reshape(1, 1),
        "gate_w": np.ascontiguousarray(inputs["L0_ffn_gate_w"], np.float32),
        "gate_b": np.asarray(inputs["L0_ffn_gate_b"], np.float32).reshape(MOE_E, 1),
        "sh_wg": np.ascontiguousarray(inputs["L0_ffn_shared_wg"], np.float32),
        "sh_wu": np.ascontiguousarray(inputs["L0_ffn_shared_wu"], np.float32),
        "sh_wd": np.ascontiguousarray(inputs["L0_ffn_shared_wd"], np.float32),
        "ex_wg": np.ascontiguousarray(inputs["L0_ffn_experts_wg"], np.float32),
        "ex_wu": np.ascontiguousarray(inputs["L0_ffn_experts_wu"], np.float32),
        "ex_wd": np.ascontiguousarray(inputs["L0_ffn_experts_wd"], np.float32),
        "wq": np.ascontiguousarray(inputs["L1_attn_wq"], np.float32),
        "wk": np.ascontiguousarray(inputs["L1_attn_wk"], np.float32),
        "wv": np.ascontiguousarray(inputs["L1_attn_wv"], np.float32),
        "wo": np.ascontiguousarray(inputs["L1_attn_wo"], np.float32),
        "Win2": np.ascontiguousarray(np.concatenate(
            [np.asarray(inputs["L1_ffn_Win"], np.float32)] * 2, axis=1)),
        "binP": np.ascontiguousarray(np.concatenate(
            [np.asarray(inputs["L1_ffn_bin_"], np.float32)] * 2).reshape(128, 1)),
        "W1P": _packpair(inputs["L1_ffn_W1"]),
        "b1P": _packpair(inputs["L1_ffn_b1"]),
        "W2P": _packpair(inputs["L1_ffn_W2"]),
        "b2": np.asarray(inputs["L1_ffn_b2"], np.float32).reshape(L1_M, 1),
        "Wout": np.ascontiguousarray(inputs["L1_ffn_Wout"], np.float32),
        "bout": _fm_vec(inputs["L1_ffn_bout"]),
    }
    shared.update(consts)
    maps = []
    for c in range(8):
        m = dict(shared)
        m["x"] = np.ascontiguousarray(z[2 * c:2 * c + 2])
        maps.append(m)
    return maps


def kernel(**inputs):
    nc, _ = _get_nc()
    in_maps = _make_in_maps(inputs)
    res = run_bass_kernel_spmd(nc, in_maps, core_ids=list(range(8)))
    return np.concatenate([r["out"] for r in res.results], axis=0)
